# version 01
# baseline (speedup 1.0000x reference)
import numpy as np
import ml_dtypes
import concourse.bass as bass
import concourse.mybir as mybir
from concourse.bass_utils import run_bass_kernel_spmd

F32 = mybir.dt.float32
BF16 = mybir.dt.bfloat16
I32 = mybir.dt.int32
AF = mybir.ActivationFunctionType
ALU = mybir.AluOpType
AX = mybir.AxisListType
BF16NP = ml_dtypes.bfloat16

D = 4096
HID = 11008
S = 4096
B = 2
NC = 8
TOK = (B * S) // NC
GI = D // 128
GO = HID // 128
NH = 32
NKV = 8
HD = 128
EPS = 1e-5
EXPB = -20.0
NTT = TOK // 128
NCB = HID // 256
QSC = float(HD) ** -0.5


class Prog:

    def __init__(self, me, eng, handles):
        self.me = me
        self.eng = eng
        self.h = handles
        self.v = {k: 0 for k in handles}

    def do(self, engname, fn, sem=None, n=1):
        tok = None
        if sem is not None:
            self.v[sem] += n
            tok = self.v[sem]
        if engname == self.me:
            ins = fn(self.eng)
            if sem is not None:
                ins.then_inc(self.h[sem], n)
        return tok

    def dma(self, engname, out, in_, sem, **kw):
        t = self.do(engname, lambda e: e.dma_start(out=out, in_=in_, **kw),
                    sem=sem, n=16)
        return (sem, t)

    def wait(self, engname, sem, tok=None):
        if isinstance(sem, tuple):
            sem, tok = sem
        elif isinstance(tok, tuple):
            sem, tok = tok
        if engname == self.me:
            self.eng.wait_ge(self.h[sem], self.v[sem] if tok is None else tok)

    def tok(self, sem):
        return self.v[sem]

    def gp(self, fn, sem="gp"):
        t = self.do("gpsimd", fn, sem=sem)
        self.wait("gpsimd", sem, t)
        return t


def run_spmd_program(nc, block, handles, body):
    def mk(engname):
        def walk(eng):
            body(Prog(engname, eng, handles))
        return walk
    block.sync(mk("sync"))
    block.gpsimd(mk("gpsimd"))
    block.tensor(mk("tensor"))
    block.vector(mk("vector"))
    block.scalar(mk("scalar"))



def build_tokenside(with_proj, with_oproj):
    nc = bass.Bass("TRN2", target_bir_lowering=False, debug=False)
    ap = lambda t: t.ap()

    E = {}
    E["xin"] = ap(nc.dram_tensor("xin", [TOK, D], F32, kind="ExternalInput"))
    E["qwg"] = ap(nc.dram_tensor("qwg", [D, HID], I32, kind="ExternalInput"))
    E["qwu"] = ap(nc.dram_tensor("qwu", [D, HID], I32, kind="ExternalInput"))
    E["qwd"] = ap(nc.dram_tensor("qwd", [HID, D], I32, kind="ExternalInput"))
    E["zg"] = ap(nc.dram_tensor("zg", [GI, HID], I32, kind="ExternalInput"))
    E["sg"] = ap(nc.dram_tensor("sg", [GI, HID], F32, kind="ExternalInput"))
    E["zu"] = ap(nc.dram_tensor("zu", [GI, HID], I32, kind="ExternalInput"))
    E["su"] = ap(nc.dram_tensor("su", [GI, HID], F32, kind="ExternalInput"))
    E["zd"] = ap(nc.dram_tensor("zd", [GO, D], I32, kind="ExternalInput"))
    E["sd"] = ap(nc.dram_tensor("sd", [GO, D], F32, kind="ExternalInput"))
    if with_proj:
        E["wq"] = ap(nc.dram_tensor("wq", [D, D], F32, kind="ExternalInput"))
        E["wk"] = ap(nc.dram_tensor("wk", [D, NKV * HD], F32, kind="ExternalInput"))
        E["wv"] = ap(nc.dram_tensor("wv", [D, NKV * HD], F32, kind="ExternalInput"))
        E["qt"] = ap(nc.dram_tensor("qt", [D, TOK], BF16, kind="ExternalOutput"))
        E["kt"] = ap(nc.dram_tensor("kt", [NKV * HD, TOK], BF16, kind="ExternalOutput"))
        E["vt"] = ap(nc.dram_tensor("vt", [NKV * HD, TOK], BF16, kind="ExternalOutput"))
    if with_oproj:
        E["attnT"] = ap(nc.dram_tensor("attnT", [D, TOK], BF16, kind="ExternalInput"))
        E["wo"] = ap(nc.dram_tensor("wo", [D, D], F32, kind="ExternalInput"))
    E["xout"] = ap(nc.dram_tensor("xout", [TOK, D], F32, kind="ExternalOutput"))
    E["xbase_d"] = ap(nc.dram_tensor("xbase_d", [TOK, D], F32))
    E["zsg_d"] = ap(nc.dram_tensor("zsg_d", [GI, HID], BF16))
    E["zsu_d"] = ap(nc.dram_tensor("zsu_d", [GI, HID], BF16))
    E["zsd_d"] = ap(nc.dram_tensor("zsd_d", [GO, D], BF16))
    E["sdeff_d"] = ap(nc.dram_tensor("sdeff_d", [GO, D], BF16))
    E["ht_d"] = ap(nc.dram_tensor("ht_d", [GO, 128, TOK], BF16))
    E["wd_d"] = ap(nc.dram_tensor("wd_d", [GO, 128, D], BF16))

    import contextlib
    with contextlib.ExitStack() as _es:
        _e = _es.enter_context
        xnT = _e(nc.sbuf_tensor("xnT", [128, GI * TOK], BF16))
        qg_sb = _e(nc.sbuf_tensor("qg", [128, 2, GI * 256], BF16))
        qu_sb = _e(nc.sbuf_tensor("qu", [128, 2, GI * 256], BF16))
        wdq = _e(nc.sbuf_tensor("wdq", [128, D], BF16))
        scb = _e(nc.sbuf_tensor("scb", [GI, 2 * 256], BF16))
        zscb = _e(nc.sbuf_tensor("zscb", [GI, 2 * 256], BF16))
        sdef_sb = _e(nc.sbuf_tensor("sdef", [GO, D], BF16))
        zsdmb = _e(nc.sbuf_tensor("zsdmb", [GO, 512], BF16))
        xsT = _e(nc.sbuf_tensor("xsT", [GI, TOK], BF16))
        hs_sb = _e(nc.sbuf_tensor("hs", [128, NTT * GO], BF16))
        hsT = _e(nc.sbuf_tensor("hsT", [GO, TOK], BF16))
        hct = _e(nc.sbuf_tensor("hct", [128, TOK], BF16))
        htin = _e(nc.sbuf_tensor("htin", [128, 2, TOK], BF16))
        wdt = _e(nc.sbuf_tensor("wdt", [128, 2, 512], BF16))
        ones1 = _e(nc.sbuf_tensor("ones1", [1, 128], BF16))
        onescol = _e(nc.sbuf_tensor("onescol", [128, 1], BF16))
        ident = _e(nc.sbuf_tensor("ident", [128, 128], BF16))
        indg = _e(nc.sbuf_tensor("indg", [128, GI * 128], BF16))
        rowsel = _e(nc.sbuf_tensor("rowsel", [128, GI], F32))
        indone = _e(nc.sbuf_tensor("indone", [128, 128], BF16))
        xld = _e(nc.sbuf_tensor("xld", [128, D], F32))
        xn_sb = _e(nc.sbuf_tensor("xn", [128, D], BF16))
        xs_sb = _e(nc.sbuf_tensor("xs", [128, NTT * GI], BF16))
        xsf = _e(nc.sbuf_tensor("xsf", [128, GI], F32))
        lnst = _e(nc.sbuf_tensor("lnst", [128, 8], F32))
        gsil = _e(nc.sbuf_tensor("gsil", [128, TOK], BF16))
        xacc = _e(nc.sbuf_tensor("xacc", [128, 2, 512], F32))
        pstg = _e(nc.sbuf_tensor("pstg", [128, 2, 512], BF16))
        bk0 = _e(nc.psum_tensor("bk0", [128, 512], F32))
        bk1 = _e(nc.psum_tensor("bk1", [128, 512], F32))
        bk2 = _e(nc.psum_tensor("bk2", [128, 512], F32))
        bk3 = _e(nc.psum_tensor("bk3", [128, 512], F32))
        bk4 = _e(nc.psum_tensor("bk4", [128, 512], F32))
        bk5 = _e(nc.psum_tensor("bk5", [128, 512], F32))
        bk6 = _e(nc.psum_tensor("bk6", [128, 512], F32))
        bk7 = _e(nc.psum_tensor("bk7", [128, 512], F32))
        semnames = ["pe", "dv", "ac", "gp", "pre", "ln", "pz", "ps", "pw", "xli",
                    "xbw", "qg0", "qg1", "qu0", "qu1", "scg", "scu", "zsgq",
                    "zsuq", "wdqi", "wdw", "htw", "ht0", "ht1", "wd0", "wd1",
                    "zsm", "xa0", "xa1", "xo0", "xo1", "pj0", "pj1", "pj2",
                    "pj3", "pw0", "pw1", "ati"]
        handles = {n: _e(nc.semaphore(n)) for n in semnames}
        block = _e(nc.Block())
        env = dict(E)
        env.update(dict(
            xnT=xnT, qg_sb=qg_sb, qu_sb=qu_sb, wdq=wdq, scb=scb, zscb=zscb,
            sdef_sb=sdef_sb, zsdmb=zsdmb, xsT=xsT, hs_sb=hs_sb, hsT=hsT,
            hct=hct, htin=htin, wdt=wdt, ones1=ones1, onescol=onescol,
            ident=ident, indg=indg, xld=xld, xn_sb=xn_sb, xs_sb=xs_sb,
            lnst=lnst, gsil=gsil, xacc=xacc, pstg=pstg, rowsel=rowsel,
            indone=indone, xsf=xsf,
            bank=[bk0, bk1, bk2, bk3, bk4, bk5, bk6, bk7],
        ))

        def body(P):
            for k in [k for k in env if k.startswith("_")]:
                del env[k]
            emit_prep(P, env)
            if with_oproj:
                emit_oproj(P, env)
                base = env["xbase_d"]
            else:
                base = env["xin"]
            emit_ln(P, env, base, write_xbase=(not with_oproj))
            emit_gateup(P, env)
            emit_down(P, env)
            if with_proj:
                emit_ln(P, env, env["xout"], write_xbase=False)
                emit_projections(P, env)
            for s_ in ("xo0", "xo1", "xbw", "pw0", "pw1", "wdw", "htw", "pw"):
                P.wait("sync", s_)

        run_spmd_program(nc, block, handles, body)
    return nc


def emit_prep(P, env):
    ones1, onescol, ident, indg = env["ones1"], env["onescol"], env["ident"], env["indg"]
    xld = env["xld"]
    CH = HID // 16

    P.gp(lambda g: g.memset(ones1[:], 1.0))
    P.gp(lambda g: g.memset(onescol[:], 1.0))
    P.gp(lambda g: g.memset(ident[:], 1.0))
    P.gp(lambda g: g.affine_select(
        ident[:], ident[:], pattern=[[1, 128]], compare_op=ALU.is_equal,
        fill=0.0, channel_multiplier=-1))
    rowsel, indone = env["rowsel"], env["indone"]
    P.gp(lambda g: g.memset(rowsel[:], 0.0))
    P.gp(lambda g: g.memset(indone[:], 1.0))
    for base_r in (0, 32, 64):
        P.gp(lambda g, base_r=base_r: g.affine_select(
            rowsel[:], rowsel[:], pattern=[[-1, GI]], compare_op=ALU.not_equal,
            fill=1.0, base=-base_r, channel_multiplier=1))
    tsel = P.do("gpsimd", lambda g: g.nop(), sem="gp")
    P.wait("vector", "gp", tsel)
    for gi in range(GI):
        P.do("vector", lambda v, gi=gi: v.tensor_scalar(
            indg[:, gi * 128:(gi + 1) * 128], indone[:],
            rowsel[:, gi:gi + 1], None, ALU.mult), sem="pre")

    prev = None
    for z_, s_, out_ in ((env["zg"], env["sg"], env["zsg_d"]),
                         (env["zu"], env["su"], env["zsu_d"])):
        for c in range(16):
            sl = slice(c * CH, (c + 1) * CH)
            if prev is not None:
                P.wait("gpsimd", prev)
                P.wait("sync", prev)
            tz = P.dma("gpsimd", xld[0:GI, 0:CH], z_[:, sl], sem="pz")
            ts = P.dma("sync", xld[0:GI, CH:2 * CH], s_[:, sl], sem="ps")
            P.wait("vector", tz)
            P.wait("vector", ts)
            tv = P.do("vector", lambda v: v.scalar_tensor_tensor(
                xld[0:GI, 2 * CH:3 * CH], xld[0:GI, 0:CH], -1.0,
                xld[0:GI, CH:2 * CH], ALU.mult, ALU.mult), sem="dv")
            P.wait("gpsimd", "dv", tv)
            prev = P.dma("gpsimd", out_[:, sl], xld[0:GI, 2 * CH:3 * CH], sem="pw")
    CH2 = D // 2
    for c in range(2):
        sl = slice(c * CH2, (c + 1) * CH2)
        P.wait("gpsimd", prev)
        P.wait("sync", prev)
        tz = P.dma("gpsimd", xld[0:86, 0:CH2], env["zd"][:, sl], sem="pz")
        ts = P.dma("sync", xld[0:86, CH2:2 * CH2], env["sd"][:, sl], sem="ps")
        P.wait("vector", tz)
        P.wait("vector", ts)
        t1 = P.do("vector", lambda v: v.tensor_scalar_mul(
            xld[0:86, CH2:2 * CH2], xld[0:86, CH2:2 * CH2], 0.5), sem="dv")
        P.wait("vector", "dv", t1)
        t2 = P.do("vector", lambda v: v.scalar_tensor_tensor(
            xld[0:86, 0:CH2], xld[0:86, 0:CH2], -1.0,
            xld[0:86, CH2:2 * CH2], ALU.mult, ALU.mult), sem="dv")
        P.wait("gpsimd", "dv", t2)
        P.dma("gpsimd", env["zsd_d"][:, sl], xld[0:86, 0:CH2], sem="pw")
        prev = P.dma("gpsimd", env["sdeff_d"][:, sl], xld[0:86, CH2:2 * CH2], sem="pw")
    P.wait("sync", prev)
    env["_prep_tok"] = P.dma("sync", env["sdef_sb"][:, :], env["sdeff_d"][:, :], sem="ps")


def emit_ln(P, env, src_d, write_xbase):
    xld, xn, lnst = env["xld"], env["xn_sb"], env["lnst"]
    xs_sb, xsT, xnT = env["xs_sb"], env["xsT"], env["xnT"]
    xsf = env["xsf"]
    ident, bank = env["ident"], env["bank"]
    s1, s2, m, var, sd_, rstd, bias = (lnst[:, i:i + 1] for i in range(7))

    P.wait("sync", "xo0", P.tok("xo0"))
    P.wait("sync", "xo1", P.tok("xo1"))
    P.wait("tensor", "dv", P.tok("dv"))
    P.wait("tensor", "ac", P.tok("ac"))

    tr_ev = []
    for tt in range(NTT):
        sl = slice(tt * 128, (tt + 1) * 128)
        if tt >= 1:
            P.wait("sync", "ac", env["_xld_free"])
        tld = P.dma("sync", xld[:, :], src_d[sl, :], sem="xli")
        if write_xbase:
            P.dma("sync", env["xbase_d"][sl, :], src_d[sl, :], sem="xbw")
        P.wait("scalar", tld)
        if tr_ev:
            P.wait("scalar", "pe", env["_ln_trmm"])
            P.wait("scalar", "dv", env["_xsf_cp"])
        ts1 = P.do("scalar", lambda s: s.activation(
            out=xn[:, :], in_=xld[:, :], func=AF.Copy, accum_out=s1), sem="ac")
        P.wait("scalar", "ac", ts1)
        ts2 = P.do("scalar", lambda s: s.activation(
            out=xn[:, :], in_=xld[:, :], func=AF.Square, accum_out=s2), sem="ac")
        P.wait("vector", "ac", ts2)
        P.do("vector", lambda v: v.tensor_scalar(m, s1, 1.0 / D, None, ALU.mult), sem="dv")
        t_mv = P.do("vector", lambda v: v.tensor_scalar(var, s2, 1.0 / D, EPS, ALU.mult, ALU.add), sem="dv")
        P.wait("vector", "dv", t_mv)
        tm = P.do("vector", lambda v: v.scalar_tensor_tensor(
            s1, s1, -1.0 / (float(D) * float(D)), s1, ALU.mult, ALU.mult), sem="dv")
        P.wait("vector", "dv", tm)
        tv = P.do("vector", lambda v: v.tensor_add(var, var, s1), sem="dv")
        P.wait("scalar", "dv", tv)
        tsq = P.do("scalar", lambda s: s.activation(out=sd_, in_=var, func=AF.Sqrt), sem="ac")
        P.wait("vector", "ac", tsq)
        tr = P.do("vector", lambda v: v.reciprocal(rstd, sd_), sem="dv")
        P.wait("vector", "dv", tr)
        tb = P.do("vector", lambda v: v.scalar_tensor_tensor(
            bias, m, -1.0, rstd, ALU.mult, ALU.mult), sem="dv")
        P.wait("scalar", "dv", tb)
        P.wait("scalar", "ac", ts2)
        txn = P.do("scalar", lambda s: s.activation(
            out=xn[:, :], in_=xld[:, :], func=AF.Identity,
            scale=rstd, bias=bias), sem="ac")
        env["_xld_free"] = txn
        P.wait("vector", "ac", txn)
        if "_xsf_cp" in env:
            P.wait("vector", "dv", env["_xsf_cp"])
        trd = None
        for gi in range(GI):
            trd = P.do("vector", lambda v, gi=gi: v.reduce_sum(
                xsf[:, gi:gi + 1],
                xn[:, gi * 128:(gi + 1) * 128], axis=AX.X), sem="dv")
        P.wait("vector", "dv", trd)
        env["_xsf_cp"] = P.do("vector", lambda v, tt=tt: v.tensor_copy(
            xs_sb[:, tt * GI:(tt + 1) * GI], xsf[:, :]), sem="dv")
        P.wait("tensor", "ac", txn)
        for gi in range(GI):
            ps = bank[gi % 4]
            if len(tr_ev) >= 4:
                P.wait("tensor", "ac", tr_ev[-4])
            tp = P.do("tensor", lambda t, gi=gi, ps=ps: t.matmul(
                ps[:, 0:128], xn[:, gi * 128:(gi + 1) * 128],
                ident[:], start=True, stop=True), sem="pe")
            if gi == GI - 1:
                env["_ln_trmm"] = tp
            P.wait("scalar", "pe", tp)
            te = P.do("scalar", lambda s, gi=gi, tt=tt, ps=ps: s.activation(
                out=xnT[:, gi * TOK + tt * 128: gi * TOK + (tt + 1) * 128],
                in_=ps[:, 0:128], func=AF.Copy), sem="ac")
            tr_ev.append(te)
    P.wait("tensor", "dv", P.tok("dv"))
    P.wait("tensor", "ac", tr_ev[-1])
    xst_dv = {}
    for tc in range(NTT):
        ps = bank[tc % 2]
        if tc >= 2:
            P.wait("tensor", "dv", xst_dv[tc - 2])
        tp = P.do("tensor", lambda t, tc=tc, ps=ps: t.matmul(
            ps[0:GI, 0:128], xs_sb[:, tc * GI:(tc + 1) * GI], ident[:],
            start=True, stop=True), sem="pe")
        P.wait("vector", "pe", tp)
        xst_dv[tc] = P.do("vector", lambda v, tc=tc, ps=ps: v.tensor_copy(
            xsT[:, tc * 128:(tc + 1) * 128], ps[0:GI, 0:128]),
            sem=("dv" if tc < NTT - 1 else "ln"))
    env["_ln_done"] = xst_dv[NTT - 1]
    env["_lnxst_dv"] = P.tok("dv")


def emit_gateup(P, env):
    qg_sb, qu_sb, scb, zscb = env["qg_sb"], env["qu_sb"], env["scb"], env["zscb"]
    xnT, xsT, indg = env["xnT"], env["xsT"], env["indg"]
    gsil, hct, hs_sb = env["gsil"], env["hct"], env["hs_sb"]
    bank = env["bank"]
    onescol, ident, hsT = env["onescol"], env["ident"], env["hsT"]
    ht_d, wd_d, wdq = env["ht_d"], env["wd_d"], env["wdq"]
    sdef_sb = env["sdef_sb"]

    P.wait("tensor", "ln", env["_ln_done"])
    P.wait("gpsimd", "ln", env["_ln_done"])
    P.wait("sync", "ln", env["_ln_done"])
    P.wait("tensor", env["_prep_tok"])
    P.wait("tensor", "ac", P.tok("ac"))

    qwg_v = env["qwg"].rearrange("(g p) c -> p g c", p=128)
    qwu_v = env["qwu"].rearrange("(g p) c -> p g c", p=128)
    qwd_v = env["qwd"].rearrange("(ct p) c -> ct p c", p=128)

    wtok, utok, dtok = {}, {}, {}
    ht_wtok = [None] * GO
    wd_wtok = [None] * GO
    env["_ht_wtok"] = ht_wtok
    env["_wd_wtok"] = wd_wtok

    def load_cb(mat, cb):
        src = qwg_v if mat == 0 else qwu_v
        dst = qg_sb if mat == 0 else qu_sb
        s_in = env["sg"] if mat == 0 else env["su"]
        zs_in = env["zsg_d"] if mat == 0 else env["zsu_d"]
        bb = cb % 2
        if (mat, cb - 2) in utok:
            P.wait("gpsimd", "pe", utok[(mat, cb - 2)])
        if (mat, cb - 1) in utok:
            P.wait("gpsimd", "pe", utok[(mat, cb - 1)])
            P.wait("sync", "pe", utok[(mat, cb - 1)])
        sl = slice(cb * 256, (cb + 1) * 256)
        t1 = P.dma("gpsimd", dst[:, bb, :].rearrange("p (g c) -> p g c", c=256),
                   src[:, :, sl], sem=("qg%d" % bb if mat == 0 else "qu%d" % bb))
        t2 = P.dma("gpsimd", scb[:, mat * 256:(mat + 1) * 256], s_in[:, sl],
                   sem=("scg" if mat == 0 else "scu"))
        t3 = P.dma("sync", zscb[:, mat * 256:(mat + 1) * 256], zs_in[:, sl],
                   sem=("zsgq" if mat == 0 else "zsuq"))
        wtok[(mat, cb)] = (t1, t2, t3)

    def load_dct(ct):
        if ct >= 1 and wd_wtok[ct - 1] is not None:
            P.wait("gpsimd", wd_wtok[ct - 1])
        dtok[ct] = P.dma("gpsimd", wdq[:, :], qwd_v[ct, :, :], sem="wdqi")

    bc_state = [0, None, None]

    def bcast(lhs_lo, lhs_n, lhs_col, rhs, ncol):
        i = bc_state[0] % 2
        bc_state[0] += 1
        bkx = bank[4 + i]
        if bc_state[1 + i] is not None:
            P.wait("tensor", "dv", bc_state[1 + i])
        tp = P.do("tensor", lambda t: t.matmul(
            bkx[:, 0:ncol], indg[lhs_lo:lhs_lo + lhs_n,
                                 lhs_col:lhs_col + 128], rhs,
            start=True, stop=True), sem="pe")
        return i, bkx, tp

    def bcast_done(i, tok):
        bc_state[1 + i] = tok

    load_cb(0, 0)
    load_cb(1, 0)
    load_dct(0)

    hs_ev = {}
    acc_free = {}
    hct_free = None
    hs_pe_tok = None

    for cb in range(NCB):
        for mat in range(2):
            W = (qg_sb if mat == 0 else qu_sb)[:, cb % 2, :]
            t1, t2, t3 = wtok[(mat, cb)]
            P.wait("tensor", t2)
            P.wait("vector", t1)
            for gi in range(GI):
                i, bkx, tp = bcast(0, GI, gi * 128,
                                   scb[:, mat * 256:(mat + 1) * 256], 256)
                P.wait("vector", "pe", tp)
                tv = P.do("vector", lambda v, gi=gi, W=W, bkx=bkx: v.tensor_mul(
                    W[:, gi * 256:(gi + 1) * 256], W[:, gi * 256:(gi + 1) * 256],
                    bkx[:, 0:256]), sem="dv")
                bcast_done(i, tv)
            if mat == 0:
                env["_dqg"] = P.tok("dv")
            else:
                env["_dqu"] = P.tok("dv")
        for ct in range(2):
            gct = 2 * cb + ct
            W0 = qg_sb[:, cb % 2, :]
            W1 = qu_sb[:, cb % 2, :]
            _, _, t3g = wtok[(0, cb)]
            _, _, t3u = wtok[(1, cb)]
            P.wait("tensor", t3g)
            P.wait("tensor", t3u)
            P.wait("tensor", "dv", env["_dqg"])
            P.wait("tensor", "dv", env["_dqu"])
            for tb in range(2):
                bk = bank[tb]
                for gi in range(GI):
                    if gi == 0 and (0, tb) in acc_free:
                        P.wait("tensor", "ac", acc_free[(0, tb)])
                    P.do("tensor", lambda t, gi=gi, ct=ct, tb=tb, bk=bk, W0=W0: t.matmul(
                        bk[:], W0[:, gi * 256 + ct * 128: gi * 256 + ct * 128 + 128],
                        xnT[:, gi * TOK + tb * 512: gi * TOK + (tb + 1) * 512],
                        start=(gi == 0), stop=False), sem="pe")
                tzg = P.do("tensor", lambda t, ct=ct, tb=tb, bk=bk: t.matmul(
                    bk[:], zscb[0:GI, ct * 128: ct * 128 + 128],
                    xsT[:, tb * 512:(tb + 1) * 512],
                    start=False, stop=True), sem="pe")
                env["_lastg_pe"] = tzg
                P.wait("scalar", "pe", tzg)
                tsl = P.do("scalar", lambda s, tb=tb, bk=bk: s.activation(
                    out=gsil[:, tb * 512:(tb + 1) * 512], in_=bk[:],
                    func=AF.Silu), sem="ac")
                acc_free[(0, tb)] = tsl
            for tb in range(2):
                bk = bank[2 + tb]
                for gi in range(GI):
                    if gi == 0 and (1, tb) in acc_free:
                        P.wait("tensor", "dv", acc_free[(1, tb)])
                    P.do("tensor", lambda t, gi=gi, ct=ct, tb=tb, bk=bk, W1=W1: t.matmul(
                        bk[:], W1[:, gi * 256 + ct * 128: gi * 256 + ct * 128 + 128],
                        xnT[:, gi * TOK + tb * 512: gi * TOK + (tb + 1) * 512],
                        start=(gi == 0), stop=False), sem="pe")
                tzu = P.do("tensor", lambda t, ct=ct, tb=tb, bk=bk: t.matmul(
                    bk[:], zscb[0:GI, 256 + ct * 128: 256 + ct * 128 + 128],
                    xsT[:, tb * 512:(tb + 1) * 512],
                    start=False, stop=True), sem="pe")
                env["_lastu_pe"] = tzu
                P.wait("vector", "pe", tzu)
                P.wait("vector", "ac", acc_free[(0, tb)])
                if hct_free is not None:
                    P.wait("vector", hct_free)
                if hs_pe_tok is not None:
                    P.wait("vector", "pe", hs_pe_tok)
                th = P.do("vector", lambda v, tb=tb, bk=bk: v.tensor_mul(
                    hct[:, tb * 512:(tb + 1) * 512], bk[:],
                    gsil[:, tb * 512:(tb + 1) * 512]), sem="dv")
                acc_free[(1, tb)] = th
            th_all = P.tok("dv")
            hb = bank[6 + gct % 2]
            P.wait("tensor", "dv", th_all)
            if gct - 2 in hs_ev:
                P.wait("tensor", "dv", hs_ev[gct - 2])
            for tc in range(NTT):
                P.do("tensor", lambda t, tc=tc, hb=hb: t.matmul(
                    hb[:, tc:tc + 1], hct[:, tc * 128:(tc + 1) * 128],
                    onescol[:], start=True, stop=True), sem="pe")
            hs_pe_tok = P.tok("pe")
            P.wait("vector", "pe", hs_pe_tok)
            tev = P.do("vector", lambda v, gct=gct, hb=hb: v.tensor_copy(
                hs_sb[:, gct * NTT:(gct + 1) * NTT], hb[:, 0:NTT]), sem="dv")
            hs_ev[gct] = tev
            P.wait("scalar", "dv", th_all)
            hct_free = P.dma("scalar", ht_d[gct, :, :], hct[:, :], sem="htw")
            ht_wtok[gct] = hct_free
            ct2 = gct
            P.wait("vector", dtok[ct2])
            base = (ct2 // 32) * 32
            nrow = min(32, GO - base)
            last_dq = None
            for mb in range(8):
                i, bkx, tp = bcast(base, nrow, (ct2 - base) * 128,
                                   sdef_sb[base:base + nrow, mb * 512:(mb + 1) * 512],
                                   512)
                P.wait("vector", "pe", tp)
                tv = P.do("vector", lambda v, mb=mb, bkx=bkx: v.tensor_mul(
                    wdq[:, mb * 512:(mb + 1) * 512], wdq[:, mb * 512:(mb + 1) * 512],
                    bkx[:, 0:512]), sem="dv")
                bcast_done(i, tv)
                last_dq = tv
            P.wait("gpsimd", "dv", last_dq)
            wd_wtok[ct2] = P.dma("gpsimd", wd_d[ct2, :, :], wdq[:, :], sem="wdw")
            if ct2 + 1 < GO:
                load_dct(ct2 + 1)
        utok[(0, cb)] = env["_lastg_pe"]
        utok[(1, cb)] = env["_lastu_pe"]
        if cb + 1 < NCB:
            load_cb(0, cb + 1)
            load_cb(1, cb + 1)
    P.wait("tensor", "dv", P.tok("dv"))
    hst_dv = {}
    for tc in range(NTT):
        ps = bank[tc % 2]
        if tc >= 2:
            P.wait("tensor", "dv", hst_dv[tc - 2])
        tp = P.do("tensor", lambda t, tc=tc, ps=ps: t.matmul(
            ps[0:GO, 0:128], hs_sb.rearrange("p (ct tc) -> p tc ct", tc=NTT)[:, tc, :],
            ident[:], start=True, stop=True), sem="pe")
        P.wait("vector", "pe", tp)
        hst_dv[tc] = P.do("vector", lambda v, tc=tc, ps=ps: v.tensor_copy(
            hsT[:, tc * 128:(tc + 1) * 128], ps[0:GO, 0:128]),
            sem=("dv" if tc < NTT - 1 else "ln"))
    env["_gu_done"] = hst_dv[NTT - 1]


def emit_down(P, env):
    htin, wdt, xacc = env["htin"], env["wdt"], env["xacc"]
    zsdmb, hsT = env["zsdmb"], env["hsT"]
    bank = env["bank"]
    ht_d, wd_d, xbase_d, xout = env["ht_d"], env["wd_d"], env["xbase_d"], env["xout"]
    ht_wtok, wd_wtok = env["_ht_wtok"], env["_wd_wtok"]

    P.wait("tensor", "ln", env["_gu_done"])
    P.wait("vector", "ln", env["_gu_done"])
    P.wait("sync", "xbw", P.tok("xbw"))

    htok, wtok2 = {}, {}
    dn_use = {}
    env["_dn_use"] = dn_use

    def load_i(i):
        mb, ct = divmod(i, GO)
        bb = i % 2
        if i >= 2:
            P.wait("scalar", "pe", dn_use[i - 2])
            P.wait("sync", "pe", dn_use[i - 2])
        P.wait("scalar", ht_wtok[ct])
        P.wait("sync", wd_wtok[ct])
        htok[i] = P.dma("scalar", htin[:, bb, :], ht_d[ct, :, :], sem="ht%d" % bb)
        wtok2[i] = P.dma("sync", wdt[:, bb, :],
                         wd_d[ct, :, mb * 512:(mb + 1) * 512], sem="wd%d" % bb)

    load_i(0)
    load_i(1)
    zt_prev = None
    evac_toks = [None] * NTT
    xw_tok = {}
    for mb in range(8):
        if zt_prev is not None:
            P.wait("sync", "pe", zt_prev)
        tz = P.dma("sync", zsdmb[:, :], env["zsd_d"][:, mb * 512:(mb + 1) * 512],
                   sem="zsm")
        for ct in range(GO):
            i = mb * GO + ct
            bb = i % 2
            P.wait("tensor", htok[i])
            P.wait("tensor", wtok2[i])
            for tt in range(NTT):
                if ct == 0 and evac_toks[tt] is not None:
                    P.wait("tensor", "dv", evac_toks[tt])
                P.do("tensor", lambda t, tt=tt, bb=bb: t.matmul(
                    bank[tt][:], htin[:, bb, tt * 128:(tt + 1) * 128],
                    wdt[:, bb, :], start=(ct == 0), stop=False), sem="pe")
            dn_use[i] = P.tok("pe")
            if i + 2 < 8 * GO:
                load_i(i + 2)
        P.wait("tensor", tz)
        for tt in range(NTT):
            tzc = P.do("tensor", lambda t, tt=tt: t.matmul(
                bank[tt][:], hsT[:, tt * 128:(tt + 1) * 128], zsdmb[:, :],
                start=False, stop=True), sem="pe")
            if tt == NTT - 1:
                zt_prev = tzc
            if (tt % 2) in xw_tok:
                P.wait("sync", xw_tok[tt % 2])
            txa = P.dma("sync", xacc[:, tt % 2, :],
                        xbase_d[tt * 128:(tt + 1) * 128, mb * 512:(mb + 1) * 512],
                        sem="xa%d" % (tt % 2))
            P.wait("vector", "pe", tzc)
            P.wait("vector", txa)
            ta = P.do("vector", lambda v, tt=tt: v.tensor_add(
                xacc[:, tt % 2, :], xacc[:, tt % 2, :], bank[tt][:]), sem="dv")
            evac_toks[tt] = ta
            P.wait("sync", "dv", ta)
            xw_tok[tt % 2] = P.dma(
                "sync", xout[tt * 128:(tt + 1) * 128, mb * 512:(mb + 1) * 512],
                xacc[:, tt % 2, :], sem="xo%d" % (tt % 2))
    env["_down_done"] = P.tok("dv")


def emit_projections(P, env):
    qg_sb, qu_sb = env["qg_sb"], env["qu_sb"]
    xnT, pstg = env["xnT"], env["pstg"]
    bank = env["bank"]

    jobs = [(env["wq"], env["qt"], D, QSC),
            (env["wk"], env["kt"], NKV * HD, 1.0),
            (env["wv"], env["vt"], NKV * HD, 1.0)]

    P.wait("gpsimd", "dv", env["_down_done"])
    P.wait("gpsimd", "pe", P.tok("pe"))
    P.wait("tensor", "ln", env["_ln_done"])
    P.wait("tensor", "ac", P.tok("ac"))
    P.wait("scalar", "dv", env["_down_done"])

    seq = []
    for (w_in, out_d, ncols, scale) in jobs:
        w_v = w_in.rearrange("(g p) c -> p g c", p=128)
        for db in range(ncols // 256):
            seq.append((w_v, out_d, db, scale))

    ltoks, used, bufs = {}, {}, {}

    def load_j(j):
        w_v, out_d, db, scale = seq[j]
        buf = (qg_sb if (j % 4) < 2 else qu_sb)[:, j % 2, :]
        if (j - 4) in used:
            P.wait("gpsimd", "pe", used[j - 4])
        sl = slice(db * 256, (db + 1) * 256)
        ltoks[j] = P.dma("gpsimd", buf.rearrange("p (g c) -> p g c", c=256),
                         w_v[:, :, sl], sem="pj%d" % (j % 4))
        return buf

    for j in range(min(4, len(seq))):
        bufs[j] = load_j(j)
    ev_free = {}
    pj_w = {}
    for j in range(len(seq)):
        w_v, out_d, db, scale = seq[j]
        buf = bufs[j]
        P.wait("tensor", ltoks[j])
        for ct in range(2):
            for tb in range(2):
                bki = ct * 2 + tb
                bk = bank[bki]
                if bki in ev_free:
                    P.wait("tensor", "ac", ev_free[bki])
                for gi in range(GI):
                    P.do("tensor", lambda t, gi=gi, ct=ct, tb=tb, bk=bk, buf=buf: t.matmul(
                        bk[:], buf[:, gi * 256 + ct * 128: gi * 256 + ct * 128 + 128],
                        xnT[:, gi * TOK + tb * 512: gi * TOK + (tb + 1) * 512],
                        start=(gi == 0), stop=(gi == GI - 1)), sem="pe")
                tmm = P.tok("pe")
                P.wait("scalar", "pe", tmm)
                if (bki % 2) in pj_w:
                    P.wait("scalar", pj_w[bki % 2])
                tev = P.do("scalar", lambda s, bk=bk, bki=bki, scale=scale: s.activation(
                    out=pstg[:, bki % 2, :], in_=bk[:], func=AF.Copy, scale=scale),
                    sem="ac")
                ev_free[bki] = tev
                P.wait("scalar", "ac", tev)
                r0 = db * 256 + ct * 128
                pj_w[bki % 2] = P.dma(
                    "scalar", out_d[r0:r0 + 128, tb * 512:(tb + 1) * 512],
                    pstg[:, bki % 2, :], sem="pw%d" % (bki % 2))
        used[j] = P.tok("pe")
        if j + 4 < len(seq):
            bufs[j + 4] = load_j(j + 4)


def emit_oproj(P, env):
    qg_sb, qu_sb = env["qg_sb"], env["qu_sb"]
    xnT, xacc = env["xnT"], env["xacc"]
    bank = env["bank"]
    attnT, wo, xin, xbase_d = env["attnT"], env["wo"], env["xin"], env["xbase_d"]

    at_v = attnT.rearrange("(g p) c -> p g c", p=128)
    t_at = P.dma("gpsimd", xnT.rearrange("p (g c) -> p g c", c=TOK), at_v, sem="ati")
    wo_v = wo.rearrange("(g p) c -> p g c", p=128)

    wtoks, wuse, bufs = {}, {}, {}

    def load_w(mbq):
        buf = (qg_sb if (mbq % 4) < 2 else qu_sb)[:, mbq % 2, :]
        if (mbq - 4) in wuse:
            P.wait("gpsimd", "pe", wuse[mbq - 4])
        sl = slice(mbq * 256, (mbq + 1) * 256)
        wtoks[mbq] = P.dma("gpsimd", buf.rearrange("p (g c) -> p g c", c=256),
                           wo_v[:, :, sl], sem="pj%d" % (mbq % 4))
        return buf

    for j in range(4):
        bufs[j] = load_w(j)
    P.wait("tensor", t_at)
    evd = {}
    xw = {}
    for mbq in range(D // 256):
        buf = bufs[mbq]
        P.wait("tensor", wtoks[mbq])
        for gi in range(GI):
            for tt in range(NTT):
                if gi == 0 and tt in evd:
                    P.wait("tensor", "dv", evd[tt])
                P.do("tensor", lambda t, gi=gi, tt=tt, buf=buf: t.matmul(
                    bank[tt][:, 0:256],
                    xnT[:, gi * TOK + tt * 128: gi * TOK + (tt + 1) * 128],
                    buf[:, gi * 256:(gi + 1) * 256],
                    start=(gi == 0), stop=(gi == GI - 1)), sem="pe")
        tmm = P.tok("pe")
        for tt in range(NTT):
            if (tt % 2) in xw:
                P.wait("sync", xw[tt % 2])
            txa = P.dma("sync", xacc[:, tt % 2, 0:256],
                        xin[tt * 128:(tt + 1) * 128, mbq * 256:(mbq + 1) * 256],
                        sem="xa%d" % (tt % 2))
            P.wait("vector", "pe", tmm)
            P.wait("vector", txa)
            ta = P.do("vector", lambda v, tt=tt: v.tensor_add(
                xacc[:, tt % 2, 0:256], xacc[:, tt % 2, 0:256],
                bank[tt][:, 0:256]), sem="dv")
            evd[tt] = ta
            P.wait("sync", "dv", ta)
            xw[tt % 2] = P.dma(
                "sync", xbase_d[tt * 128:(tt + 1) * 128, mbq * 256:(mbq + 1) * 256],
                xacc[:, tt % 2, 0:256], sem="xo%d" % (tt % 2))
        wuse[mbq] = P.tok("pe")
        if mbq + 4 < D // 256:
            bufs[mbq + 4] = load_w(mbq + 4)
    env["_op_done"] = P.do("vector", lambda v: v.nop(), sem="ln")



def build_attention():
    nc = bass.Bass("TRN2", target_bir_lowering=False, debug=False)
    ap = lambda t: t.ap()
    qtp = ap(nc.dram_tensor("qtp", [2, 4 * HD, S], BF16, kind="ExternalInput"))
    ktp = ap(nc.dram_tensor("ktp", [2, HD, S], BF16, kind="ExternalInput"))
    vtp = ap(nc.dram_tensor("vtp", [2, HD, S], BF16, kind="ExternalInput"))
    ao = ap(nc.dram_tensor("ao", [2 * 4 * HD, S], BF16, kind="ExternalOutput"))

    import contextlib
    with contextlib.ExitStack() as _es:
        _e = _es.enter_context
        qts = _e(nc.sbuf_tensor("qts", [128, 4 * S], BF16))
        kts = _e(nc.sbuf_tensor("kts", [128, S], BF16))
        vtq = _e(nc.sbuf_tensor("vtq", [128, S], BF16))
        vts = _e(nc.sbuf_tensor("vts", [128, S], BF16))
        et = _e(nc.sbuf_tensor("et", [128, 2, 512], BF16))
        eacc = _e(nc.sbuf_tensor("eacc", [128, 512], F32))
        eaccb = _e(nc.sbuf_tensor("eaccb", [128, 512], BF16))
        lrec = _e(nc.sbuf_tensor("lrec", [128, 512], F32))
        masks = _e(nc.sbuf_tensor("masks", [128, 4 * 512], BF16))
        ones128 = _e(nc.sbuf_tensor("ones128", [128, 128], BF16))
        ident = _e(nc.sbuf_tensor("ident", [128, 128], BF16))
        astg = _e(nc.sbuf_tensor("astg", [128, 2, 512], BF16))
        expb = _e(nc.sbuf_tensor("expb", [128, 1], F32))
        bk0 = _e(nc.psum_tensor("bk0", [128, 512], F32))
        bk1 = _e(nc.psum_tensor("bk1", [128, 512], F32))
        bk2 = _e(nc.psum_tensor("bk2", [128, 512], F32))
        bk3 = _e(nc.psum_tensor("bk3", [128, 512], F32))
        bk4 = _e(nc.psum_tensor("bk4", [128, 512], F32))
        bk5 = _e(nc.psum_tensor("bk5", [128, 512], F32))
        semnames = ["pe", "dv", "ac", "use", "gp", "qld", "kld", "vld",
                    "ao0", "ao1"]
        handles = {n: _e(nc.semaphore(n)) for n in semnames}
        block = _e(nc.Block())
        bank = [bk0, bk1, bk2, bk3, bk4, bk5]

        def body(P):
            P.gp(lambda g: g.memset(ones128[:], 1.0), sem="use")
            P.gp(lambda g: g.memset(expb[:], EXPB), sem="use")
            P.gp(lambda g: g.memset(ident[:], 1.0), sem="use")
            P.gp(lambda g: g.affine_select(
                ident[:], ident[:], pattern=[[1, 128]], compare_op=ALU.is_equal,
                fill=0.0, channel_multiplier=-1), sem="use")
            P.gp(lambda g: g.memset(masks[:], 1.0), sem="use")
            tmask = None
            for j in range(4):
                tmask = P.gp(lambda g, j=j: g.affine_select(
                    masks[:, j * 512:(j + 1) * 512], masks[:, j * 512:(j + 1) * 512],
                    pattern=[[1, 512]], compare_op=ALU.is_ge, fill=0.0,
                    base=-128 * j, channel_multiplier=-1), sem="use")
            ao_last = {}
            for pp in range(2):
                if pp == 1:
                    P.wait("sync", "ao0", P.tok("ao0"))
                    P.wait("sync", "ao1", P.tok("ao1"))
                    P.wait("sync", "ac", P.tok("ac"))
                    P.wait("sync", "pe", P.tok("pe"))
                tq = P.dma("sync", qts.rearrange("p (h c) -> p h c", c=S),
                           qtp[pp].rearrange("(h p) c -> p h c", p=128), sem="qld")
                tk = P.dma("sync", kts[:, :], ktp[pp, :, :], sem="kld")
                tv = P.dma("sync", vtq[:, :], vtp[pp, :, :], sem="vld")
                P.wait("tensor", tv)
                if pp == 0:
                    P.wait("tensor", "use", tmask)
                vt_ev = []
                for kt in range(32):
                    ps = bank[4 + kt % 2]
                    if len(vt_ev) >= 2:
                        P.wait("tensor", "dv", vt_ev[-2])
                    tp = P.do("tensor", lambda t, kt=kt, ps=ps: t.matmul(
                        ps[:, 0:128], vtq[:, kt * 128:(kt + 1) * 128], ident[:],
                        start=True, stop=True), sem="pe")
                    P.wait("vector", "pe", tp)
                    te = P.do("vector", lambda v, kt=kt, ps=ps: v.tensor_copy(
                        vts[:, kt * 128:(kt + 1) * 128], ps[:, 0:128]), sem="dv")
                    vt_ev.append(te)
                P.wait("tensor", tq)
                P.wait("tensor", tk)
                P.wait("tensor", "dv", vt_ev[-1])
                if pp == 0:
                    P.wait("scalar", "use", tmask)
                    P.wait("vector", "use", tmask)
                sc_free = [None, None]
                et_use = [None, None]
                av_free = None
                l_free = None
                lm_tok = None
                for hh in range(4):
                    for qb in range(8):
                        nkt = 4 * (qb + 1)
                        av_tok = None
                        prev_tla = None
                        for kt in range(nkt):
                            sb = bank[kt % 2]
                            if sc_free[kt % 2] is not None:
                                P.wait("tensor", "ac", sc_free[kt % 2])
                            tmm = P.do("tensor", lambda t, kt=kt, hh=hh, qb=qb, sb=sb: t.matmul(
                                sb[:], kts[:, kt * 128:(kt + 1) * 128],
                                qts[:, hh * S + qb * 512: hh * S + (qb + 1) * 512],
                                start=True, stop=True), sem="pe")
                            P.wait("scalar", "pe", tmm)
                            if et_use[kt % 2] is not None:
                                pe_t, dv_t = et_use[kt % 2]
                                P.wait("scalar", "pe", pe_t)
                                P.wait("scalar", "dv", dv_t)
                            tex = P.do("scalar", lambda s, kt=kt, sb=sb: s.activation(
                                out=et[:, kt % 2, :], in_=sb[:], func=AF.Exp,
                                bias=expb[:, 0:1]), sem="ac")
                            sc_free[kt % 2] = tex
                            dj = kt - 4 * qb
                            P.wait("vector", "ac", tex)
                            if dj >= 0:
                                tmk = P.do("vector", lambda v, kt=kt, dj=dj: v.tensor_mul(
                                    et[:, kt % 2, :], et[:, kt % 2, :],
                                    masks[:, dj * 512:(dj + 1) * 512]), sem="dv")
                            else:
                                tmk = None
                            if prev_tla is not None:
                                P.wait("vector", "dv", prev_tla)
                            if kt == 0:
                                tla = P.do("vector", lambda v, kt=kt: v.tensor_copy(
                                    eacc[:, :], et[:, kt % 2, :]), sem="dv")
                            else:
                                tla = P.do("vector", lambda v, kt=kt: v.tensor_add(
                                    eacc[:, :], eacc[:, :], et[:, kt % 2, :]), sem="dv")
                            prev_tla = tla
                            if tmk is not None:
                                P.wait("tensor", "dv", tmk)
                            else:
                                P.wait("tensor", "ac", tex)
                            if kt == 0 and av_free is not None:
                                P.wait("tensor", "dv", av_free)
                            tav = P.do("tensor", lambda t, kt=kt, nkt=nkt: t.matmul(
                                bank[2][:], vts[:, kt * 128:(kt + 1) * 128],
                                et[:, kt % 2, :], start=(kt == 0),
                                stop=(kt == nkt - 1)), sem="pe")
                            et_use[kt % 2] = (tav, tla)
                            av_tok = tav
                        if lm_tok is not None:
                            P.wait("vector", "pe", lm_tok)
                        if prev_tla is not None:
                            P.wait("vector", "dv", prev_tla)
                        tcb = P.do("vector", lambda v: v.tensor_copy(
                            eaccb[:, :], eacc[:, :]), sem="dv")
                        P.wait("tensor", "dv", tcb)
                        if l_free is not None:
                            P.wait("tensor", "dv", l_free)
                        tlm = P.do("tensor", lambda t: t.matmul(
                            bank[3][:], ones128[:], eaccb[:, :],
                            start=True, stop=True), sem="pe")
                        lm_tok = tlm
                        P.wait("vector", "pe", tlm)
                        trc = P.do("vector", lambda v: v.reciprocal(
                            lrec[:, :], bank[3][:]), sem="dv")
                        l_free = trc
                        P.wait("vector", "pe", av_tok)
                        P.wait("vector", "dv", trc)
                        if ((hh * 8 + qb) % 2) in ao_last:
                            P.wait("vector", ao_last[(hh * 8 + qb) % 2])
                        tnm = P.do("vector", lambda v, hh=hh, qb=qb: v.tensor_mul(
                            astg[:, (hh * 8 + qb) % 2, :], bank[2][:], lrec[:, :]),
                            sem="dv")
                        av_free = tnm
                        P.wait("sync", "dv", tnm)
                        ao_last[(hh * 8 + qb) % 2] = P.dma(
                            "sync",
                            ao[pp * 512 + hh * 128: pp * 512 + (hh + 1) * 128,
                               qb * 512:(qb + 1) * 512],
                            astg[:, (hh * 8 + qb) % 2, :],
                            sem="ao%d" % ((hh * 8 + qb) % 2))
            P.wait("sync", "ao0")
            P.wait("sync", "ao1")

        run_spmd_program(nc, block, handles, body)
    return nc



_NC_A = None
_NC_B = None
_NC_C = None


def _get_programs():
    global _NC_A, _NC_B, _NC_C
    if _NC_A is None:
        _NC_A = build_tokenside(with_proj=True, with_oproj=False)
        _NC_B = build_attention()
        _NC_C = build_tokenside(with_proj=False, with_oproj=True)
    return _NC_A, _NC_B, _NC_C


def kernel(L0_attn_hidden_states, L0_attn_w_q, L0_attn_w_k, L0_attn_w_v,
           L0_attn_w_o, L0_ffn_qw_gate, L0_ffn_z_gate, L0_ffn_s_gate,
           L0_ffn_qw_up, L0_ffn_z_up, L0_ffn_s_up, L0_ffn_qw_down,
           L0_ffn_z_down, L0_ffn_s_down):
    nc_a, nc_b, nc_c = _get_programs()

    x0 = np.ascontiguousarray(np.asarray(L0_attn_hidden_states, dtype=np.float32)
                              .reshape(B * S, D))
    qwg = np.ascontiguousarray(np.asarray(L0_ffn_qw_gate, dtype=np.int32))
    qwu = np.ascontiguousarray(np.asarray(L0_ffn_qw_up, dtype=np.int32))
    qwd = np.ascontiguousarray(np.asarray(L0_ffn_qw_down, dtype=np.int32))
    zg = np.ascontiguousarray(np.asarray(L0_ffn_z_gate, dtype=np.int32))
    sg = np.ascontiguousarray(np.asarray(L0_ffn_s_gate, dtype=np.float32))
    zu = np.ascontiguousarray(np.asarray(L0_ffn_z_up, dtype=np.int32))
    su = np.ascontiguousarray(np.asarray(L0_ffn_s_up, dtype=np.float32))
    zd = np.ascontiguousarray(np.asarray(L0_ffn_z_down, dtype=np.int32))
    sd = np.ascontiguousarray(np.asarray(L0_ffn_s_down, dtype=np.float32))
    wq = np.ascontiguousarray(np.asarray(L0_attn_w_q, dtype=np.float32))
    wk = np.ascontiguousarray(np.asarray(L0_attn_w_k, dtype=np.float32))
    wv = np.ascontiguousarray(np.asarray(L0_attn_w_v, dtype=np.float32))
    wo = np.ascontiguousarray(np.asarray(L0_attn_w_o, dtype=np.float32))

    common = dict(qwg=qwg, qwu=qwu, qwd=qwd, zg=zg, sg=sg, zu=zu, su=su,
                  zd=zd, sd=sd)

    in_a = []
    for c in range(NC):
        m = dict(common)
        m.update(dict(xin=x0[c * TOK:(c + 1) * TOK], wq=wq, wk=wk, wv=wv))
        in_a.append(m)
    res_a = run_bass_kernel_spmd(nc_a, in_a, core_ids=list(range(NC))).results
    x1 = [res_a[c]["xout"] for c in range(NC)]
    qts = [res_a[c]["qt"] for c in range(NC)]
    kts = [res_a[c]["kt"] for c in range(NC)]
    vts = [res_a[c]["vt"] for c in range(NC)]

    in_b = []
    for c in range(NC):
        b = (2 * c) // 8
        owners = [4 * b + o for o in range(4)]
        qtp = np.empty((2, 4 * HD, S), dtype=BF16NP)
        ktp = np.empty((2, HD, S), dtype=BF16NP)
        vtp = np.empty((2, HD, S), dtype=BF16NP)
        for k in range(2):
            g = (2 * c + k) % 8
            qtp[k] = np.concatenate([qts[o][g * 512:(g + 1) * 512, :] for o in owners], axis=1)
            ktp[k] = np.concatenate([kts[o][g * 128:(g + 1) * 128, :] for o in owners], axis=1)
            vtp[k] = np.concatenate([vts[o][g * 128:(g + 1) * 128, :] for o in owners], axis=1)
        in_b.append(dict(qtp=qtp, ktp=ktp, vtp=vtp))
    res_b = run_bass_kernel_spmd(nc_b, in_b, core_ids=list(range(NC))).results
    aos = [res_b[c]["ao"] for c in range(NC)]

    attnT_full = []
    for b in range(B):
        blocks = []
        for g in range(8):
            oc = 4 * b + g // 2
            slot = g % 2
            blocks.append(aos[oc][slot * 512:(slot + 1) * 512, :])
        attnT_full.append(np.concatenate(blocks, axis=0))
    in_c = []
    for c in range(NC):
        b = c // 4
        cs = (c % 4) * TOK
        m = dict(common)
        m.update(dict(
            xin=np.ascontiguousarray(x1[c]),
            attnT=np.ascontiguousarray(attnT_full[b][:, cs:cs + TOK]),
            wo=wo,
        ))
        in_c.append(m)
    res_c = run_bass_kernel_spmd(nc_c, in_c, core_ids=list(range(NC))).results
    out = np.concatenate([res_c[c]["xout"] for c in range(NC)], axis=0)
    return out.reshape(B, S, D).astype(np.float32)


# version 02
# speedup vs baseline: 1.1277x; 1.1277x over previous
import numpy as np
import ml_dtypes
import concourse.bass as bass
import concourse.mybir as mybir
from concourse.bass_utils import run_bass_kernel_spmd

F32 = mybir.dt.float32
BF16 = mybir.dt.bfloat16
I32 = mybir.dt.int32
I8 = mybir.dt.int8
AF = mybir.ActivationFunctionType
ALU = mybir.AluOpType
AX = mybir.AxisListType
BF16NP = ml_dtypes.bfloat16

D = 4096
HID = 11008
S = 4096
B = 2
NC = 8
TOK = (B * S) // NC
GI = D // 128
GO = HID // 128
NH = 32
NKV = 8
HD = 128
EPS = 1e-5
EXPB = -20.0
NTT = TOK // 128
NCB = HID // 256
QSC = float(HD) ** -0.5


class Prog:

    def __init__(self, me, eng, handles):
        self.me = me
        self.eng = eng
        self.h = handles
        self.v = {k: 0 for k in handles}

    def do(self, engname, fn, sem=None, n=1):
        tok = None
        if sem is not None:
            self.v[sem] += n
            tok = self.v[sem]
        if engname == self.me:
            ins = fn(self.eng)
            if sem is not None:
                ins.then_inc(self.h[sem], n)
        return tok

    def dma(self, engname, out, in_, sem, **kw):
        t = self.do(engname, lambda e: e.dma_start(out=out, in_=in_, **kw),
                    sem=sem, n=16)
        return (sem, t)

    def wait(self, engname, sem, tok=None):
        if isinstance(sem, tuple):
            sem, tok = sem
        elif isinstance(tok, tuple):
            sem, tok = tok
        if engname == self.me:
            self.eng.wait_ge(self.h[sem], self.v[sem] if tok is None else tok)

    def tok(self, sem):
        return self.v[sem]

    def gp(self, fn, sem="gp"):
        t = self.do("gpsimd", fn, sem=sem)
        self.wait("gpsimd", sem, t)
        return t


def run_spmd_program(nc, block, handles, body):
    def mk(engname):
        def walk(eng):
            body(Prog(engname, eng, handles))
        return walk
    block.sync(mk("sync"))
    block.gpsimd(mk("gpsimd"))
    block.tensor(mk("tensor"))
    block.vector(mk("vector"))
    block.scalar(mk("scalar"))



def build_tokenside(with_proj, with_oproj):
    nc = bass.Bass("TRN2", target_bir_lowering=False, debug=False)
    ap = lambda t: t.ap()

    E = {}
    E["xin"] = ap(nc.dram_tensor("xin", [TOK, D], F32, kind="ExternalInput"))
    E["qwg"] = ap(nc.dram_tensor("qwg", [D, HID], I8, kind="ExternalInput"))
    E["qwu"] = ap(nc.dram_tensor("qwu", [D, HID], I8, kind="ExternalInput"))
    E["qwd"] = ap(nc.dram_tensor("qwd", [HID, D], I8, kind="ExternalInput"))
    E["zg"] = ap(nc.dram_tensor("zg", [GI, HID], I8, kind="ExternalInput"))
    E["sg"] = ap(nc.dram_tensor("sg", [GI, HID], F32, kind="ExternalInput"))
    E["zu"] = ap(nc.dram_tensor("zu", [GI, HID], I8, kind="ExternalInput"))
    E["su"] = ap(nc.dram_tensor("su", [GI, HID], F32, kind="ExternalInput"))
    E["zd"] = ap(nc.dram_tensor("zd", [GO, D], I8, kind="ExternalInput"))
    E["sd"] = ap(nc.dram_tensor("sd", [GO, D], F32, kind="ExternalInput"))
    if with_proj:
        E["wq"] = ap(nc.dram_tensor("wq", [D, D], F32, kind="ExternalInput"))
        E["wk"] = ap(nc.dram_tensor("wk", [D, NKV * HD], F32, kind="ExternalInput"))
        E["wv"] = ap(nc.dram_tensor("wv", [D, NKV * HD], F32, kind="ExternalInput"))
        E["qt"] = ap(nc.dram_tensor("qt", [D, TOK], BF16, kind="ExternalOutput"))
        E["kt"] = ap(nc.dram_tensor("kt", [NKV * HD, TOK], BF16, kind="ExternalOutput"))
        E["vt"] = ap(nc.dram_tensor("vt", [NKV * HD, TOK], BF16, kind="ExternalOutput"))
    if with_oproj:
        E["attnT"] = ap(nc.dram_tensor("attnT", [D, TOK], BF16, kind="ExternalInput"))
        E["wo"] = ap(nc.dram_tensor("wo", [D, D], F32, kind="ExternalInput"))
    E["xout"] = ap(nc.dram_tensor("xout", [TOK, D], F32, kind="ExternalOutput"))
    E["xbase_d"] = ap(nc.dram_tensor("xbase_d", [TOK, D], F32))
    E["zsg_d"] = ap(nc.dram_tensor("zsg_d", [GI, HID], BF16))
    E["zsu_d"] = ap(nc.dram_tensor("zsu_d", [GI, HID], BF16))
    E["zsd_d"] = ap(nc.dram_tensor("zsd_d", [GO, D], BF16))
    E["sdeff_d"] = ap(nc.dram_tensor("sdeff_d", [GO, D], BF16))
    E["ht_d"] = ap(nc.dram_tensor("ht_d", [GO, 128, TOK], BF16))
    E["wd_d"] = ap(nc.dram_tensor("wd_d", [GO, 128, D], BF16))

    import contextlib
    with contextlib.ExitStack() as _es:
        _e = _es.enter_context
        xnT = _e(nc.sbuf_tensor("xnT", [128, GI * TOK], BF16))
        qg_sb = _e(nc.sbuf_tensor("qg", [128, 2, GI * 256], BF16))
        qu_sb = _e(nc.sbuf_tensor("qu", [128, 2, GI * 256], BF16))
        wdq = _e(nc.sbuf_tensor("wdq", [128, D], BF16))
        scb = _e(nc.sbuf_tensor("scb", [GI, 2 * 256], BF16))
        zscb = _e(nc.sbuf_tensor("zscb", [GI, 2 * 256], BF16))
        sdef_sb = _e(nc.sbuf_tensor("sdef", [GO, D], BF16))
        zsdmb = _e(nc.sbuf_tensor("zsdmb", [GO, 512], BF16))
        xsT = _e(nc.sbuf_tensor("xsT", [GI, TOK], BF16))
        hs_sb = _e(nc.sbuf_tensor("hs", [128, NTT * GO], BF16))
        hsT = _e(nc.sbuf_tensor("hsT", [GO, TOK], BF16))
        hct = _e(nc.sbuf_tensor("hct", [128, TOK], BF16))
        htin = _e(nc.sbuf_tensor("htin", [128, 2, TOK], BF16))
        wdt = _e(nc.sbuf_tensor("wdt", [128, 2, 512], BF16))
        ones1 = _e(nc.sbuf_tensor("ones1", [1, 128], BF16))
        onescol = _e(nc.sbuf_tensor("onescol", [128, 1], BF16))
        ident = _e(nc.sbuf_tensor("ident", [128, 128], BF16))
        indg = _e(nc.sbuf_tensor("indg", [128, GI * 128], BF16))
        rowsel = _e(nc.sbuf_tensor("rowsel", [128, GI], F32))
        indone = _e(nc.sbuf_tensor("indone", [128, 128], BF16))
        xld = _e(nc.sbuf_tensor("xld", [128, D], F32))
        xn_sb = _e(nc.sbuf_tensor("xn", [128, D], BF16))
        xs_sb = _e(nc.sbuf_tensor("xs", [128, NTT * GI], BF16))
        xsf = _e(nc.sbuf_tensor("xsf", [128, GI], F32))
        lnst = _e(nc.sbuf_tensor("lnst", [128, 8], F32))
        gsil = _e(nc.sbuf_tensor("gsil", [128, TOK], BF16))
        xacc = _e(nc.sbuf_tensor("xacc", [128, 2, 512], F32))
        pstg = _e(nc.sbuf_tensor("pstg", [128, 2, 512], BF16))
        bk0 = _e(nc.psum_tensor("bk0", [128, 512], F32))
        bk1 = _e(nc.psum_tensor("bk1", [128, 512], F32))
        bk2 = _e(nc.psum_tensor("bk2", [128, 512], F32))
        bk3 = _e(nc.psum_tensor("bk3", [128, 512], F32))
        bk4 = _e(nc.psum_tensor("bk4", [128, 512], F32))
        bk5 = _e(nc.psum_tensor("bk5", [128, 512], F32))
        bk6 = _e(nc.psum_tensor("bk6", [128, 512], F32))
        bk7 = _e(nc.psum_tensor("bk7", [128, 512], F32))
        semnames = ["pe", "dv", "ac", "gp", "pre", "ln", "pz", "ps", "pz0",
                    "pz1", "ps0", "ps1", "pwa", "pwb", "pw", "xli",
                    "xbw", "qg0", "qg1", "qu0", "qu1", "scg", "scu", "zsgq",
                    "zsuq", "wdqi", "wdw", "htw", "ht0", "ht1", "wd0", "wd1",
                    "zsm", "xa0", "xa1", "xo0", "xo1", "pj0", "pj1", "pj2",
                    "pj3", "pw0", "pw1", "ati"]
        handles = {n: _e(nc.semaphore(n)) for n in semnames}
        block = _e(nc.Block())
        env = dict(E)
        env.update(dict(
            xnT=xnT, qg_sb=qg_sb, qu_sb=qu_sb, wdq=wdq, scb=scb, zscb=zscb,
            sdef_sb=sdef_sb, zsdmb=zsdmb, xsT=xsT, hs_sb=hs_sb, hsT=hsT,
            hct=hct, htin=htin, wdt=wdt, ones1=ones1, onescol=onescol,
            ident=ident, indg=indg, xld=xld, xn_sb=xn_sb, xs_sb=xs_sb,
            lnst=lnst, gsil=gsil, xacc=xacc, pstg=pstg, rowsel=rowsel,
            indone=indone, xsf=xsf,
            bank=[bk0, bk1, bk2, bk3, bk4, bk5, bk6, bk7],
        ))

        def body(P):
            for k in [k for k in env if k.startswith("_")]:
                del env[k]
            emit_prep(P, env)
            if with_oproj:
                emit_oproj(P, env)
                base = env["xbase_d"]
            else:
                base = env["xin"]
            emit_ln(P, env, base, write_xbase=(not with_oproj))
            emit_gateup(P, env)
            emit_down(P, env)
            if with_proj:
                emit_ln(P, env, env["xout"], write_xbase=False)
                emit_projections(P, env)
            for s_ in ("xo0", "xo1", "xbw", "pw0", "pw1", "wdw", "htw", "pw"):
                P.wait("sync", s_)

        run_spmd_program(nc, block, handles, body)
    return nc


def emit_prep(P, env):
    ones1, onescol, ident, indg = env["ones1"], env["onescol"], env["ident"], env["indg"]
    xld = env["xld"]
    CH = HID // 16

    P.gp(lambda g: g.memset(ones1[:], 1.0))
    P.gp(lambda g: g.memset(onescol[:], 1.0))
    P.gp(lambda g: g.memset(ident[:], 1.0))
    P.gp(lambda g: g.affine_select(
        ident[:], ident[:], pattern=[[1, 128]], compare_op=ALU.is_equal,
        fill=0.0, channel_multiplier=-1))
    rowsel, indone = env["rowsel"], env["indone"]
    P.gp(lambda g: g.memset(rowsel[:], 0.0))
    P.gp(lambda g: g.memset(indone[:], 1.0))
    for base_r in (0, 32, 64):
        P.gp(lambda g, base_r=base_r: g.affine_select(
            rowsel[:], rowsel[:], pattern=[[-1, GI]], compare_op=ALU.not_equal,
            fill=1.0, base=-base_r, channel_multiplier=1))
    tsel = P.do("gpsimd", lambda g: g.nop(), sem="gp")
    P.wait("vector", "gp", tsel)
    for gi in range(GI):
        P.do("vector", lambda v, gi=gi: v.tensor_scalar(
            indg[:, gi * 128:(gi + 1) * 128], indone[:],
            rowsel[:, gi:gi + 1], None, ALU.mult), sem="pre")

    items = []
    for z_, s_, out_ in ((env["zg"], env["sg"], env["zsg_d"]),
                         (env["zu"], env["su"], env["zsu_d"])):
        for c in range(16):
            items.append((z_, s_, out_, slice(c * CH, (c + 1) * CH)))
    ztk, stk, dvt, wtk = {}, {}, {}, {}

    def prow(i):
        return slice(32 * (i % 2), 32 * (i % 2) + GI)

    def pload(i):
        z_, s_, out_, sl = items[i]
        r = prow(i)
        if i >= 2:
            P.wait("gpsimd", "dv", dvt[i - 2])
            P.wait("sync", "dv", dvt[i - 2])
            P.wait("gpsimd", wtk[i - 2])
            P.wait("sync", wtk[i - 2])
        ztk[i] = P.dma("gpsimd", xld[r, 0:CH], z_[:, sl], sem="pz%d" % (i % 2))
        stk[i] = P.dma("sync", xld[r, CH:2 * CH], s_[:, sl], sem="ps%d" % (i % 2))

    pload(0)
    pload(1)
    prev = None
    for i, (z_, s_, out_, sl) in enumerate(items):
        r = prow(i)
        P.wait("vector", ztk[i])
        P.wait("vector", stk[i])
        tv = P.do("vector", lambda v, r=r: v.scalar_tensor_tensor(
            xld[r, 2 * CH:3 * CH], xld[r, 0:CH], -1.0,
            xld[r, CH:2 * CH], ALU.mult, ALU.mult), sem="dv")
        dvt[i] = tv
        P.wait("gpsimd", "dv", tv)
        prev = P.dma("gpsimd", out_[:, sl], xld[r, 2 * CH:3 * CH],
                     sem="pw%d" % (i % 2) if False else "pwa" if i % 2 == 0 else "pwb")
        wtk[i] = prev
        if i + 2 < len(items):
            pload(i + 2)
    P.wait("gpsimd", wtk[len(items) - 2])
    P.wait("sync", wtk[len(items) - 2])
    P.wait("gpsimd", wtk[len(items) - 1])
    P.wait("sync", wtk[len(items) - 1])
    P.wait("sync", "dv", dvt[len(items) - 1])
    P.wait("gpsimd", "dv", dvt[len(items) - 1])
    CH2 = D // 2
    for c in range(2):
        sl = slice(c * CH2, (c + 1) * CH2)
        P.wait("gpsimd", prev)
        P.wait("sync", prev)
        tz = P.dma("gpsimd", xld[0:86, 0:CH2], env["zd"][:, sl], sem="pz")
        ts = P.dma("sync", xld[0:86, CH2:2 * CH2], env["sd"][:, sl], sem="ps")
        P.wait("vector", tz)
        P.wait("vector", ts)
        t1 = P.do("vector", lambda v: v.tensor_scalar_mul(
            xld[0:86, CH2:2 * CH2], xld[0:86, CH2:2 * CH2], 0.5), sem="dv")
        P.wait("vector", "dv", t1)
        t2 = P.do("vector", lambda v: v.scalar_tensor_tensor(
            xld[0:86, 0:CH2], xld[0:86, 0:CH2], -1.0,
            xld[0:86, CH2:2 * CH2], ALU.mult, ALU.mult), sem="dv")
        P.wait("gpsimd", "dv", t2)
        P.dma("gpsimd", env["zsd_d"][:, sl], xld[0:86, 0:CH2], sem="pw")
        prev = P.dma("gpsimd", env["sdeff_d"][:, sl], xld[0:86, CH2:2 * CH2], sem="pw")
    P.wait("sync", prev)
    env["_prep_tok"] = P.dma("sync", env["sdef_sb"][:, :], env["sdeff_d"][:, :], sem="ps")


def emit_ln(P, env, src_d, write_xbase):
    xld, xn, lnst = env["xld"], env["xn_sb"], env["lnst"]
    xs_sb, xsT, xnT = env["xs_sb"], env["xsT"], env["xnT"]
    xsf = env["xsf"]
    ident, bank = env["ident"], env["bank"]
    s1, s2, m, var, sd_, rstd, bias = (lnst[:, i:i + 1] for i in range(7))

    P.wait("sync", "xo0", P.tok("xo0"))
    P.wait("sync", "xo1", P.tok("xo1"))
    P.wait("tensor", "dv", P.tok("dv"))
    P.wait("tensor", "ac", P.tok("ac"))

    tr_ev = []
    for tt in range(NTT):
        sl = slice(tt * 128, (tt + 1) * 128)
        if tt >= 1:
            P.wait("sync", "ac", env["_xld_free"])
        tld = P.dma("sync", xld[:, :], src_d[sl, :], sem="xli")
        if write_xbase:
            P.dma("sync", env["xbase_d"][sl, :], src_d[sl, :], sem="xbw")
        P.wait("scalar", tld)
        if tr_ev:
            P.wait("scalar", "pe", env["_ln_trmm"])
            P.wait("scalar", "dv", env["_xsf_cp"])
        ts1 = P.do("scalar", lambda s: s.activation(
            out=xn[:, :], in_=xld[:, :], func=AF.Copy, accum_out=s1), sem="ac")
        P.wait("scalar", "ac", ts1)
        ts2 = P.do("scalar", lambda s: s.activation(
            out=xn[:, :], in_=xld[:, :], func=AF.Square, accum_out=s2), sem="ac")
        P.wait("vector", "ac", ts2)
        P.do("vector", lambda v: v.tensor_scalar(m, s1, 1.0 / D, None, ALU.mult), sem="dv")
        t_mv = P.do("vector", lambda v: v.tensor_scalar(var, s2, 1.0 / D, EPS, ALU.mult, ALU.add), sem="dv")
        P.wait("vector", "dv", t_mv)
        tm = P.do("vector", lambda v: v.scalar_tensor_tensor(
            s1, s1, -1.0 / (float(D) * float(D)), s1, ALU.mult, ALU.mult), sem="dv")
        P.wait("vector", "dv", tm)
        tv = P.do("vector", lambda v: v.tensor_add(var, var, s1), sem="dv")
        P.wait("scalar", "dv", tv)
        tsq = P.do("scalar", lambda s: s.activation(out=sd_, in_=var, func=AF.Sqrt), sem="ac")
        P.wait("vector", "ac", tsq)
        tr = P.do("vector", lambda v: v.reciprocal(rstd, sd_), sem="dv")
        P.wait("vector", "dv", tr)
        tb = P.do("vector", lambda v: v.scalar_tensor_tensor(
            bias, m, -1.0, rstd, ALU.mult, ALU.mult), sem="dv")
        P.wait("scalar", "dv", tb)
        P.wait("scalar", "ac", ts2)
        txn = P.do("scalar", lambda s: s.activation(
            out=xn[:, :], in_=xld[:, :], func=AF.Identity,
            scale=rstd, bias=bias), sem="ac")
        env["_xld_free"] = txn
        P.wait("vector", "ac", txn)
        if "_xsf_cp" in env:
            P.wait("vector", "dv", env["_xsf_cp"])
        trd = None
        for gi in range(GI):
            trd = P.do("vector", lambda v, gi=gi: v.reduce_sum(
                xsf[:, gi:gi + 1],
                xn[:, gi * 128:(gi + 1) * 128], axis=AX.X), sem="dv")
        P.wait("vector", "dv", trd)
        env["_xsf_cp"] = P.do("vector", lambda v, tt=tt: v.tensor_copy(
            xs_sb[:, tt * GI:(tt + 1) * GI], xsf[:, :]), sem="dv")
        P.wait("tensor", "ac", txn)
        for gi in range(GI):
            ps = bank[gi % 4]
            if len(tr_ev) >= 4:
                P.wait("tensor", "ac", tr_ev[-4])
            tp = P.do("tensor", lambda t, gi=gi, ps=ps: t.matmul(
                ps[:, 0:128], xn[:, gi * 128:(gi + 1) * 128],
                ident[:], start=True, stop=True), sem="pe")
            if gi == GI - 1:
                env["_ln_trmm"] = tp
            P.wait("scalar", "pe", tp)
            te = P.do("scalar", lambda s, gi=gi, tt=tt, ps=ps: s.activation(
                out=xnT[:, gi * TOK + tt * 128: gi * TOK + (tt + 1) * 128],
                in_=ps[:, 0:128], func=AF.Copy), sem="ac")
            tr_ev.append(te)
    P.wait("tensor", "dv", P.tok("dv"))
    P.wait("tensor", "ac", tr_ev[-1])
    xst_dv = {}
    for tc in range(NTT):
        ps = bank[tc % 2]
        if tc >= 2:
            P.wait("tensor", "dv", xst_dv[tc - 2])
        tp = P.do("tensor", lambda t, tc=tc, ps=ps: t.matmul(
            ps[0:GI, 0:128], xs_sb[:, tc * GI:(tc + 1) * GI], ident[:],
            start=True, stop=True), sem="pe")
        P.wait("vector", "pe", tp)
        xst_dv[tc] = P.do("vector", lambda v, tc=tc, ps=ps: v.tensor_copy(
            xsT[:, tc * 128:(tc + 1) * 128], ps[0:GI, 0:128]),
            sem=("dv" if tc < NTT - 1 else "ln"))
    env["_ln_done"] = xst_dv[NTT - 1]
    env["_lnxst_dv"] = P.tok("dv")


def emit_gateup(P, env):
    qg_sb, qu_sb, scb, zscb = env["qg_sb"], env["qu_sb"], env["scb"], env["zscb"]
    xnT, xsT, indg = env["xnT"], env["xsT"], env["indg"]
    gsil, hct, hs_sb = env["gsil"], env["hct"], env["hs_sb"]
    bank = env["bank"]
    onescol, ident, hsT = env["onescol"], env["ident"], env["hsT"]
    ht_d, wd_d, wdq = env["ht_d"], env["wd_d"], env["wdq"]
    sdef_sb = env["sdef_sb"]

    P.wait("tensor", "ln", env["_ln_done"])
    P.wait("gpsimd", "ln", env["_ln_done"])
    P.wait("sync", "ln", env["_ln_done"])
    P.wait("tensor", env["_prep_tok"])
    P.wait("tensor", "ac", P.tok("ac"))

    qwg_v = env["qwg"].rearrange("(g p) c -> p g c", p=128)
    qwu_v = env["qwu"].rearrange("(g p) c -> p g c", p=128)
    qwd_v = env["qwd"].rearrange("(ct p) c -> ct p c", p=128)

    wtok, utok, dtok = {}, {}, {}
    ht_wtok = [None] * GO
    wd_wtok = [None] * GO
    env["_ht_wtok"] = ht_wtok
    env["_wd_wtok"] = wd_wtok

    def load_cb(mat, cb):
        src = qwg_v if mat == 0 else qwu_v
        dst = qg_sb if mat == 0 else qu_sb
        s_in = env["sg"] if mat == 0 else env["su"]
        zs_in = env["zsg_d"] if mat == 0 else env["zsu_d"]
        bb = cb % 2
        if (mat, cb - 2) in utok:
            P.wait("gpsimd", "pe", utok[(mat, cb - 2)])
        if (mat, cb - 1) in utok:
            P.wait("gpsimd", "pe", utok[(mat, cb - 1)])
            P.wait("sync", "pe", utok[(mat, cb - 1)])
        sl = slice(cb * 256, (cb + 1) * 256)
        t1 = P.dma("gpsimd", dst[:, bb, :].rearrange("p (g c) -> p g c", c=256),
                   src[:, :, sl], sem=("qg%d" % bb if mat == 0 else "qu%d" % bb))
        t2 = P.dma("gpsimd", scb[:, mat * 256:(mat + 1) * 256], s_in[:, sl],
                   sem=("scg" if mat == 0 else "scu"))
        t3 = P.dma("sync", zscb[:, mat * 256:(mat + 1) * 256], zs_in[:, sl],
                   sem=("zsgq" if mat == 0 else "zsuq"))
        wtok[(mat, cb)] = (t1, t2, t3)

    def load_dct(ct):
        if ct >= 1 and wd_wtok[ct - 1] is not None:
            P.wait("gpsimd", wd_wtok[ct - 1])
        dtok[ct] = P.dma("gpsimd", wdq[:, :], qwd_v[ct, :, :], sem="wdqi")

    bc_state = [0, None, None]

    def bcast(lhs_lo, lhs_n, lhs_col, rhs, ncol):
        i = bc_state[0] % 2
        bc_state[0] += 1
        bkx = bank[4 + i]
        if bc_state[1 + i] is not None:
            P.wait("tensor", "dv", bc_state[1 + i])
        tp = P.do("tensor", lambda t: t.matmul(
            bkx[:, 0:ncol], indg[lhs_lo:lhs_lo + lhs_n,
                                 lhs_col:lhs_col + 128], rhs,
            start=True, stop=True), sem="pe")
        return i, bkx, tp

    def bcast_done(i, tok):
        bc_state[1 + i] = tok

    load_cb(0, 0)
    load_cb(1, 0)
    load_dct(0)

    hs_ev = {}
    acc_free = {}
    hct_free = None
    hs_pe_tok = None

    for cb in range(NCB):
        for mat in range(2):
            W = (qg_sb if mat == 0 else qu_sb)[:, cb % 2, :]
            t1, t2, t3 = wtok[(mat, cb)]
            P.wait("tensor", t2)
            P.wait("vector", t1)
            for gi in range(GI):
                i, bkx, tp = bcast(0, GI, gi * 128,
                                   scb[:, mat * 256:(mat + 1) * 256], 256)
                P.wait("vector", "pe", tp)
                tv = P.do("vector", lambda v, gi=gi, W=W, bkx=bkx: v.tensor_mul(
                    W[:, gi * 256:(gi + 1) * 256], W[:, gi * 256:(gi + 1) * 256],
                    bkx[:, 0:256]), sem="dv")
                bcast_done(i, tv)
            if mat == 0:
                env["_dqg"] = P.tok("dv")
            else:
                env["_dqu"] = P.tok("dv")
        for ct in range(2):
            gct = 2 * cb + ct
            W0 = qg_sb[:, cb % 2, :]
            W1 = qu_sb[:, cb % 2, :]
            _, _, t3g = wtok[(0, cb)]
            _, _, t3u = wtok[(1, cb)]
            P.wait("tensor", t3g)
            P.wait("tensor", t3u)
            P.wait("tensor", "dv", env["_dqg"])
            P.wait("tensor", "dv", env["_dqu"])
            for tb in range(2):
                bk = bank[tb]
                for gi in range(GI):
                    if gi == 0 and (0, tb) in acc_free:
                        P.wait("tensor", "ac", acc_free[(0, tb)])
                    P.do("tensor", lambda t, gi=gi, ct=ct, tb=tb, bk=bk, W0=W0: t.matmul(
                        bk[:], W0[:, gi * 256 + ct * 128: gi * 256 + ct * 128 + 128],
                        xnT[:, gi * TOK + tb * 512: gi * TOK + (tb + 1) * 512],
                        start=(gi == 0), stop=False), sem="pe")
                tzg = P.do("tensor", lambda t, ct=ct, tb=tb, bk=bk: t.matmul(
                    bk[:], zscb[0:GI, ct * 128: ct * 128 + 128],
                    xsT[:, tb * 512:(tb + 1) * 512],
                    start=False, stop=True), sem="pe")
                env["_lastg_pe"] = tzg
                P.wait("scalar", "pe", tzg)
                tsl = P.do("scalar", lambda s, tb=tb, bk=bk: s.activation(
                    out=gsil[:, tb * 512:(tb + 1) * 512], in_=bk[:],
                    func=AF.Silu), sem="ac")
                acc_free[(0, tb)] = tsl
            for tb in range(2):
                bk = bank[2 + tb]
                for gi in range(GI):
                    if gi == 0 and (1, tb) in acc_free:
                        P.wait("tensor", "dv", acc_free[(1, tb)])
                    P.do("tensor", lambda t, gi=gi, ct=ct, tb=tb, bk=bk, W1=W1: t.matmul(
                        bk[:], W1[:, gi * 256 + ct * 128: gi * 256 + ct * 128 + 128],
                        xnT[:, gi * TOK + tb * 512: gi * TOK + (tb + 1) * 512],
                        start=(gi == 0), stop=False), sem="pe")
                tzu = P.do("tensor", lambda t, ct=ct, tb=tb, bk=bk: t.matmul(
                    bk[:], zscb[0:GI, 256 + ct * 128: 256 + ct * 128 + 128],
                    xsT[:, tb * 512:(tb + 1) * 512],
                    start=False, stop=True), sem="pe")
                env["_lastu_pe"] = tzu
                P.wait("vector", "pe", tzu)
                P.wait("vector", "ac", acc_free[(0, tb)])
                if hct_free is not None:
                    P.wait("vector", hct_free)
                if hs_pe_tok is not None:
                    P.wait("vector", "pe", hs_pe_tok)
                th = P.do("vector", lambda v, tb=tb, bk=bk: v.tensor_mul(
                    hct[:, tb * 512:(tb + 1) * 512], bk[:],
                    gsil[:, tb * 512:(tb + 1) * 512]), sem="dv")
                acc_free[(1, tb)] = th
            th_all = P.tok("dv")
            hb = bank[6 + gct % 2]
            P.wait("tensor", "dv", th_all)
            if gct - 2 in hs_ev:
                P.wait("tensor", "dv", hs_ev[gct - 2])
            for tc in range(NTT):
                P.do("tensor", lambda t, tc=tc, hb=hb: t.matmul(
                    hb[:, tc:tc + 1], hct[:, tc * 128:(tc + 1) * 128],
                    onescol[:], start=True, stop=True), sem="pe")
            hs_pe_tok = P.tok("pe")
            P.wait("vector", "pe", hs_pe_tok)
            tev = P.do("vector", lambda v, gct=gct, hb=hb: v.tensor_copy(
                hs_sb[:, gct * NTT:(gct + 1) * NTT], hb[:, 0:NTT]), sem="dv")
            hs_ev[gct] = tev
            P.wait("scalar", "dv", th_all)
            hct_free = P.dma("scalar", ht_d[gct, :, :], hct[:, :], sem="htw")
            ht_wtok[gct] = hct_free
            ct2 = gct
            P.wait("vector", dtok[ct2])
            base = (ct2 // 32) * 32
            nrow = min(32, GO - base)
            last_dq = None
            for mb in range(8):
                i, bkx, tp = bcast(base, nrow, (ct2 - base) * 128,
                                   sdef_sb[base:base + nrow, mb * 512:(mb + 1) * 512],
                                   512)
                P.wait("vector", "pe", tp)
                tv = P.do("vector", lambda v, mb=mb, bkx=bkx: v.tensor_mul(
                    wdq[:, mb * 512:(mb + 1) * 512], wdq[:, mb * 512:(mb + 1) * 512],
                    bkx[:, 0:512]), sem="dv")
                bcast_done(i, tv)
                last_dq = tv
            P.wait("scalar", "dv", last_dq)
            wd_wtok[ct2] = P.dma("scalar", wd_d[ct2, :, :], wdq[:, :], sem="wdw")
            if ct2 + 1 < GO:
                load_dct(ct2 + 1)
        utok[(0, cb)] = env["_lastg_pe"]
        utok[(1, cb)] = env["_lastu_pe"]
        if cb + 1 < NCB:
            load_cb(0, cb + 1)
            load_cb(1, cb + 1)
    P.wait("tensor", "dv", P.tok("dv"))
    hst_dv = {}
    for tc in range(NTT):
        ps = bank[tc % 2]
        if tc >= 2:
            P.wait("tensor", "dv", hst_dv[tc - 2])
        tp = P.do("tensor", lambda t, tc=tc, ps=ps: t.matmul(
            ps[0:GO, 0:128], hs_sb.rearrange("p (ct tc) -> p tc ct", tc=NTT)[:, tc, :],
            ident[:], start=True, stop=True), sem="pe")
        P.wait("vector", "pe", tp)
        hst_dv[tc] = P.do("vector", lambda v, tc=tc, ps=ps: v.tensor_copy(
            hsT[:, tc * 128:(tc + 1) * 128], ps[0:GO, 0:128]),
            sem=("dv" if tc < NTT - 1 else "ln"))
    env["_gu_done"] = hst_dv[NTT - 1]


def emit_down(P, env):
    htin, wdt, xacc = env["htin"], env["wdt"], env["xacc"]
    zsdmb, hsT = env["zsdmb"], env["hsT"]
    bank = env["bank"]
    ht_d, wd_d, xbase_d, xout = env["ht_d"], env["wd_d"], env["xbase_d"], env["xout"]
    ht_wtok, wd_wtok = env["_ht_wtok"], env["_wd_wtok"]

    P.wait("tensor", "ln", env["_gu_done"])
    P.wait("vector", "ln", env["_gu_done"])
    P.wait("sync", "xbw", P.tok("xbw"))

    htok, wtok2 = {}, {}
    dn_use = {}
    env["_dn_use"] = dn_use
    qg_sb, qu_sb = env["qg_sb"], env["qu_sb"]
    chunks = []
    c0 = 0
    while c0 < GO:
        chunks.append((c0, min(4, GO - c0)))
        c0 += 4
    NCH = len(chunks)

    def ht4(s):
        return qg_sb[:, s, 0:4 * TOK].rearrange("p (b c) -> p b c", c=TOK)

    def wd4(s):
        return qu_sb[:, s, 0:4 * 512].rearrange("p (b c) -> p b c", c=512)

    P.wait("scalar", "pe", P.tok("pe"))
    P.wait("sync", "pe", P.tok("pe"))

    def load_chunk(k):
        mb, kc = divmod(k, NCH)
        ct0, nct = chunks[kc]
        s = k % 2
        if k >= 2:
            P.wait("scalar", "pe", dn_use[k - 2])
            P.wait("sync", "pe", dn_use[k - 2])
        P.wait("scalar", ht_wtok[ct0 + nct - 1])
        P.wait("sync", wd_wtok[ct0 + nct - 1])
        htok[k] = P.dma(
            "scalar", ht4(s)[:, 0:nct, :],
            ht_d[ct0:ct0 + nct, :, :].rearrange("ct p t -> p ct t"),
            sem="ht%d" % s)
        wtok2[k] = P.dma(
            "sync", wd4(s)[:, 0:nct, :],
            wd_d[ct0:ct0 + nct, :, mb * 512:(mb + 1) * 512]
            .rearrange("ct p t -> p ct t"),
            sem="wd%d" % s)

    load_chunk(0)
    load_chunk(1)
    zt_prev = None
    evac_toks = [None] * NTT
    xw_tok = {}
    for mb in range(8):
        if zt_prev is not None:
            P.wait("sync", "pe", zt_prev)
        tz = P.dma("sync", zsdmb[:, :], env["zsd_d"][:, mb * 512:(mb + 1) * 512],
                   sem="zsm")
        for kc in range(NCH):
            k = mb * NCH + kc
            ct0, nct = chunks[kc]
            s = k % 2
            P.wait("tensor", htok[k])
            P.wait("tensor", wtok2[k])
            for j in range(nct):
                for tt in range(NTT):
                    if ct0 + j == 0 and evac_toks[tt] is not None:
                        P.wait("tensor", "dv", evac_toks[tt])
                    P.do("tensor", lambda t, tt=tt, j=j, s=s: t.matmul(
                        bank[tt][:], ht4(s)[:, j, tt * 128:(tt + 1) * 128],
                        wd4(s)[:, j, :], start=(ct0 + j == 0), stop=False),
                        sem="pe")
            dn_use[k] = P.tok("pe")
            if k + 2 < 8 * NCH:
                load_chunk(k + 2)
        P.wait("tensor", tz)
        for tt in range(NTT):
            tzc = P.do("tensor", lambda t, tt=tt: t.matmul(
                bank[tt][:], hsT[:, tt * 128:(tt + 1) * 128], zsdmb[:, :],
                start=False, stop=True), sem="pe")
            if tt == NTT - 1:
                zt_prev = tzc
            if (tt % 2) in xw_tok:
                P.wait("sync", xw_tok[tt % 2])
            txa = P.dma("sync", xacc[:, tt % 2, :],
                        xbase_d[tt * 128:(tt + 1) * 128, mb * 512:(mb + 1) * 512],
                        sem="xa%d" % (tt % 2))
            P.wait("vector", "pe", tzc)
            P.wait("vector", txa)
            ta = P.do("vector", lambda v, tt=tt: v.tensor_add(
                xacc[:, tt % 2, :], xacc[:, tt % 2, :], bank[tt][:]), sem="dv")
            evac_toks[tt] = ta
            P.wait("sync", "dv", ta)
            xw_tok[tt % 2] = P.dma(
                "sync", xout[tt * 128:(tt + 1) * 128, mb * 512:(mb + 1) * 512],
                xacc[:, tt % 2, :], sem="xo%d" % (tt % 2))
    env["_down_done"] = P.tok("dv")


def emit_projections(P, env):
    qg_sb, qu_sb = env["qg_sb"], env["qu_sb"]
    xnT, pstg = env["xnT"], env["pstg"]
    bank = env["bank"]

    jobs = [(env["wq"], env["qt"], D, QSC),
            (env["wk"], env["kt"], NKV * HD, 1.0),
            (env["wv"], env["vt"], NKV * HD, 1.0)]

    P.wait("gpsimd", "dv", env["_down_done"])
    P.wait("gpsimd", "pe", P.tok("pe"))
    P.wait("tensor", "ln", env["_ln_done"])
    P.wait("tensor", "ac", P.tok("ac"))
    P.wait("scalar", "dv", env["_down_done"])

    seq = []
    for (w_in, out_d, ncols, scale) in jobs:
        w_v = w_in.rearrange("(g p) c -> p g c", p=128)
        for db in range(ncols // 256):
            seq.append((w_v, out_d, db, scale))

    ltoks, used, bufs = {}, {}, {}

    def load_j(j):
        w_v, out_d, db, scale = seq[j]
        buf = (qg_sb if (j % 4) < 2 else qu_sb)[:, j % 2, :]
        if (j - 4) in used:
            P.wait("gpsimd", "pe", used[j - 4])
        sl = slice(db * 256, (db + 1) * 256)
        ltoks[j] = P.dma("gpsimd", buf.rearrange("p (g c) -> p g c", c=256),
                         w_v[:, :, sl], sem="pj%d" % (j % 4))
        return buf

    for j in range(min(4, len(seq))):
        bufs[j] = load_j(j)
    ev_free = {}
    pj_w = {}
    for j in range(len(seq)):
        w_v, out_d, db, scale = seq[j]
        buf = bufs[j]
        P.wait("tensor", ltoks[j])
        for ct in range(2):
            for tb in range(2):
                bki = ct * 2 + tb
                bk = bank[bki]
                if bki in ev_free:
                    P.wait("tensor", "ac", ev_free[bki])
                for gi in range(GI):
                    P.do("tensor", lambda t, gi=gi, ct=ct, tb=tb, bk=bk, buf=buf: t.matmul(
                        bk[:], buf[:, gi * 256 + ct * 128: gi * 256 + ct * 128 + 128],
                        xnT[:, gi * TOK + tb * 512: gi * TOK + (tb + 1) * 512],
                        start=(gi == 0), stop=(gi == GI - 1)), sem="pe")
                tmm = P.tok("pe")
                P.wait("scalar", "pe", tmm)
                if (bki % 2) in pj_w:
                    P.wait("scalar", pj_w[bki % 2])
                tev = P.do("scalar", lambda s, bk=bk, bki=bki, scale=scale: s.activation(
                    out=pstg[:, bki % 2, :], in_=bk[:], func=AF.Copy, scale=scale),
                    sem="ac")
                ev_free[bki] = tev
                P.wait("scalar", "ac", tev)
                r0 = db * 256 + ct * 128
                pj_w[bki % 2] = P.dma(
                    "scalar", out_d[r0:r0 + 128, tb * 512:(tb + 1) * 512],
                    pstg[:, bki % 2, :], sem="pw%d" % (bki % 2))
        used[j] = P.tok("pe")
        if j + 4 < len(seq):
            bufs[j + 4] = load_j(j + 4)


def emit_oproj(P, env):
    qg_sb, qu_sb = env["qg_sb"], env["qu_sb"]
    xnT, xacc = env["xnT"], env["xacc"]
    bank = env["bank"]
    attnT, wo, xin, xbase_d = env["attnT"], env["wo"], env["xin"], env["xbase_d"]

    at_v = attnT.rearrange("(g p) c -> p g c", p=128)
    t_at = P.dma("gpsimd", xnT.rearrange("p (g c) -> p g c", c=TOK), at_v, sem="ati")
    wo_v = wo.rearrange("(g p) c -> p g c", p=128)

    wtoks, wuse, bufs = {}, {}, {}

    def load_w(mbq):
        buf = (qg_sb if (mbq % 4) < 2 else qu_sb)[:, mbq % 2, :]
        if (mbq - 4) in wuse:
            P.wait("gpsimd", "pe", wuse[mbq - 4])
        sl = slice(mbq * 256, (mbq + 1) * 256)
        wtoks[mbq] = P.dma("gpsimd", buf.rearrange("p (g c) -> p g c", c=256),
                           wo_v[:, :, sl], sem="pj%d" % (mbq % 4))
        return buf

    for j in range(4):
        bufs[j] = load_w(j)
    P.wait("tensor", t_at)
    evd = {}
    xw = {}
    for mbq in range(D // 256):
        buf = bufs[mbq]
        P.wait("tensor", wtoks[mbq])
        for gi in range(GI):
            for tt in range(NTT):
                if gi == 0 and tt in evd:
                    P.wait("tensor", "dv", evd[tt])
                P.do("tensor", lambda t, gi=gi, tt=tt, buf=buf: t.matmul(
                    bank[tt][:, 0:256],
                    xnT[:, gi * TOK + tt * 128: gi * TOK + (tt + 1) * 128],
                    buf[:, gi * 256:(gi + 1) * 256],
                    start=(gi == 0), stop=(gi == GI - 1)), sem="pe")
        tmm = P.tok("pe")
        for tt in range(NTT):
            if (tt % 2) in xw:
                P.wait("sync", xw[tt % 2])
            txa = P.dma("sync", xacc[:, tt % 2, 0:256],
                        xin[tt * 128:(tt + 1) * 128, mbq * 256:(mbq + 1) * 256],
                        sem="xa%d" % (tt % 2))
            P.wait("vector", "pe", tmm)
            P.wait("vector", txa)
            ta = P.do("vector", lambda v, tt=tt: v.tensor_add(
                xacc[:, tt % 2, 0:256], xacc[:, tt % 2, 0:256],
                bank[tt][:, 0:256]), sem="dv")
            evd[tt] = ta
            P.wait("sync", "dv", ta)
            xw[tt % 2] = P.dma(
                "sync", xbase_d[tt * 128:(tt + 1) * 128, mbq * 256:(mbq + 1) * 256],
                xacc[:, tt % 2, 0:256], sem="xo%d" % (tt % 2))
        wuse[mbq] = P.tok("pe")
        if mbq + 4 < D // 256:
            bufs[mbq + 4] = load_w(mbq + 4)
    env["_op_done"] = P.do("vector", lambda v: v.nop(), sem="ln")



def build_attention():
    nc = bass.Bass("TRN2", target_bir_lowering=False, debug=False)
    ap = lambda t: t.ap()
    qtp = ap(nc.dram_tensor("qtp", [2, 4 * HD, S], BF16, kind="ExternalInput"))
    ktp = ap(nc.dram_tensor("ktp", [2, HD, S], BF16, kind="ExternalInput"))
    vtp = ap(nc.dram_tensor("vtp", [2, HD, S], BF16, kind="ExternalInput"))
    ao = ap(nc.dram_tensor("ao", [2 * 4 * HD, S], BF16, kind="ExternalOutput"))

    import contextlib
    with contextlib.ExitStack() as _es:
        _e = _es.enter_context
        qts = _e(nc.sbuf_tensor("qts", [128, 4 * S], BF16))
        kts = _e(nc.sbuf_tensor("kts", [128, S], BF16))
        vtq = _e(nc.sbuf_tensor("vtq", [128, S], BF16))
        vts = _e(nc.sbuf_tensor("vts", [128, S], BF16))
        et = _e(nc.sbuf_tensor("et", [128, 2, 512], BF16))
        eacc = _e(nc.sbuf_tensor("eacc", [128, 512], F32))
        eaccb = _e(nc.sbuf_tensor("eaccb", [128, 512], BF16))
        lrec = _e(nc.sbuf_tensor("lrec", [128, 512], F32))
        masks = _e(nc.sbuf_tensor("masks", [128, 4 * 512], BF16))
        ones128 = _e(nc.sbuf_tensor("ones128", [128, 128], BF16))
        ident = _e(nc.sbuf_tensor("ident", [128, 128], BF16))
        astg = _e(nc.sbuf_tensor("astg", [128, 2, 512], BF16))
        expb = _e(nc.sbuf_tensor("expb", [128, 1], F32))
        bk0 = _e(nc.psum_tensor("bk0", [128, 512], F32))
        bk1 = _e(nc.psum_tensor("bk1", [128, 512], F32))
        bk2 = _e(nc.psum_tensor("bk2", [128, 512], F32))
        bk3 = _e(nc.psum_tensor("bk3", [128, 512], F32))
        bk4 = _e(nc.psum_tensor("bk4", [128, 512], F32))
        bk5 = _e(nc.psum_tensor("bk5", [128, 512], F32))
        semnames = ["pe", "dv", "ac", "use", "gp", "qld", "kld", "vld",
                    "ao0", "ao1"]
        handles = {n: _e(nc.semaphore(n)) for n in semnames}
        block = _e(nc.Block())
        bank = [bk0, bk1, bk2, bk3, bk4, bk5]

        def body(P):
            P.gp(lambda g: g.memset(ones128[:], 1.0), sem="use")
            P.gp(lambda g: g.memset(expb[:], EXPB), sem="use")
            P.gp(lambda g: g.memset(ident[:], 1.0), sem="use")
            P.gp(lambda g: g.affine_select(
                ident[:], ident[:], pattern=[[1, 128]], compare_op=ALU.is_equal,
                fill=0.0, channel_multiplier=-1), sem="use")
            P.gp(lambda g: g.memset(masks[:], 1.0), sem="use")
            tmask = None
            for j in range(4):
                tmask = P.gp(lambda g, j=j: g.affine_select(
                    masks[:, j * 512:(j + 1) * 512], masks[:, j * 512:(j + 1) * 512],
                    pattern=[[1, 512]], compare_op=ALU.is_ge, fill=0.0,
                    base=-128 * j, channel_multiplier=-1), sem="use")
            ao_last = {}
            for pp in range(2):
                if pp == 1:
                    P.wait("sync", "ao0", P.tok("ao0"))
                    P.wait("sync", "ao1", P.tok("ao1"))
                    P.wait("sync", "ac", P.tok("ac"))
                    P.wait("sync", "pe", P.tok("pe"))
                tq = P.dma("sync", qts.rearrange("p (h c) -> p h c", c=S),
                           qtp[pp].rearrange("(h p) c -> p h c", p=128), sem="qld")
                tk = P.dma("sync", kts[:, :], ktp[pp, :, :], sem="kld")
                tv = P.dma("sync", vtq[:, :], vtp[pp, :, :], sem="vld")
                P.wait("tensor", tv)
                if pp == 0:
                    P.wait("tensor", "use", tmask)
                vt_ev = []
                for kt in range(32):
                    ps = bank[4 + kt % 2]
                    if len(vt_ev) >= 2:
                        P.wait("tensor", "dv", vt_ev[-2])
                    tp = P.do("tensor", lambda t, kt=kt, ps=ps: t.matmul(
                        ps[:, 0:128], vtq[:, kt * 128:(kt + 1) * 128], ident[:],
                        start=True, stop=True), sem="pe")
                    P.wait("vector", "pe", tp)
                    te = P.do("vector", lambda v, kt=kt, ps=ps: v.tensor_copy(
                        vts[:, kt * 128:(kt + 1) * 128], ps[:, 0:128]), sem="dv")
                    vt_ev.append(te)
                P.wait("tensor", tq)
                P.wait("tensor", tk)
                P.wait("tensor", "dv", vt_ev[-1])
                if pp == 0:
                    P.wait("scalar", "use", tmask)
                    P.wait("vector", "use", tmask)
                sc_free = [None, None]
                et_use = [None, None]
                av_free = None
                l_free = None
                lm_tok = None
                for hh in range(4):
                    for qb in range(8):
                        nkt = 4 * (qb + 1)
                        av_tok = None
                        prev_tla = None
                        for kt in range(nkt):
                            sb = bank[kt % 2]
                            if sc_free[kt % 2] is not None:
                                P.wait("tensor", "ac", sc_free[kt % 2])
                            tmm = P.do("tensor", lambda t, kt=kt, hh=hh, qb=qb, sb=sb: t.matmul(
                                sb[:], kts[:, kt * 128:(kt + 1) * 128],
                                qts[:, hh * S + qb * 512: hh * S + (qb + 1) * 512],
                                start=True, stop=True), sem="pe")
                            P.wait("scalar", "pe", tmm)
                            if et_use[kt % 2] is not None:
                                pe_t, dv_t = et_use[kt % 2]
                                P.wait("scalar", "pe", pe_t)
                                P.wait("scalar", "dv", dv_t)
                            tex = P.do("scalar", lambda s, kt=kt, sb=sb: s.activation(
                                out=et[:, kt % 2, :], in_=sb[:], func=AF.Exp,
                                bias=expb[:, 0:1]), sem="ac")
                            sc_free[kt % 2] = tex
                            dj = kt - 4 * qb
                            P.wait("vector", "ac", tex)
                            if dj >= 0:
                                tmk = P.do("vector", lambda v, kt=kt, dj=dj: v.tensor_mul(
                                    et[:, kt % 2, :], et[:, kt % 2, :],
                                    masks[:, dj * 512:(dj + 1) * 512]), sem="dv")
                            else:
                                tmk = None
                            if prev_tla is not None:
                                P.wait("vector", "dv", prev_tla)
                            if kt == 0:
                                tla = P.do("vector", lambda v, kt=kt: v.tensor_copy(
                                    eacc[:, :], et[:, kt % 2, :]), sem="dv")
                            else:
                                tla = P.do("vector", lambda v, kt=kt: v.tensor_add(
                                    eacc[:, :], eacc[:, :], et[:, kt % 2, :]), sem="dv")
                            prev_tla = tla
                            if tmk is not None:
                                P.wait("tensor", "dv", tmk)
                            else:
                                P.wait("tensor", "ac", tex)
                            if kt == 0 and av_free is not None:
                                P.wait("tensor", "dv", av_free)
                            tav = P.do("tensor", lambda t, kt=kt, nkt=nkt: t.matmul(
                                bank[2][:], vts[:, kt * 128:(kt + 1) * 128],
                                et[:, kt % 2, :], start=(kt == 0),
                                stop=(kt == nkt - 1)), sem="pe")
                            et_use[kt % 2] = (tav, tla)
                            av_tok = tav
                        if lm_tok is not None:
                            P.wait("vector", "pe", lm_tok)
                        if prev_tla is not None:
                            P.wait("vector", "dv", prev_tla)
                        tcb = P.do("vector", lambda v: v.tensor_copy(
                            eaccb[:, :], eacc[:, :]), sem="dv")
                        P.wait("tensor", "dv", tcb)
                        if l_free is not None:
                            P.wait("tensor", "dv", l_free)
                        tlm = P.do("tensor", lambda t: t.matmul(
                            bank[3][:], ones128[:], eaccb[:, :],
                            start=True, stop=True), sem="pe")
                        lm_tok = tlm
                        P.wait("vector", "pe", tlm)
                        trc = P.do("vector", lambda v: v.reciprocal(
                            lrec[:, :], bank[3][:]), sem="dv")
                        l_free = trc
                        P.wait("vector", "pe", av_tok)
                        P.wait("vector", "dv", trc)
                        if ((hh * 8 + qb) % 2) in ao_last:
                            P.wait("vector", ao_last[(hh * 8 + qb) % 2])
                        tnm = P.do("vector", lambda v, hh=hh, qb=qb: v.tensor_mul(
                            astg[:, (hh * 8 + qb) % 2, :], bank[2][:], lrec[:, :]),
                            sem="dv")
                        av_free = tnm
                        P.wait("sync", "dv", tnm)
                        ao_last[(hh * 8 + qb) % 2] = P.dma(
                            "sync",
                            ao[pp * 512 + hh * 128: pp * 512 + (hh + 1) * 128,
                               qb * 512:(qb + 1) * 512],
                            astg[:, (hh * 8 + qb) % 2, :],
                            sem="ao%d" % ((hh * 8 + qb) % 2))
            P.wait("sync", "ao0")
            P.wait("sync", "ao1")

        run_spmd_program(nc, block, handles, body)
    return nc



_NC_A = None
_NC_B = None
_NC_C = None


def _get_programs():
    global _NC_A, _NC_B, _NC_C
    if _NC_A is None:
        _NC_A = build_tokenside(with_proj=True, with_oproj=False)
        _NC_B = build_attention()
        _NC_C = build_tokenside(with_proj=False, with_oproj=True)
    return _NC_A, _NC_B, _NC_C


def kernel(L0_attn_hidden_states, L0_attn_w_q, L0_attn_w_k, L0_attn_w_v,
           L0_attn_w_o, L0_ffn_qw_gate, L0_ffn_z_gate, L0_ffn_s_gate,
           L0_ffn_qw_up, L0_ffn_z_up, L0_ffn_s_up, L0_ffn_qw_down,
           L0_ffn_z_down, L0_ffn_s_down):
    nc_a, nc_b, nc_c = _get_programs()

    x0 = np.ascontiguousarray(np.asarray(L0_attn_hidden_states, dtype=np.float32)
                              .reshape(B * S, D))
    qwg = np.ascontiguousarray(np.asarray(L0_ffn_qw_gate).astype(np.int8))
    qwu = np.ascontiguousarray(np.asarray(L0_ffn_qw_up).astype(np.int8))
    qwd = np.ascontiguousarray(np.asarray(L0_ffn_qw_down).astype(np.int8))
    zg = np.ascontiguousarray(np.asarray(L0_ffn_z_gate).astype(np.int8))
    sg = np.ascontiguousarray(np.asarray(L0_ffn_s_gate, dtype=np.float32))
    zu = np.ascontiguousarray(np.asarray(L0_ffn_z_up).astype(np.int8))
    su = np.ascontiguousarray(np.asarray(L0_ffn_s_up, dtype=np.float32))
    zd = np.ascontiguousarray(np.asarray(L0_ffn_z_down).astype(np.int8))
    sd = np.ascontiguousarray(np.asarray(L0_ffn_s_down, dtype=np.float32))
    wq = np.ascontiguousarray(np.asarray(L0_attn_w_q, dtype=np.float32))
    wk = np.ascontiguousarray(np.asarray(L0_attn_w_k, dtype=np.float32))
    wv = np.ascontiguousarray(np.asarray(L0_attn_w_v, dtype=np.float32))
    wo = np.ascontiguousarray(np.asarray(L0_attn_w_o, dtype=np.float32))

    common = dict(qwg=qwg, qwu=qwu, qwd=qwd, zg=zg, sg=sg, zu=zu, su=su,
                  zd=zd, sd=sd)

    in_a = []
    for c in range(NC):
        m = dict(common)
        m.update(dict(xin=x0[c * TOK:(c + 1) * TOK], wq=wq, wk=wk, wv=wv))
        in_a.append(m)
    res_a = run_bass_kernel_spmd(nc_a, in_a, core_ids=list(range(NC))).results
    x1 = [res_a[c]["xout"] for c in range(NC)]
    qts = [res_a[c]["qt"] for c in range(NC)]
    kts = [res_a[c]["kt"] for c in range(NC)]
    vts = [res_a[c]["vt"] for c in range(NC)]

    in_b = []
    for c in range(NC):
        b = (2 * c) // 8
        owners = [4 * b + o for o in range(4)]
        qtp = np.empty((2, 4 * HD, S), dtype=BF16NP)
        ktp = np.empty((2, HD, S), dtype=BF16NP)
        vtp = np.empty((2, HD, S), dtype=BF16NP)
        for k in range(2):
            g = (2 * c + k) % 8
            qtp[k] = np.concatenate([qts[o][g * 512:(g + 1) * 512, :] for o in owners], axis=1)
            ktp[k] = np.concatenate([kts[o][g * 128:(g + 1) * 128, :] for o in owners], axis=1)
            vtp[k] = np.concatenate([vts[o][g * 128:(g + 1) * 128, :] for o in owners], axis=1)
        in_b.append(dict(qtp=qtp, ktp=ktp, vtp=vtp))
    res_b = run_bass_kernel_spmd(nc_b, in_b, core_ids=list(range(NC))).results
    aos = [res_b[c]["ao"] for c in range(NC)]

    attnT_full = []
    for b in range(B):
        blocks = []
        for g in range(8):
            oc = 4 * b + g // 2
            slot = g % 2
            blocks.append(aos[oc][slot * 512:(slot + 1) * 512, :])
        attnT_full.append(np.concatenate(blocks, axis=0))
    in_c = []
    for c in range(NC):
        b = c // 4
        cs = (c % 4) * TOK
        m = dict(common)
        m.update(dict(
            xin=np.ascontiguousarray(x1[c]),
            attnT=np.ascontiguousarray(attnT_full[b][:, cs:cs + TOK]),
            wo=wo,
        ))
        in_c.append(m)
    res_c = run_bass_kernel_spmd(nc_c, in_c, core_ids=list(range(NC))).results
    out = np.concatenate([res_c[c]["xout"] for c in range(NC)], axis=0)
    return out.reshape(B, S, D).astype(np.float32)


# version 03
# speedup vs baseline: 1.1835x; 1.0495x over previous
import numpy as np
import ml_dtypes
import concourse.bass as bass
import concourse.mybir as mybir
from concourse.bass_utils import run_bass_kernel_spmd

F32 = mybir.dt.float32
BF16 = mybir.dt.bfloat16
I32 = mybir.dt.int32
I8 = mybir.dt.int8
AF = mybir.ActivationFunctionType
ALU = mybir.AluOpType
AX = mybir.AxisListType
BF16NP = ml_dtypes.bfloat16

D = 4096
HID = 11008
S = 4096
B = 2
NC = 8
TOK = (B * S) // NC
GI = D // 128
GO = HID // 128
NH = 32
NKV = 8
HD = 128
EPS = 1e-5
EXPB = -20.0
NTT = TOK // 128
NCB = HID // 256
QSC = float(HD) ** -0.5


class Prog:

    def __init__(self, me, eng, handles):
        self.me = me
        self.eng = eng
        self.h = handles
        self.v = {k: 0 for k in handles}

    def do(self, engname, fn, sem=None, n=1):
        tok = None
        if sem is not None:
            self.v[sem] += n
            tok = self.v[sem]
        if engname == self.me:
            ins = fn(self.eng)
            if sem is not None:
                ins.then_inc(self.h[sem], n)
        return tok

    def dma(self, engname, out, in_, sem, **kw):
        t = self.do(engname, lambda e: e.dma_start(out=out, in_=in_, **kw),
                    sem=sem, n=16)
        return (sem, t)

    def wait(self, engname, sem, tok=None):
        if isinstance(sem, tuple):
            sem, tok = sem
        elif isinstance(tok, tuple):
            sem, tok = tok
        if engname == self.me:
            self.eng.wait_ge(self.h[sem], self.v[sem] if tok is None else tok)

    def tok(self, sem):
        return self.v[sem]

    def gp(self, fn, sem="gp"):
        t = self.do("gpsimd", fn, sem=sem)
        self.wait("gpsimd", sem, t)
        return t


def run_spmd_program(nc, block, handles, body):
    def mk(engname):
        def walk(eng):
            body(Prog(engname, eng, handles))
        return walk
    block.sync(mk("sync"))
    block.gpsimd(mk("gpsimd"))
    block.tensor(mk("tensor"))
    block.vector(mk("vector"))
    block.scalar(mk("scalar"))



def build_tokenside(with_proj, with_oproj):
    nc = bass.Bass("TRN2", target_bir_lowering=False, debug=False)
    ap = lambda t: t.ap()

    E = {}
    E["xin"] = ap(nc.dram_tensor("xin", [TOK, D], F32, kind="ExternalInput"))
    E["qwg"] = ap(nc.dram_tensor("qwg", [D, HID], I8, kind="ExternalInput"))
    E["qwu"] = ap(nc.dram_tensor("qwu", [D, HID], I8, kind="ExternalInput"))
    E["qwd"] = ap(nc.dram_tensor("qwd", [HID, D], I8, kind="ExternalInput"))
    E["zg"] = ap(nc.dram_tensor("zg", [GI, HID], I8, kind="ExternalInput"))
    E["sg"] = ap(nc.dram_tensor("sg", [GI, HID], F32, kind="ExternalInput"))
    E["zu"] = ap(nc.dram_tensor("zu", [GI, HID], I8, kind="ExternalInput"))
    E["su"] = ap(nc.dram_tensor("su", [GI, HID], F32, kind="ExternalInput"))
    E["zd"] = ap(nc.dram_tensor("zd", [GO, D], I8, kind="ExternalInput"))
    E["sd"] = ap(nc.dram_tensor("sd", [GO, D], F32, kind="ExternalInput"))
    if with_proj:
        E["wq"] = ap(nc.dram_tensor("wq", [D, D], F32, kind="ExternalInput"))
        E["wk"] = ap(nc.dram_tensor("wk", [D, NKV * HD], F32, kind="ExternalInput"))
        E["wv"] = ap(nc.dram_tensor("wv", [D, NKV * HD], F32, kind="ExternalInput"))
        E["qt"] = ap(nc.dram_tensor("qt", [D, TOK], BF16, kind="ExternalOutput"))
        E["kt"] = ap(nc.dram_tensor("kt", [NKV * HD, TOK], BF16, kind="ExternalOutput"))
        E["vt"] = ap(nc.dram_tensor("vt", [NKV * HD, TOK], BF16, kind="ExternalOutput"))
    if with_oproj:
        E["attnT"] = ap(nc.dram_tensor("attnT", [D, TOK], BF16, kind="ExternalInput"))
        E["wo"] = ap(nc.dram_tensor("wo", [D, D], F32, kind="ExternalInput"))
    E["xout"] = ap(nc.dram_tensor("xout", [TOK, D], F32, kind="ExternalOutput"))
    E["xbase_d"] = ap(nc.dram_tensor("xbase_d", [TOK, D], F32))
    E["zsg_d"] = ap(nc.dram_tensor("zsg_d", [GI, HID], BF16))
    E["zsu_d"] = ap(nc.dram_tensor("zsu_d", [GI, HID], BF16))
    E["zsd_d"] = ap(nc.dram_tensor("zsd_d", [GO, D], BF16))
    E["sdeff_d"] = ap(nc.dram_tensor("sdeff_d", [GO, D], BF16))
    E["ht_d"] = ap(nc.dram_tensor("ht_d", [GO, 128, TOK], BF16))
    E["wd_d"] = ap(nc.dram_tensor("wd_d", [GO, 128, D], BF16))

    import contextlib
    with contextlib.ExitStack() as _es:
        _e = _es.enter_context
        xnT = _e(nc.sbuf_tensor("xnT", [128, GI * TOK], BF16))
        qg_sb = _e(nc.sbuf_tensor("qg", [128, 2, GI * 256], BF16))
        qu_sb = _e(nc.sbuf_tensor("qu", [128, 2, GI * 256], BF16))
        wdq = _e(nc.sbuf_tensor("wdq", [128, D], BF16))
        scb = _e(nc.sbuf_tensor("scb", [GI, 2 * 256], BF16))
        zscb = _e(nc.sbuf_tensor("zscb", [GI, 2 * 256], BF16))
        sdef_sb = _e(nc.sbuf_tensor("sdef", [GO, D], BF16))
        zsdmb = _e(nc.sbuf_tensor("zsdmb", [GO, 512], BF16))
        xsT = _e(nc.sbuf_tensor("xsT", [GI, TOK], BF16))
        hs_sb = _e(nc.sbuf_tensor("hs", [128, NTT * GO], BF16))
        hsT = _e(nc.sbuf_tensor("hsT", [GO, TOK], BF16))
        hct = _e(nc.sbuf_tensor("hct", [128, TOK], BF16))
        htin = _e(nc.sbuf_tensor("htin", [128, 2, TOK], BF16))
        wdt = _e(nc.sbuf_tensor("wdt", [128, 2, 512], BF16))
        ones1 = _e(nc.sbuf_tensor("ones1", [1, 128], BF16))
        onescol = _e(nc.sbuf_tensor("onescol", [128, 1], BF16))
        ident = _e(nc.sbuf_tensor("ident", [128, 128], BF16))
        indg = _e(nc.sbuf_tensor("indg", [128, GI * 128], BF16))
        rowsel = _e(nc.sbuf_tensor("rowsel", [128, GI], F32))
        indone = _e(nc.sbuf_tensor("indone", [128, 128], BF16))
        xld = _e(nc.sbuf_tensor("xld", [128, D], F32))
        xn_sb = _e(nc.sbuf_tensor("xn", [128, D], BF16))
        xs_sb = _e(nc.sbuf_tensor("xs", [128, NTT * GI], BF16))
        xsf = _e(nc.sbuf_tensor("xsf", [128, GI], F32))
        lnst = _e(nc.sbuf_tensor("lnst", [128, 8], F32))
        gsil = _e(nc.sbuf_tensor("gsil", [128, TOK], BF16))
        xacc = _e(nc.sbuf_tensor("xacc", [128, 2, 512], F32))
        pstg = _e(nc.sbuf_tensor("pstg", [128, 2, 512], BF16))
        bk0 = _e(nc.psum_tensor("bk0", [128, 512], F32))
        bk1 = _e(nc.psum_tensor("bk1", [128, 512], F32))
        bk2 = _e(nc.psum_tensor("bk2", [128, 512], F32))
        bk3 = _e(nc.psum_tensor("bk3", [128, 512], F32))
        bk4 = _e(nc.psum_tensor("bk4", [128, 512], F32))
        bk5 = _e(nc.psum_tensor("bk5", [128, 512], F32))
        bk6 = _e(nc.psum_tensor("bk6", [128, 512], F32))
        bk7 = _e(nc.psum_tensor("bk7", [128, 512], F32))
        semnames = ["pe", "dv", "ac", "gp", "pre", "ln", "pz", "ps", "pz0",
                    "pz1", "ps0", "ps1", "pwa", "pwb", "pw", "xli",
                    "xbw", "qg0", "qg1", "qu0", "qu1", "scg", "scu", "zsgq",
                    "zsuq", "wdqi", "wdw", "htw", "ht0", "ht1", "ht2", "ht3",
                    "wd0", "wd1", "wd2", "wd3",
                    "zsm", "xa0", "xa1", "xo0", "xo1", "pj0", "pj1", "pj2",
                    "pj3", "pw0", "pw1", "ati"]
        handles = {n: _e(nc.semaphore(n)) for n in semnames}
        block = _e(nc.Block())
        env = dict(E)
        env.update(dict(
            xnT=xnT, qg_sb=qg_sb, qu_sb=qu_sb, wdq=wdq, scb=scb, zscb=zscb,
            sdef_sb=sdef_sb, zsdmb=zsdmb, xsT=xsT, hs_sb=hs_sb, hsT=hsT,
            hct=hct, htin=htin, wdt=wdt, ones1=ones1, onescol=onescol,
            ident=ident, indg=indg, xld=xld, xn_sb=xn_sb, xs_sb=xs_sb,
            lnst=lnst, gsil=gsil, xacc=xacc, pstg=pstg, rowsel=rowsel,
            indone=indone, xsf=xsf,
            bank=[bk0, bk1, bk2, bk3, bk4, bk5, bk6, bk7],
        ))

        def body(P):
            for k in [k for k in env if k.startswith("_")]:
                del env[k]
            emit_prep(P, env)
            if with_oproj:
                emit_oproj(P, env)
                base = env["xbase_d"]
            else:
                base = env["xin"]
            emit_ln(P, env, base, write_xbase=(not with_oproj))
            emit_gateup(P, env)
            emit_down(P, env)
            if with_proj:
                emit_ln(P, env, env["xout"], write_xbase=False)
                emit_projections(P, env)
            for s_ in ("xo0", "xo1", "xbw", "pw0", "pw1", "wdw", "htw", "pw"):
                P.wait("sync", s_)

        run_spmd_program(nc, block, handles, body)
    return nc


def emit_prep(P, env):
    ones1, onescol, ident, indg = env["ones1"], env["onescol"], env["ident"], env["indg"]
    xld = env["xld"]
    CH = HID // 16

    P.gp(lambda g: g.memset(ones1[:], 1.0))
    P.gp(lambda g: g.memset(onescol[:], 1.0))
    P.gp(lambda g: g.memset(ident[:], 1.0))
    P.gp(lambda g: g.affine_select(
        ident[:], ident[:], pattern=[[1, 128]], compare_op=ALU.is_equal,
        fill=0.0, channel_multiplier=-1))
    rowsel, indone = env["rowsel"], env["indone"]
    P.gp(lambda g: g.memset(rowsel[:], 0.0))
    P.gp(lambda g: g.memset(indone[:], 1.0))
    for base_r in (0, 32, 64):
        P.gp(lambda g, base_r=base_r: g.affine_select(
            rowsel[:], rowsel[:], pattern=[[-1, GI]], compare_op=ALU.not_equal,
            fill=1.0, base=-base_r, channel_multiplier=1))
    tsel = P.do("gpsimd", lambda g: g.nop(), sem="gp")
    P.wait("vector", "gp", tsel)
    for gi in range(GI):
        P.do("vector", lambda v, gi=gi: v.tensor_scalar(
            indg[:, gi * 128:(gi + 1) * 128], indone[:],
            rowsel[:, gi:gi + 1], None, ALU.mult), sem="pre")

    items = []
    for z_, s_, out_ in ((env["zg"], env["sg"], env["zsg_d"]),
                         (env["zu"], env["su"], env["zsu_d"])):
        for c in range(16):
            items.append((z_, s_, out_, slice(c * CH, (c + 1) * CH)))
    ztk, stk, dvt, wtk = {}, {}, {}, {}

    def prow(i):
        return slice(32 * (i % 2), 32 * (i % 2) + GI)

    def pload(i):
        z_, s_, out_, sl = items[i]
        r = prow(i)
        if i >= 2:
            P.wait("gpsimd", "dv", dvt[i - 2])
            P.wait("sync", "dv", dvt[i - 2])
            P.wait("gpsimd", wtk[i - 2])
            P.wait("sync", wtk[i - 2])
        ztk[i] = P.dma("gpsimd", xld[r, 0:CH], z_[:, sl], sem="pz%d" % (i % 2))
        stk[i] = P.dma("sync", xld[r, CH:2 * CH], s_[:, sl], sem="ps%d" % (i % 2))

    pload(0)
    pload(1)
    prev = None
    for i, (z_, s_, out_, sl) in enumerate(items):
        r = prow(i)
        P.wait("vector", ztk[i])
        P.wait("vector", stk[i])
        tv = P.do("vector", lambda v, r=r: v.scalar_tensor_tensor(
            xld[r, 2 * CH:3 * CH], xld[r, 0:CH], -1.0,
            xld[r, CH:2 * CH], ALU.mult, ALU.mult), sem="dv")
        dvt[i] = tv
        P.wait("gpsimd", "dv", tv)
        prev = P.dma("gpsimd", out_[:, sl], xld[r, 2 * CH:3 * CH],
                     sem="pw%d" % (i % 2) if False else "pwa" if i % 2 == 0 else "pwb")
        wtk[i] = prev
        if i + 2 < len(items):
            pload(i + 2)
    P.wait("gpsimd", wtk[len(items) - 2])
    P.wait("sync", wtk[len(items) - 2])
    P.wait("gpsimd", wtk[len(items) - 1])
    P.wait("sync", wtk[len(items) - 1])
    P.wait("sync", "dv", dvt[len(items) - 1])
    P.wait("gpsimd", "dv", dvt[len(items) - 1])
    CH2 = D // 2
    for c in range(2):
        sl = slice(c * CH2, (c + 1) * CH2)
        P.wait("gpsimd", prev)
        P.wait("sync", prev)
        tz = P.dma("gpsimd", xld[0:86, 0:CH2], env["zd"][:, sl], sem="pz")
        ts = P.dma("sync", xld[0:86, CH2:2 * CH2], env["sd"][:, sl], sem="ps")
        P.wait("vector", tz)
        P.wait("vector", ts)
        t1 = P.do("vector", lambda v: v.tensor_scalar_mul(
            xld[0:86, CH2:2 * CH2], xld[0:86, CH2:2 * CH2], 0.5), sem="dv")
        P.wait("vector", "dv", t1)
        t2 = P.do("vector", lambda v: v.scalar_tensor_tensor(
            xld[0:86, 0:CH2], xld[0:86, 0:CH2], -1.0,
            xld[0:86, CH2:2 * CH2], ALU.mult, ALU.mult), sem="dv")
        P.wait("gpsimd", "dv", t2)
        P.dma("gpsimd", env["zsd_d"][:, sl], xld[0:86, 0:CH2], sem="pw")
        prev = P.dma("gpsimd", env["sdeff_d"][:, sl], xld[0:86, CH2:2 * CH2], sem="pw")
    P.wait("sync", prev)
    env["_prep_tok"] = P.dma("sync", env["sdef_sb"][:, :], env["sdeff_d"][:, :], sem="ps")


def emit_ln(P, env, src_d, write_xbase):
    xld, xn, lnst = env["xld"], env["xn_sb"], env["lnst"]
    xs_sb, xsT, xnT = env["xs_sb"], env["xsT"], env["xnT"]
    xsf = env["xsf"]
    ident, bank = env["ident"], env["bank"]
    s1, s2, m, var, sd_, rstd, bias = (lnst[:, i:i + 1] for i in range(7))

    P.wait("sync", "xo0", P.tok("xo0"))
    P.wait("sync", "xo1", P.tok("xo1"))
    P.wait("sync", "ac", P.tok("ac"))
    P.wait("tensor", "dv", P.tok("dv"))
    P.wait("tensor", "ac", P.tok("ac"))

    tr_ev = []
    for tt in range(NTT):
        sl = slice(tt * 128, (tt + 1) * 128)
        if tt >= 1:
            P.wait("sync", "ac", env["_xld_free"])
        tld = P.dma("sync", xld[:, :], src_d[sl, :], sem="xli")
        if write_xbase:
            P.dma("sync", env["xbase_d"][sl, :], src_d[sl, :], sem="xbw")
        P.wait("scalar", tld)
        if tr_ev:
            P.wait("scalar", "pe", env["_ln_trmm"])
            P.wait("scalar", "dv", env["_xsf_cp"])
        ts1 = P.do("scalar", lambda s: s.activation(
            out=xn[:, :], in_=xld[:, :], func=AF.Copy, accum_out=s1), sem="ac")
        P.wait("scalar", "ac", ts1)
        ts2 = P.do("scalar", lambda s: s.activation(
            out=xn[:, :], in_=xld[:, :], func=AF.Square, accum_out=s2), sem="ac")
        P.wait("vector", "ac", ts2)
        P.do("vector", lambda v: v.tensor_scalar(m, s1, 1.0 / D, None, ALU.mult), sem="dv")
        t_mv = P.do("vector", lambda v: v.tensor_scalar(var, s2, 1.0 / D, EPS, ALU.mult, ALU.add), sem="dv")
        P.wait("vector", "dv", t_mv)
        tm = P.do("vector", lambda v: v.scalar_tensor_tensor(
            s1, s1, -1.0 / (float(D) * float(D)), s1, ALU.mult, ALU.mult), sem="dv")
        P.wait("vector", "dv", tm)
        tv = P.do("vector", lambda v: v.tensor_add(var, var, s1), sem="dv")
        P.wait("scalar", "dv", tv)
        tsq = P.do("scalar", lambda s: s.activation(out=sd_, in_=var, func=AF.Sqrt), sem="ac")
        P.wait("vector", "ac", tsq)
        tr = P.do("vector", lambda v: v.reciprocal(rstd, sd_), sem="dv")
        P.wait("vector", "dv", tr)
        tb = P.do("vector", lambda v: v.scalar_tensor_tensor(
            bias, m, -1.0, rstd, ALU.mult, ALU.mult), sem="dv")
        P.wait("scalar", "dv", tb)
        P.wait("scalar", "ac", ts2)
        txn = P.do("scalar", lambda s: s.activation(
            out=xn[:, :], in_=xld[:, :], func=AF.Identity,
            scale=rstd, bias=bias), sem="ac")
        env["_xld_free"] = txn
        P.wait("vector", "ac", txn)
        if "_xsf_cp" in env:
            P.wait("vector", "dv", env["_xsf_cp"])
        trd = None
        for gi in range(GI):
            trd = P.do("vector", lambda v, gi=gi: v.reduce_sum(
                xsf[:, gi:gi + 1],
                xn[:, gi * 128:(gi + 1) * 128], axis=AX.X), sem="dv")
        P.wait("vector", "dv", trd)
        env["_xsf_cp"] = P.do("vector", lambda v, tt=tt: v.tensor_copy(
            xs_sb[:, tt * GI:(tt + 1) * GI], xsf[:, :]), sem="dv")
        P.wait("tensor", "ac", txn)
        for gi in range(GI):
            ps = bank[gi % 4]
            if len(tr_ev) >= 4:
                P.wait("tensor", "ac", tr_ev[-4])
            tp = P.do("tensor", lambda t, gi=gi, ps=ps: t.matmul(
                ps[:, 0:128], xn[:, gi * 128:(gi + 1) * 128],
                ident[:], start=True, stop=True), sem="pe")
            if gi == GI - 1:
                env["_ln_trmm"] = tp
            P.wait("scalar", "pe", tp)
            te = P.do("scalar", lambda s, gi=gi, tt=tt, ps=ps: s.activation(
                out=xnT[:, gi * TOK + tt * 128: gi * TOK + (tt + 1) * 128],
                in_=ps[:, 0:128], func=AF.Copy), sem="ac")
            tr_ev.append(te)
    P.wait("tensor", "dv", P.tok("dv"))
    P.wait("tensor", "ac", tr_ev[-1])
    xst_dv = {}
    for tc in range(NTT):
        ps = bank[tc % 2]
        if tc >= 2:
            P.wait("tensor", "dv", xst_dv[tc - 2])
        tp = P.do("tensor", lambda t, tc=tc, ps=ps: t.matmul(
            ps[0:GI, 0:128], xs_sb[:, tc * GI:(tc + 1) * GI], ident[:],
            start=True, stop=True), sem="pe")
        P.wait("vector", "pe", tp)
        xst_dv[tc] = P.do("vector", lambda v, tc=tc, ps=ps: v.tensor_copy(
            xsT[:, tc * 128:(tc + 1) * 128], ps[0:GI, 0:128]),
            sem=("dv" if tc < NTT - 1 else "ln"))
    env["_ln_done"] = xst_dv[NTT - 1]
    env["_lnxst_dv"] = P.tok("dv")


def emit_gateup(P, env):
    qg_sb, qu_sb, scb, zscb = env["qg_sb"], env["qu_sb"], env["scb"], env["zscb"]
    xnT, xsT, indg = env["xnT"], env["xsT"], env["indg"]
    gsil, hct, hs_sb = env["gsil"], env["hct"], env["hs_sb"]
    bank = env["bank"]
    onescol, ident, hsT = env["onescol"], env["ident"], env["hsT"]
    ht_d, wd_d, wdq = env["ht_d"], env["wd_d"], env["wdq"]
    sdef_sb = env["sdef_sb"]

    P.wait("tensor", "ln", env["_ln_done"])
    P.wait("gpsimd", "ln", env["_ln_done"])
    P.wait("sync", "ln", env["_ln_done"])
    P.wait("tensor", env["_prep_tok"])
    P.wait("tensor", "ac", P.tok("ac"))

    qwg_v = env["qwg"].rearrange("(g p) c -> p g c", p=128)
    qwu_v = env["qwu"].rearrange("(g p) c -> p g c", p=128)
    qwd_v = env["qwd"].rearrange("(ct p) c -> ct p c", p=128)

    wtok, utok, dtok = {}, {}, {}
    ht_wtok = [None] * GO
    wd_wtok = [None] * GO
    env["_ht_wtok"] = ht_wtok
    env["_wd_wtok"] = wd_wtok

    def load_cb(mat, cb):
        src = qwg_v if mat == 0 else qwu_v
        dst = qg_sb if mat == 0 else qu_sb
        s_in = env["sg"] if mat == 0 else env["su"]
        zs_in = env["zsg_d"] if mat == 0 else env["zsu_d"]
        bb = cb % 2
        if (mat, cb - 2) in utok:
            P.wait("gpsimd", "pe", utok[(mat, cb - 2)])
        if (mat, cb - 1) in utok:
            P.wait("gpsimd", "pe", utok[(mat, cb - 1)])
            P.wait("sync", "pe", utok[(mat, cb - 1)])
        sl = slice(cb * 256, (cb + 1) * 256)
        t1 = P.dma("gpsimd", dst[:, bb, :].rearrange("p (g c) -> p g c", c=256),
                   src[:, :, sl], sem=("qg%d" % bb if mat == 0 else "qu%d" % bb))
        t2 = P.dma("gpsimd", scb[:, mat * 256:(mat + 1) * 256], s_in[:, sl],
                   sem=("scg" if mat == 0 else "scu"))
        t3 = P.dma("sync", zscb[:, mat * 256:(mat + 1) * 256], zs_in[:, sl],
                   sem=("zsgq" if mat == 0 else "zsuq"))
        wtok[(mat, cb)] = (t1, t2, t3)

    def load_dct(ct):
        if ct >= 1 and wd_wtok[ct - 1] is not None:
            P.wait("gpsimd", wd_wtok[ct - 1])
        dtok[ct] = P.dma("gpsimd", wdq[:, :], qwd_v[ct, :, :], sem="wdqi")

    bc_state = [0, None, None]

    def bcast(lhs_lo, lhs_n, lhs_col, rhs, ncol):
        i = bc_state[0] % 2
        bc_state[0] += 1
        bkx = bank[4 + i]
        if bc_state[1 + i] is not None:
            P.wait("tensor", "dv", bc_state[1 + i])
        tp = P.do("tensor", lambda t: t.matmul(
            bkx[:, 0:ncol], indg[lhs_lo:lhs_lo + lhs_n,
                                 lhs_col:lhs_col + 128], rhs,
            start=True, stop=True), sem="pe")
        return i, bkx, tp

    def bcast_done(i, tok):
        bc_state[1 + i] = tok

    load_cb(0, 0)
    load_cb(1, 0)
    load_dct(0)

    hs_ev = {}
    acc_free = {}
    hct_free = None
    hs_pe_tok = None

    for cb in range(NCB):
        for mat in range(2):
            W = (qg_sb if mat == 0 else qu_sb)[:, cb % 2, :]
            t1, t2, t3 = wtok[(mat, cb)]
            P.wait("tensor", t2)
            P.wait("vector", t1)
            for gp in range(GI // 2):
                gi0, gi1 = 2 * gp, 2 * gp + 1
                i = bc_state[0] % 2
                bc_state[0] += 1
                bkx = bank[4 + i]
                if bc_state[1 + i] is not None:
                    P.wait("tensor", "dv", bc_state[1 + i])
                P.do("tensor", lambda t, gi0=gi0, bkx=bkx, mat=mat: t.matmul(
                    bkx[:, 0:256], indg[0:GI, gi0 * 128:gi0 * 128 + 128],
                    scb[:, mat * 256:(mat + 1) * 256],
                    start=True, stop=True), sem="pe")
                tp = P.do("tensor", lambda t, gi1=gi1, bkx=bkx, mat=mat: t.matmul(
                    bkx[:, 256:512], indg[0:GI, gi1 * 128:gi1 * 128 + 128],
                    scb[:, mat * 256:(mat + 1) * 256],
                    start=True, stop=True), sem="pe")
                P.wait("vector", "pe", tp)
                tv = P.do("vector", lambda v, gp=gp, W=W, bkx=bkx: v.tensor_mul(
                    W[:, gp * 512:(gp + 1) * 512], W[:, gp * 512:(gp + 1) * 512],
                    bkx[:, 0:512]), sem="dv")
                bc_state[1 + i] = tv
            if mat == 0:
                env["_dqg"] = P.tok("dv")
            else:
                env["_dqu"] = P.tok("dv")
        for ct in range(2):
            gct = 2 * cb + ct
            W0 = qg_sb[:, cb % 2, :]
            W1 = qu_sb[:, cb % 2, :]
            _, _, t3g = wtok[(0, cb)]
            _, _, t3u = wtok[(1, cb)]
            P.wait("tensor", t3g)
            P.wait("tensor", t3u)
            P.wait("tensor", "dv", env["_dqg"])
            P.wait("tensor", "dv", env["_dqu"])
            for tb in range(2):
                bk = bank[tb]
                for gi in range(GI):
                    if gi == 0 and (0, tb) in acc_free:
                        P.wait("tensor", "ac", acc_free[(0, tb)])
                    P.do("tensor", lambda t, gi=gi, ct=ct, tb=tb, bk=bk, W0=W0: t.matmul(
                        bk[:], W0[:, gi * 256 + ct * 128: gi * 256 + ct * 128 + 128],
                        xnT[:, gi * TOK + tb * 512: gi * TOK + (tb + 1) * 512],
                        start=(gi == 0), stop=False), sem="pe")
                tzg = P.do("tensor", lambda t, ct=ct, tb=tb, bk=bk: t.matmul(
                    bk[:], zscb[0:GI, ct * 128: ct * 128 + 128],
                    xsT[:, tb * 512:(tb + 1) * 512],
                    start=False, stop=True), sem="pe")
                env["_lastg_pe"] = tzg
                P.wait("scalar", "pe", tzg)
                tsl = P.do("scalar", lambda s, tb=tb, bk=bk: s.activation(
                    out=gsil[:, tb * 512:(tb + 1) * 512], in_=bk[:],
                    func=AF.Silu), sem="ac")
                acc_free[(0, tb)] = tsl
            for tb in range(2):
                bk = bank[2 + tb]
                for gi in range(GI):
                    if gi == 0 and (1, tb) in acc_free:
                        P.wait("tensor", "dv", acc_free[(1, tb)])
                    P.do("tensor", lambda t, gi=gi, ct=ct, tb=tb, bk=bk, W1=W1: t.matmul(
                        bk[:], W1[:, gi * 256 + ct * 128: gi * 256 + ct * 128 + 128],
                        xnT[:, gi * TOK + tb * 512: gi * TOK + (tb + 1) * 512],
                        start=(gi == 0), stop=False), sem="pe")
                tzu = P.do("tensor", lambda t, ct=ct, tb=tb, bk=bk: t.matmul(
                    bk[:], zscb[0:GI, 256 + ct * 128: 256 + ct * 128 + 128],
                    xsT[:, tb * 512:(tb + 1) * 512],
                    start=False, stop=True), sem="pe")
                env["_lastu_pe"] = tzu
                P.wait("vector", "pe", tzu)
                P.wait("vector", "ac", acc_free[(0, tb)])
                if hct_free is not None:
                    P.wait("vector", hct_free)
                if hs_pe_tok is not None:
                    P.wait("vector", "pe", hs_pe_tok)
                th = P.do("vector", lambda v, tb=tb, bk=bk: v.tensor_mul(
                    hct[:, tb * 512:(tb + 1) * 512], bk[:],
                    gsil[:, tb * 512:(tb + 1) * 512]), sem="dv")
                acc_free[(1, tb)] = th
            th_all = P.tok("dv")
            hb = bank[6 + gct % 2]
            P.wait("tensor", "dv", th_all)
            if gct - 2 in hs_ev:
                P.wait("tensor", "dv", hs_ev[gct - 2])
            for tc in range(NTT):
                P.do("tensor", lambda t, tc=tc, hb=hb: t.matmul(
                    hb[:, tc:tc + 1], hct[:, tc * 128:(tc + 1) * 128],
                    onescol[:], start=True, stop=True), sem="pe")
            hs_pe_tok = P.tok("pe")
            P.wait("vector", "pe", hs_pe_tok)
            tev = P.do("vector", lambda v, gct=gct, hb=hb: v.tensor_copy(
                hs_sb[:, gct * NTT:(gct + 1) * NTT], hb[:, 0:NTT]), sem="dv")
            hs_ev[gct] = tev
            P.wait("scalar", "dv", th_all)
            hct_free = P.dma("scalar", ht_d[gct, :, :], hct[:, :], sem="htw")
            ht_wtok[gct] = hct_free
            ct2 = gct
            P.wait("vector", dtok[ct2])
            base = (ct2 // 32) * 32
            nrow = min(32, GO - base)
            last_dq = None
            for mb in range(8):
                i, bkx, tp = bcast(base, nrow, (ct2 - base) * 128,
                                   sdef_sb[base:base + nrow, mb * 512:(mb + 1) * 512],
                                   512)
                P.wait("vector", "pe", tp)
                tv = P.do("vector", lambda v, mb=mb, bkx=bkx: v.tensor_mul(
                    wdq[:, mb * 512:(mb + 1) * 512], wdq[:, mb * 512:(mb + 1) * 512],
                    bkx[:, 0:512]), sem="dv")
                bcast_done(i, tv)
                last_dq = tv
            P.wait("scalar", "dv", last_dq)
            wd_wtok[ct2] = P.dma("scalar", wd_d[ct2, :, :], wdq[:, :], sem="wdw")
            if ct2 + 1 < GO:
                load_dct(ct2 + 1)
        utok[(0, cb)] = env["_lastg_pe"]
        utok[(1, cb)] = env["_lastu_pe"]
        if cb + 1 < NCB:
            load_cb(0, cb + 1)
            load_cb(1, cb + 1)
    P.wait("tensor", "dv", P.tok("dv"))
    hst_dv = {}
    for tc in range(NTT):
        ps = bank[tc % 2]
        if tc >= 2:
            P.wait("tensor", "dv", hst_dv[tc - 2])
        tp = P.do("tensor", lambda t, tc=tc, ps=ps: t.matmul(
            ps[0:GO, 0:128], hs_sb.rearrange("p (ct tc) -> p tc ct", tc=NTT)[:, tc, :],
            ident[:], start=True, stop=True), sem="pe")
        P.wait("vector", "pe", tp)
        hst_dv[tc] = P.do("vector", lambda v, tc=tc, ps=ps: v.tensor_copy(
            hsT[:, tc * 128:(tc + 1) * 128], ps[0:GO, 0:128]),
            sem=("dv" if tc < NTT - 1 else "ln"))
    env["_gu_done"] = hst_dv[NTT - 1]


def emit_down(P, env):
    htin, wdt, xacc = env["htin"], env["wdt"], env["xacc"]
    zsdmb, hsT = env["zsdmb"], env["hsT"]
    bank = env["bank"]
    ht_d, wd_d, xbase_d, xout = env["ht_d"], env["wd_d"], env["xbase_d"], env["xout"]
    ht_wtok, wd_wtok = env["_ht_wtok"], env["_wd_wtok"]

    P.wait("tensor", "ln", env["_gu_done"])
    P.wait("vector", "ln", env["_gu_done"])
    P.wait("sync", "xbw", P.tok("xbw"))

    htok, wtok2 = {}, {}
    dn_use = {}
    env["_dn_use"] = dn_use
    qg_sb, qu_sb = env["qg_sb"], env["qu_sb"]
    chunks = []
    c0 = 0
    while c0 < GO:
        chunks.append((c0, min(4, GO - c0)))
        c0 += 4
    NCH = len(chunks)

    def ht4(s):
        return qg_sb[:, s // 2, (s % 2) * 4 * TOK:(s % 2 + 1) * 4 * TOK] \
            .rearrange("p (b c) -> p b c", c=TOK)

    def wd4(s):
        return qu_sb[:, s // 2, (s % 2) * 4 * 512:(s % 2 + 1) * 4 * 512] \
            .rearrange("p (b c) -> p b c", c=512)

    P.wait("scalar", "pe", P.tok("pe"))
    P.wait("sync", "pe", P.tok("pe"))

    def load_chunk(k):
        mb, kc = divmod(k, NCH)
        ct0, nct = chunks[kc]
        s = k % 4
        if k >= 4:
            P.wait("scalar", "pe", dn_use[k - 4])
            P.wait("sync", "pe", dn_use[k - 4])
        P.wait("scalar", ht_wtok[ct0 + nct - 1])
        P.wait("sync", wd_wtok[ct0 + nct - 1])
        htok[k] = P.dma(
            "scalar", ht4(s)[:, 0:nct, :],
            ht_d[ct0:ct0 + nct, :, :].rearrange("ct p t -> p ct t"),
            sem="ht%d" % s)
        wtok2[k] = P.dma(
            "sync", wd4(s)[:, 0:nct, :],
            wd_d[ct0:ct0 + nct, :, mb * 512:(mb + 1) * 512]
            .rearrange("ct p t -> p ct t"),
            sem="wd%d" % s)

    for _k in range(4):
        load_chunk(_k)
    zt_prev = None
    evac_toks = [None] * NTT
    xw_tok = {}
    for mb in range(8):
        if zt_prev is not None:
            P.wait("sync", "pe", zt_prev)
        tz = P.dma("sync", zsdmb[:, :], env["zsd_d"][:, mb * 512:(mb + 1) * 512],
                   sem="zsm")
        for kc in range(NCH):
            k = mb * NCH + kc
            ct0, nct = chunks[kc]
            s = k % 4
            P.wait("tensor", htok[k])
            P.wait("tensor", wtok2[k])
            for j in range(nct):
                for tt in range(NTT):
                    if ct0 + j == 0 and evac_toks[tt] is not None:
                        P.wait("tensor", "dv", evac_toks[tt])
                    P.do("tensor", lambda t, tt=tt, j=j, s=s: t.matmul(
                        bank[tt][:], ht4(s)[:, j, tt * 128:(tt + 1) * 128],
                        wd4(s)[:, j, :], start=(ct0 + j == 0), stop=False),
                        sem="pe")
            dn_use[k] = P.tok("pe")
            if k + 4 < 8 * NCH:
                load_chunk(k + 4)
        P.wait("tensor", tz)
        for tt in range(NTT):
            tzc = P.do("tensor", lambda t, tt=tt: t.matmul(
                bank[tt][:], hsT[:, tt * 128:(tt + 1) * 128], zsdmb[:, :],
                start=False, stop=True), sem="pe")
            if tt == NTT - 1:
                zt_prev = tzc
            if (tt % 2) in xw_tok:
                P.wait("sync", xw_tok[tt % 2])
            txa = P.dma("sync", xacc[:, tt % 2, :],
                        xbase_d[tt * 128:(tt + 1) * 128, mb * 512:(mb + 1) * 512],
                        sem="xa%d" % (tt % 2))
            P.wait("vector", "pe", tzc)
            P.wait("vector", txa)
            ta = P.do("vector", lambda v, tt=tt: v.tensor_add(
                xacc[:, tt % 2, :], xacc[:, tt % 2, :], bank[tt][:]), sem="dv")
            evac_toks[tt] = ta
            P.wait("sync", "dv", ta)
            xw_tok[tt % 2] = P.dma(
                "sync", xout[tt * 128:(tt + 1) * 128, mb * 512:(mb + 1) * 512],
                xacc[:, tt % 2, :], sem="xo%d" % (tt % 2))
    env["_down_done"] = P.tok("dv")


def emit_projections(P, env):
    qg_sb, qu_sb = env["qg_sb"], env["qu_sb"]
    xnT, pstg = env["xnT"], env["pstg"]
    bank = env["bank"]

    jobs = [(env["wq"], env["qt"], D, QSC),
            (env["wk"], env["kt"], NKV * HD, 1.0),
            (env["wv"], env["vt"], NKV * HD, 1.0)]

    P.wait("gpsimd", "dv", env["_down_done"])
    P.wait("gpsimd", "pe", P.tok("pe"))
    P.wait("tensor", "ln", env["_ln_done"])
    P.wait("tensor", "ac", P.tok("ac"))
    P.wait("scalar", "dv", env["_down_done"])

    seq = []
    for (w_in, out_d, ncols, scale) in jobs:
        w_v = w_in.rearrange("(g p) c -> p g c", p=128)
        for db in range(ncols // 256):
            seq.append((w_v, out_d, db, scale))

    ltoks, used, bufs = {}, {}, {}

    def load_j(j):
        w_v, out_d, db, scale = seq[j]
        buf = (qg_sb if (j % 4) < 2 else qu_sb)[:, j % 2, :]
        if (j - 4) in used:
            P.wait("gpsimd", "pe", used[j - 4])
        sl = slice(db * 256, (db + 1) * 256)
        ltoks[j] = P.dma("gpsimd", buf.rearrange("p (g c) -> p g c", c=256),
                         w_v[:, :, sl], sem="pj%d" % (j % 4))
        return buf

    for j in range(min(4, len(seq))):
        bufs[j] = load_j(j)
    ev_free = {}
    pj_w = {}
    for j in range(len(seq)):
        w_v, out_d, db, scale = seq[j]
        buf = bufs[j]
        P.wait("tensor", ltoks[j])
        for ct in range(2):
            for tb in range(2):
                bki = ct * 2 + tb
                bk = bank[bki]
                if bki in ev_free:
                    P.wait("tensor", "ac", ev_free[bki])
                for gi in range(GI):
                    P.do("tensor", lambda t, gi=gi, ct=ct, tb=tb, bk=bk, buf=buf: t.matmul(
                        bk[:], buf[:, gi * 256 + ct * 128: gi * 256 + ct * 128 + 128],
                        xnT[:, gi * TOK + tb * 512: gi * TOK + (tb + 1) * 512],
                        start=(gi == 0), stop=(gi == GI - 1)), sem="pe")
                tmm = P.tok("pe")
                P.wait("scalar", "pe", tmm)
                if (bki % 2) in pj_w:
                    P.wait("scalar", pj_w[bki % 2])
                tev = P.do("scalar", lambda s, bk=bk, bki=bki, scale=scale: s.activation(
                    out=pstg[:, bki % 2, :], in_=bk[:], func=AF.Copy, scale=scale),
                    sem="ac")
                ev_free[bki] = tev
                P.wait("scalar", "ac", tev)
                r0 = db * 256 + ct * 128
                pj_w[bki % 2] = P.dma(
                    "scalar", out_d[r0:r0 + 128, tb * 512:(tb + 1) * 512],
                    pstg[:, bki % 2, :], sem="pw%d" % (bki % 2))
        used[j] = P.tok("pe")
        if j + 4 < len(seq):
            bufs[j + 4] = load_j(j + 4)


def emit_oproj(P, env):
    qg_sb, qu_sb = env["qg_sb"], env["qu_sb"]
    xnT, xacc, xld = env["xnT"], env["xacc"], env["xld"]
    bank = env["bank"]
    attnT, wo, xin, xbase_d = env["attnT"], env["wo"], env["xin"], env["xbase_d"]

    at_v = attnT.rearrange("(g p) c -> p g c", p=128)
    t_at = P.dma("sync", xnT.rearrange("p (g c) -> p g c", c=TOK), at_v, sem="ati")
    wo_v = wo.rearrange("(g p) c -> p g c", p=128)

    P.wait("sync", "dv", P.tok("dv"))
    P.wait("scalar", "dv", P.tok("dv"))
    for s_ in ("pwa", "pwb", "pw"):
        P.wait("sync", s_, P.tok(s_))
        P.wait("scalar", s_, P.tok(s_))

    ldt, castt, wuse = {}, {}, {}
    NQ = 4 * (D // 256)

    def stage(q):
        mbq, qh = divmod(q, 4)
        eng = "sync" if q % 2 == 0 else "scalar"
        if q >= 2:
            P.wait(eng, "ac", castt[q - 2])
        sl = slice(mbq * 256, (mbq + 1) * 256)
        ldt[q] = P.dma(
            eng, xld[:, (q % 2) * 2048:(q % 2) * 2048 + 2048]
            .rearrange("p (g c) -> p g c", c=256),
            wo_v[:, qh * 8:(qh + 1) * 8, sl], sem="pj%d" % (q % 4))

    def docast(q):
        mbq, qh = divmod(q, 4)
        buf = (qg_sb if (mbq % 4) < 2 else qu_sb)[:, mbq % 2, :]
        P.wait("scalar", ldt[q])
        if qh == 0 and (mbq - 4) in wuse:
            P.wait("scalar", "pe", wuse[mbq - 4])
        castt[q] = P.do("scalar", lambda s, buf=buf, qh=qh, q=q: s.activation(
            out=buf[:, qh * 2048:(qh + 1) * 2048],
            in_=xld[:, (q % 2) * 2048:(q % 2) * 2048 + 2048],
            func=AF.Copy), sem="ac")

    stage(0)
    stage(1)
    P.wait("tensor", t_at)
    evd = {}
    xw = {}
    for mbq in range(D // 256):
        buf = (qg_sb if (mbq % 4) < 2 else qu_sb)[:, mbq % 2, :]
        for q in range(4 * mbq, 4 * mbq + 4):
            docast(q)
            if q + 2 < NQ:
                stage(q + 2)
        P.wait("tensor", "ac", castt[4 * mbq + 3])
        for gi in range(GI):
            for tt in range(NTT):
                if gi == 0 and tt in evd:
                    P.wait("tensor", "dv", evd[tt])
                P.do("tensor", lambda t, gi=gi, tt=tt, buf=buf: t.matmul(
                    bank[tt][:, 0:256],
                    xnT[:, gi * TOK + tt * 128: gi * TOK + (tt + 1) * 128],
                    buf[:, gi * 256:(gi + 1) * 256],
                    start=(gi == 0), stop=(gi == GI - 1)), sem="pe")
        tmm = P.tok("pe")
        for tt in range(NTT):
            if (tt % 2) in xw:
                P.wait("sync", xw[tt % 2])
            txa = P.dma("sync", xacc[:, tt % 2, 0:256],
                        xin[tt * 128:(tt + 1) * 128, mbq * 256:(mbq + 1) * 256],
                        sem="xa%d" % (tt % 2))
            P.wait("vector", "pe", tmm)
            P.wait("vector", txa)
            ta = P.do("vector", lambda v, tt=tt: v.tensor_add(
                xacc[:, tt % 2, 0:256], xacc[:, tt % 2, 0:256],
                bank[tt][:, 0:256]), sem="dv")
            evd[tt] = ta
            P.wait("sync", "dv", ta)
            xw[tt % 2] = P.dma(
                "sync", xbase_d[tt * 128:(tt + 1) * 128, mbq * 256:(mbq + 1) * 256],
                xacc[:, tt % 2, 0:256], sem="xo%d" % (tt % 2))
        wuse[mbq] = P.tok("pe")
    env["_op_done"] = P.do("vector", lambda v: v.nop(), sem="ln")



def build_attention():
    nc = bass.Bass("TRN2", target_bir_lowering=False, debug=False)
    ap = lambda t: t.ap()
    qtp = ap(nc.dram_tensor("qtp", [2, 4 * HD, S], BF16, kind="ExternalInput"))
    ktp = ap(nc.dram_tensor("ktp", [2, HD, S], BF16, kind="ExternalInput"))
    vtp = ap(nc.dram_tensor("vtp", [2, HD, S], BF16, kind="ExternalInput"))
    ao = ap(nc.dram_tensor("ao", [2 * 4 * HD, S], BF16, kind="ExternalOutput"))

    import contextlib
    with contextlib.ExitStack() as _es:
        _e = _es.enter_context
        qts = _e(nc.sbuf_tensor("qts", [128, 4 * S], BF16))
        kts = _e(nc.sbuf_tensor("kts", [128, S], BF16))
        vtq = _e(nc.sbuf_tensor("vtq", [128, S], BF16))
        vts = _e(nc.sbuf_tensor("vts", [128, S], BF16))
        et = _e(nc.sbuf_tensor("et", [128, 2, 512], BF16))
        eacc = _e(nc.sbuf_tensor("eacc", [128, 512], F32))
        eaccb = _e(nc.sbuf_tensor("eaccb", [128, 512], BF16))
        lrec = _e(nc.sbuf_tensor("lrec", [128, 512], F32))
        masks = _e(nc.sbuf_tensor("masks", [128, 4 * 512], BF16))
        ones128 = _e(nc.sbuf_tensor("ones128", [128, 128], BF16))
        ident = _e(nc.sbuf_tensor("ident", [128, 128], BF16))
        astg = _e(nc.sbuf_tensor("astg", [128, 2, 512], BF16))
        expb = _e(nc.sbuf_tensor("expb", [128, 1], F32))
        bk0 = _e(nc.psum_tensor("bk0", [128, 512], F32))
        bk1 = _e(nc.psum_tensor("bk1", [128, 512], F32))
        bk2 = _e(nc.psum_tensor("bk2", [128, 512], F32))
        bk3 = _e(nc.psum_tensor("bk3", [128, 512], F32))
        bk4 = _e(nc.psum_tensor("bk4", [128, 512], F32))
        bk5 = _e(nc.psum_tensor("bk5", [128, 512], F32))
        semnames = ["pe", "dv", "ac", "use", "gp", "qld", "kld", "vld",
                    "ao0", "ao1"]
        handles = {n: _e(nc.semaphore(n)) for n in semnames}
        block = _e(nc.Block())
        bank = [bk0, bk1, bk2, bk3, bk4, bk5]

        def body(P):
            P.gp(lambda g: g.memset(ones128[:], 1.0), sem="use")
            P.gp(lambda g: g.memset(expb[:], EXPB), sem="use")
            P.gp(lambda g: g.memset(ident[:], 1.0), sem="use")
            P.gp(lambda g: g.affine_select(
                ident[:], ident[:], pattern=[[1, 128]], compare_op=ALU.is_equal,
                fill=0.0, channel_multiplier=-1), sem="use")
            P.gp(lambda g: g.memset(masks[:], 1.0), sem="use")
            tmask = None
            for j in range(4):
                tmask = P.gp(lambda g, j=j: g.affine_select(
                    masks[:, j * 512:(j + 1) * 512], masks[:, j * 512:(j + 1) * 512],
                    pattern=[[1, 512]], compare_op=ALU.is_ge, fill=0.0,
                    base=-128 * j, channel_multiplier=-1), sem="use")
            ao_last = {}
            for pp in range(2):
                if pp == 1:
                    P.wait("sync", "ao0", P.tok("ao0"))
                    P.wait("sync", "ao1", P.tok("ao1"))
                    P.wait("sync", "ac", P.tok("ac"))
                    P.wait("sync", "pe", P.tok("pe"))
                tq = P.dma("sync", qts.rearrange("p (h c) -> p h c", c=S),
                           qtp[pp].rearrange("(h p) c -> p h c", p=128), sem="qld")
                tk = P.dma("sync", kts[:, :], ktp[pp, :, :], sem="kld")
                tv = P.dma("sync", vtq[:, :], vtp[pp, :, :], sem="vld")
                P.wait("tensor", tv)
                if pp == 0:
                    P.wait("tensor", "use", tmask)
                vt_ev = []
                for kt in range(32):
                    ps = bank[4 + kt % 2]
                    if len(vt_ev) >= 2:
                        P.wait("tensor", "dv", vt_ev[-2])
                    tp = P.do("tensor", lambda t, kt=kt, ps=ps: t.matmul(
                        ps[:, 0:128], vtq[:, kt * 128:(kt + 1) * 128], ident[:],
                        start=True, stop=True), sem="pe")
                    P.wait("vector", "pe", tp)
                    te = P.do("vector", lambda v, kt=kt, ps=ps: v.tensor_copy(
                        vts[:, kt * 128:(kt + 1) * 128], ps[:, 0:128]), sem="dv")
                    vt_ev.append(te)
                P.wait("tensor", tq)
                P.wait("tensor", tk)
                P.wait("tensor", "dv", vt_ev[-1])
                if pp == 0:
                    P.wait("scalar", "use", tmask)
                    P.wait("vector", "use", tmask)
                sc_free = [None, None]
                et_use = [None, None]
                av_free = None
                l_free = None
                lm_tok = None
                for hh in range(4):
                    for qb in range(8):
                        nkt = 4 * (qb + 1)
                        av_tok = None
                        prev_tla = None
                        for kt in range(nkt):
                            sb = bank[kt % 2]
                            if sc_free[kt % 2] is not None:
                                P.wait("tensor", "ac", sc_free[kt % 2])
                            tmm = P.do("tensor", lambda t, kt=kt, hh=hh, qb=qb, sb=sb: t.matmul(
                                sb[:], kts[:, kt * 128:(kt + 1) * 128],
                                qts[:, hh * S + qb * 512: hh * S + (qb + 1) * 512],
                                start=True, stop=True), sem="pe")
                            P.wait("scalar", "pe", tmm)
                            if et_use[kt % 2] is not None:
                                pe_t, dv_t = et_use[kt % 2]
                                P.wait("scalar", "pe", pe_t)
                                P.wait("scalar", "dv", dv_t)
                            tex = P.do("scalar", lambda s, kt=kt, sb=sb: s.activation(
                                out=et[:, kt % 2, :], in_=sb[:], func=AF.Exp,
                                bias=expb[:, 0:1]), sem="ac")
                            sc_free[kt % 2] = tex
                            dj = kt - 4 * qb
                            P.wait("vector", "ac", tex)
                            if dj >= 0:
                                tmk = P.do("vector", lambda v, kt=kt, dj=dj: v.tensor_mul(
                                    et[:, kt % 2, :], et[:, kt % 2, :],
                                    masks[:, dj * 512:(dj + 1) * 512]), sem="dv")
                            else:
                                tmk = None
                            if prev_tla is not None:
                                P.wait("vector", "dv", prev_tla)
                            if kt == 0:
                                tla = P.do("vector", lambda v, kt=kt: v.tensor_copy(
                                    eacc[:, :], et[:, kt % 2, :]), sem="dv")
                            else:
                                tla = P.do("vector", lambda v, kt=kt: v.tensor_add(
                                    eacc[:, :], eacc[:, :], et[:, kt % 2, :]), sem="dv")
                            prev_tla = tla
                            if tmk is not None:
                                P.wait("tensor", "dv", tmk)
                            else:
                                P.wait("tensor", "ac", tex)
                            if kt == 0 and av_free is not None:
                                P.wait("tensor", "dv", av_free)
                            tav = P.do("tensor", lambda t, kt=kt, nkt=nkt: t.matmul(
                                bank[2][:], vts[:, kt * 128:(kt + 1) * 128],
                                et[:, kt % 2, :], start=(kt == 0),
                                stop=(kt == nkt - 1)), sem="pe")
                            et_use[kt % 2] = (tav, tla)
                            av_tok = tav
                        if lm_tok is not None:
                            P.wait("vector", "pe", lm_tok)
                        if prev_tla is not None:
                            P.wait("vector", "dv", prev_tla)
                        tcb = P.do("vector", lambda v: v.tensor_copy(
                            eaccb[:, :], eacc[:, :]), sem="dv")
                        P.wait("tensor", "dv", tcb)
                        if l_free is not None:
                            P.wait("tensor", "dv", l_free)
                        tlm = P.do("tensor", lambda t: t.matmul(
                            bank[3][:], ones128[:], eaccb[:, :],
                            start=True, stop=True), sem="pe")
                        lm_tok = tlm
                        P.wait("vector", "pe", tlm)
                        trc = P.do("vector", lambda v: v.reciprocal(
                            lrec[:, :], bank[3][:]), sem="dv")
                        l_free = trc
                        P.wait("vector", "pe", av_tok)
                        P.wait("vector", "dv", trc)
                        if ((hh * 8 + qb) % 2) in ao_last:
                            P.wait("vector", ao_last[(hh * 8 + qb) % 2])
                        tnm = P.do("vector", lambda v, hh=hh, qb=qb: v.tensor_mul(
                            astg[:, (hh * 8 + qb) % 2, :], bank[2][:], lrec[:, :]),
                            sem="dv")
                        av_free = tnm
                        P.wait("sync", "dv", tnm)
                        ao_last[(hh * 8 + qb) % 2] = P.dma(
                            "sync",
                            ao[pp * 512 + hh * 128: pp * 512 + (hh + 1) * 128,
                               qb * 512:(qb + 1) * 512],
                            astg[:, (hh * 8 + qb) % 2, :],
                            sem="ao%d" % ((hh * 8 + qb) % 2))
            P.wait("sync", "ao0")
            P.wait("sync", "ao1")

        run_spmd_program(nc, block, handles, body)
    return nc



_NC_A = None
_NC_B = None
_NC_C = None


def _get_programs():
    global _NC_A, _NC_B, _NC_C
    if _NC_A is None:
        _NC_A = build_tokenside(with_proj=True, with_oproj=False)
        _NC_B = build_attention()
        _NC_C = build_tokenside(with_proj=False, with_oproj=True)
    return _NC_A, _NC_B, _NC_C


def kernel(L0_attn_hidden_states, L0_attn_w_q, L0_attn_w_k, L0_attn_w_v,
           L0_attn_w_o, L0_ffn_qw_gate, L0_ffn_z_gate, L0_ffn_s_gate,
           L0_ffn_qw_up, L0_ffn_z_up, L0_ffn_s_up, L0_ffn_qw_down,
           L0_ffn_z_down, L0_ffn_s_down):
    nc_a, nc_b, nc_c = _get_programs()

    x0 = np.ascontiguousarray(np.asarray(L0_attn_hidden_states, dtype=np.float32)
                              .reshape(B * S, D))
    qwg = np.ascontiguousarray(np.asarray(L0_ffn_qw_gate).astype(np.int8))
    qwu = np.ascontiguousarray(np.asarray(L0_ffn_qw_up).astype(np.int8))
    qwd = np.ascontiguousarray(np.asarray(L0_ffn_qw_down).astype(np.int8))
    zg = np.ascontiguousarray(np.asarray(L0_ffn_z_gate).astype(np.int8))
    sg = np.ascontiguousarray(np.asarray(L0_ffn_s_gate, dtype=np.float32))
    zu = np.ascontiguousarray(np.asarray(L0_ffn_z_up).astype(np.int8))
    su = np.ascontiguousarray(np.asarray(L0_ffn_s_up, dtype=np.float32))
    zd = np.ascontiguousarray(np.asarray(L0_ffn_z_down).astype(np.int8))
    sd = np.ascontiguousarray(np.asarray(L0_ffn_s_down, dtype=np.float32))
    wq = np.ascontiguousarray(np.asarray(L0_attn_w_q, dtype=np.float32))
    wk = np.ascontiguousarray(np.asarray(L0_attn_w_k, dtype=np.float32))
    wv = np.ascontiguousarray(np.asarray(L0_attn_w_v, dtype=np.float32))
    wo = np.ascontiguousarray(np.asarray(L0_attn_w_o, dtype=np.float32))

    common = dict(qwg=qwg, qwu=qwu, qwd=qwd, zg=zg, sg=sg, zu=zu, su=su,
                  zd=zd, sd=sd)

    in_a = []
    for c in range(NC):
        m = dict(common)
        m.update(dict(xin=x0[c * TOK:(c + 1) * TOK], wq=wq, wk=wk, wv=wv))
        in_a.append(m)
    res_a = run_bass_kernel_spmd(nc_a, in_a, core_ids=list(range(NC))).results
    x1 = [res_a[c]["xout"] for c in range(NC)]
    qts = [res_a[c]["qt"] for c in range(NC)]
    kts = [res_a[c]["kt"] for c in range(NC)]
    vts = [res_a[c]["vt"] for c in range(NC)]

    in_b = []
    for c in range(NC):
        b = (2 * c) // 8
        owners = [4 * b + o for o in range(4)]
        qtp = np.empty((2, 4 * HD, S), dtype=BF16NP)
        ktp = np.empty((2, HD, S), dtype=BF16NP)
        vtp = np.empty((2, HD, S), dtype=BF16NP)
        for k in range(2):
            g = (2 * c + k) % 8
            qtp[k] = np.concatenate([qts[o][g * 512:(g + 1) * 512, :] for o in owners], axis=1)
            ktp[k] = np.concatenate([kts[o][g * 128:(g + 1) * 128, :] for o in owners], axis=1)
            vtp[k] = np.concatenate([vts[o][g * 128:(g + 1) * 128, :] for o in owners], axis=1)
        in_b.append(dict(qtp=qtp, ktp=ktp, vtp=vtp))
    res_b = run_bass_kernel_spmd(nc_b, in_b, core_ids=list(range(NC))).results
    aos = [res_b[c]["ao"] for c in range(NC)]

    attnT_full = []
    for b in range(B):
        blocks = []
        for g in range(8):
            oc = 4 * b + g // 2
            slot = g % 2
            blocks.append(aos[oc][slot * 512:(slot + 1) * 512, :])
        attnT_full.append(np.concatenate(blocks, axis=0))
    in_c = []
    for c in range(NC):
        b = c // 4
        cs = (c % 4) * TOK
        m = dict(common)
        m.update(dict(
            xin=np.ascontiguousarray(x1[c]),
            attnT=np.ascontiguousarray(attnT_full[b][:, cs:cs + TOK]),
            wo=wo,
        ))
        in_c.append(m)
    res_c = run_bass_kernel_spmd(nc_c, in_c, core_ids=list(range(NC))).results
    out = np.concatenate([res_c[c]["xout"] for c in range(NC)], axis=0)
    return out.reshape(B, S, D).astype(np.float32)


# version 04
# speedup vs baseline: 1.2470x; 1.0537x over previous
import numpy as np
import ml_dtypes
import concourse.bass as bass
import concourse.mybir as mybir
from concourse.bass_utils import run_bass_kernel_spmd

F32 = mybir.dt.float32
BF16 = mybir.dt.bfloat16
I32 = mybir.dt.int32
I8 = mybir.dt.int8
AF = mybir.ActivationFunctionType
ALU = mybir.AluOpType
AX = mybir.AxisListType
BF16NP = ml_dtypes.bfloat16

D = 4096
HID = 11008
S = 4096
B = 2
NC = 8
TOK = (B * S) // NC
GI = D // 128
GO = HID // 128
NH = 32
NKV = 8
HD = 128
EPS = 1e-5
EXPB = -20.0
NTT = TOK // 128
NCB = HID // 256
QSC = float(HD) ** -0.5


class Prog:

    def __init__(self, me, eng, handles):
        self.me = me
        self.eng = eng
        self.h = handles
        self.v = {k: 0 for k in handles}

    def do(self, engname, fn, sem=None, n=1):
        tok = None
        if sem is not None:
            self.v[sem] += n
            tok = self.v[sem]
        if engname == self.me:
            ins = fn(self.eng)
            if sem is not None:
                ins.then_inc(self.h[sem], n)
        return tok

    def dma(self, engname, out, in_, sem, **kw):
        t = self.do(engname, lambda e: e.dma_start(out=out, in_=in_, **kw),
                    sem=sem, n=16)
        return (sem, t)

    def wait(self, engname, sem, tok=None):
        if isinstance(sem, tuple):
            sem, tok = sem
        elif isinstance(tok, tuple):
            sem, tok = tok
        if engname == self.me:
            self.eng.wait_ge(self.h[sem], self.v[sem] if tok is None else tok)

    def tok(self, sem):
        return self.v[sem]

    def gp(self, fn, sem="gp"):
        t = self.do("gpsimd", fn, sem=sem)
        self.wait("gpsimd", sem, t)
        return t


def run_spmd_program(nc, block, handles, body):
    def mk(engname):
        def walk(eng):
            body(Prog(engname, eng, handles))
        return walk
    block.sync(mk("sync"))
    block.gpsimd(mk("gpsimd"))
    block.tensor(mk("tensor"))
    block.vector(mk("vector"))
    block.scalar(mk("scalar"))



def build_tokenside(with_proj, with_oproj):
    nc = bass.Bass("TRN2", target_bir_lowering=False, debug=False)
    ap = lambda t: t.ap()

    E = {}
    E["xin"] = ap(nc.dram_tensor("xin", [TOK, D], F32, kind="ExternalInput"))
    E["qwg"] = ap(nc.dram_tensor("qwg", [D, HID], I8, kind="ExternalInput"))
    E["qwu"] = ap(nc.dram_tensor("qwu", [D, HID], I8, kind="ExternalInput"))
    E["qwd"] = ap(nc.dram_tensor("qwd", [HID, D], I8, kind="ExternalInput"))
    E["zg"] = ap(nc.dram_tensor("zg", [GI, HID], I8, kind="ExternalInput"))
    E["sg"] = ap(nc.dram_tensor("sg", [GI, HID], F32, kind="ExternalInput"))
    E["zu"] = ap(nc.dram_tensor("zu", [GI, HID], I8, kind="ExternalInput"))
    E["su"] = ap(nc.dram_tensor("su", [GI, HID], F32, kind="ExternalInput"))
    E["zd"] = ap(nc.dram_tensor("zd", [GO, D], I8, kind="ExternalInput"))
    E["sd"] = ap(nc.dram_tensor("sd", [GO, D], F32, kind="ExternalInput"))
    if with_proj:
        E["wq"] = ap(nc.dram_tensor("wq", [D, D], F32, kind="ExternalInput"))
        E["wk"] = ap(nc.dram_tensor("wk", [D, NKV * HD], F32, kind="ExternalInput"))
        E["wv"] = ap(nc.dram_tensor("wv", [D, NKV * HD], F32, kind="ExternalInput"))
        E["qt"] = ap(nc.dram_tensor("qt", [D, TOK], BF16, kind="ExternalOutput"))
        E["kt"] = ap(nc.dram_tensor("kt", [NKV * HD, TOK], BF16, kind="ExternalOutput"))
        E["vt"] = ap(nc.dram_tensor("vt", [NKV * HD, TOK], BF16, kind="ExternalOutput"))
    if with_oproj:
        E["attnT"] = ap(nc.dram_tensor("attnT", [D, TOK], BF16, kind="ExternalInput"))
        E["wo"] = ap(nc.dram_tensor("wo", [D, D], F32, kind="ExternalInput"))
    E["xout"] = ap(nc.dram_tensor("xout", [TOK, D], F32, kind="ExternalOutput"))
    E["xbase_d"] = ap(nc.dram_tensor("xbase_d", [TOK, D], F32))
    E["zsg_d"] = ap(nc.dram_tensor("zsg_d", [GI, HID], BF16))
    E["zsu_d"] = ap(nc.dram_tensor("zsu_d", [GI, HID], BF16))
    E["zsd_d"] = ap(nc.dram_tensor("zsd_d", [GO, D], BF16))
    E["sdeff_d"] = ap(nc.dram_tensor("sdeff_d", [GO, D], BF16))
    E["ht_d"] = ap(nc.dram_tensor("ht_d", [GO, 128, TOK], BF16))
    E["wd_d"] = ap(nc.dram_tensor("wd_d", [GO, 128, D], BF16))

    import contextlib
    with contextlib.ExitStack() as _es:
        _e = _es.enter_context
        xnT = _e(nc.sbuf_tensor("xnT", [128, GI * TOK], BF16))
        wg_sb = _e(nc.sbuf_tensor("wg", [128, GI * 256], BF16))
        wu_sb = _e(nc.sbuf_tensor("wu", [128, GI * 256], BF16))
        qg_i8 = _e(nc.sbuf_tensor("qgi", [128, 2, GI * 256], I8))
        qu_i8 = _e(nc.sbuf_tensor("qui", [128, 2, GI * 256], I8))
        wdq = _e(nc.sbuf_tensor("wdq", [128, D], BF16))
        scb = _e(nc.sbuf_tensor("scb", [GI, 2 * 256], BF16))
        zscb = _e(nc.sbuf_tensor("zscb", [GI, 2, 2 * 256], BF16))
        sdef_sb = _e(nc.sbuf_tensor("sdef", [GO, D], BF16))
        zsdmb = _e(nc.sbuf_tensor("zsdmb", [GO, 512], BF16))
        xsT = _e(nc.sbuf_tensor("xsT", [GI, TOK], BF16))
        hs_sb = _e(nc.sbuf_tensor("hs", [128, NTT * GO], BF16))
        hsT = _e(nc.sbuf_tensor("hsT", [GO, TOK], BF16))
        hct = _e(nc.sbuf_tensor("hct", [128, TOK], BF16))
        htin = _e(nc.sbuf_tensor("htin", [128, 2, TOK], BF16))
        wdt = _e(nc.sbuf_tensor("wdt", [128, 2, 512], BF16))
        ones1 = _e(nc.sbuf_tensor("ones1", [1, 128], BF16))
        onescol = _e(nc.sbuf_tensor("onescol", [128, 1], BF16))
        ident = _e(nc.sbuf_tensor("ident", [128, 128], BF16))
        indg = _e(nc.sbuf_tensor("indg", [128, GI * 128], BF16))
        rowsel = _e(nc.sbuf_tensor("rowsel", [128, GI], F32))
        indone = _e(nc.sbuf_tensor("indone", [128, 128], BF16))
        xld = _e(nc.sbuf_tensor("xld", [128, D], F32))
        xn_sb = _e(nc.sbuf_tensor("xn", [128, D], BF16))
        xs_sb = _e(nc.sbuf_tensor("xs", [128, NTT * GI], BF16))
        xsf = _e(nc.sbuf_tensor("xsf", [128, GI], F32))
        lnst = _e(nc.sbuf_tensor("lnst", [128, 8], F32))
        gsil = _e(nc.sbuf_tensor("gsil", [128, TOK], BF16))
        xacc = _e(nc.sbuf_tensor("xacc", [128, 2, 512], F32))
        pstg = _e(nc.sbuf_tensor("pstg", [128, 2, 512], BF16))
        bk0 = _e(nc.psum_tensor("bk0", [128, 512], F32))
        bk1 = _e(nc.psum_tensor("bk1", [128, 512], F32))
        bk2 = _e(nc.psum_tensor("bk2", [128, 512], F32))
        bk3 = _e(nc.psum_tensor("bk3", [128, 512], F32))
        bk4 = _e(nc.psum_tensor("bk4", [128, 512], F32))
        bk5 = _e(nc.psum_tensor("bk5", [128, 512], F32))
        bk6 = _e(nc.psum_tensor("bk6", [128, 512], F32))
        bk7 = _e(nc.psum_tensor("bk7", [128, 512], F32))
        semnames = ["pe", "dv", "ac", "gp", "pre", "ln", "pz", "ps", "pz0",
                    "pz1", "ps0", "ps1", "pwa", "pwb", "pw", "xli",
                    "xbw", "qg0", "qg1", "qu0", "qu1", "scg", "scu", "zsgq",
                    "zsuq", "wdqi", "wdw", "htw", "ht0", "ht1", "ht2", "ht3",
                    "wd0", "wd1", "wd2", "wd3",
                    "zsm", "xa0", "xa1", "xo0", "xo1", "pj0", "pj1", "pj2",
                    "pj3", "pw0", "pw1", "ati"]
        handles = {n: _e(nc.semaphore(n)) for n in semnames}
        block = _e(nc.Block())
        env = dict(E)
        env.update(dict(
            xnT=xnT, wg_sb=wg_sb, wu_sb=wu_sb, qg_i8=qg_i8, qu_i8=qu_i8,
            wdq=wdq, scb=scb, zscb=zscb,
            sdef_sb=sdef_sb, zsdmb=zsdmb, xsT=xsT, hs_sb=hs_sb, hsT=hsT,
            hct=hct, htin=htin, wdt=wdt, ones1=ones1, onescol=onescol,
            ident=ident, indg=indg, xld=xld, xn_sb=xn_sb, xs_sb=xs_sb,
            lnst=lnst, gsil=gsil, xacc=xacc, pstg=pstg, rowsel=rowsel,
            indone=indone, xsf=xsf,
            bank=[bk0, bk1, bk2, bk3, bk4, bk5, bk6, bk7],
        ))

        def body(P):
            for k in [k for k in env if k.startswith("_")]:
                del env[k]
            emit_prep(P, env)
            if with_oproj:
                emit_oproj(P, env)
                base = env["xbase_d"]
            else:
                base = env["xin"]
            emit_ln(P, env, base, write_xbase=(not with_oproj))
            emit_gateup(P, env)
            emit_down(P, env)
            if with_proj:
                emit_ln(P, env, env["xout"], write_xbase=False)
                emit_projections(P, env)
            for s_ in ("xo0", "xo1", "xbw", "pw0", "pw1", "wdw", "htw", "pw"):
                P.wait("sync", s_)

        run_spmd_program(nc, block, handles, body)
    return nc


def emit_prep(P, env):
    ones1, onescol, ident, indg = env["ones1"], env["onescol"], env["ident"], env["indg"]
    xld = env["xld"]
    CH = HID // 16

    P.gp(lambda g: g.memset(ones1[:], 1.0))
    P.gp(lambda g: g.memset(onescol[:], 1.0))
    P.gp(lambda g: g.memset(ident[:], 1.0))
    P.gp(lambda g: g.affine_select(
        ident[:], ident[:], pattern=[[1, 128]], compare_op=ALU.is_equal,
        fill=0.0, channel_multiplier=-1))
    rowsel, indone = env["rowsel"], env["indone"]
    P.gp(lambda g: g.memset(rowsel[:], 0.0))
    P.gp(lambda g: g.memset(indone[:], 1.0))
    for base_r in (0, 32, 64):
        P.gp(lambda g, base_r=base_r: g.affine_select(
            rowsel[:], rowsel[:], pattern=[[-1, GI]], compare_op=ALU.not_equal,
            fill=1.0, base=-base_r, channel_multiplier=1))
    tsel = P.do("gpsimd", lambda g: g.nop(), sem="gp")
    P.wait("vector", "gp", tsel)
    for gi in range(GI):
        P.do("vector", lambda v, gi=gi: v.tensor_scalar(
            indg[:, gi * 128:(gi + 1) * 128], indone[:],
            rowsel[:, gi:gi + 1], None, ALU.mult), sem="pre")

    items = []
    for z_, s_, out_ in ((env["zg"], env["sg"], env["zsg_d"]),
                         (env["zu"], env["su"], env["zsu_d"])):
        for c in range(16):
            items.append((z_, s_, out_, slice(c * CH, (c + 1) * CH)))
    ztk, stk, dvt, wtk = {}, {}, {}, {}

    def prow(i):
        return slice(32 * (i % 2), 32 * (i % 2) + GI)

    def pload(i):
        z_, s_, out_, sl = items[i]
        r = prow(i)
        if i >= 2:
            P.wait("gpsimd", "dv", dvt[i - 2])
            P.wait("sync", "dv", dvt[i - 2])
            P.wait("gpsimd", wtk[i - 2])
            P.wait("sync", wtk[i - 2])
        ztk[i] = P.dma("gpsimd", xld[r, 0:CH], z_[:, sl], sem="pz%d" % (i % 2))
        stk[i] = P.dma("sync", xld[r, CH:2 * CH], s_[:, sl], sem="ps%d" % (i % 2))

    pload(0)
    pload(1)
    prev = None
    for i, (z_, s_, out_, sl) in enumerate(items):
        r = prow(i)
        P.wait("vector", ztk[i])
        P.wait("vector", stk[i])
        tv = P.do("vector", lambda v, r=r: v.scalar_tensor_tensor(
            xld[r, 2 * CH:3 * CH], xld[r, 0:CH], -1.0,
            xld[r, CH:2 * CH], ALU.mult, ALU.mult), sem="dv")
        dvt[i] = tv
        P.wait("gpsimd", "dv", tv)
        prev = P.dma("gpsimd", out_[:, sl], xld[r, 2 * CH:3 * CH],
                     sem="pw%d" % (i % 2) if False else "pwa" if i % 2 == 0 else "pwb")
        wtk[i] = prev
        if i + 2 < len(items):
            pload(i + 2)
    P.wait("gpsimd", wtk[len(items) - 2])
    P.wait("sync", wtk[len(items) - 2])
    P.wait("gpsimd", wtk[len(items) - 1])
    P.wait("sync", wtk[len(items) - 1])
    P.wait("sync", "dv", dvt[len(items) - 1])
    P.wait("gpsimd", "dv", dvt[len(items) - 1])
    CH2 = D // 2
    for c in range(2):
        sl = slice(c * CH2, (c + 1) * CH2)
        P.wait("gpsimd", prev)
        P.wait("sync", prev)
        tz = P.dma("gpsimd", xld[0:86, 0:CH2], env["zd"][:, sl], sem="pz")
        ts = P.dma("sync", xld[0:86, CH2:2 * CH2], env["sd"][:, sl], sem="ps")
        P.wait("vector", tz)
        P.wait("vector", ts)
        t1 = P.do("vector", lambda v: v.tensor_scalar_mul(
            xld[0:86, CH2:2 * CH2], xld[0:86, CH2:2 * CH2], 0.5), sem="dv")
        P.wait("vector", "dv", t1)
        t2 = P.do("vector", lambda v: v.scalar_tensor_tensor(
            xld[0:86, 0:CH2], xld[0:86, 0:CH2], -1.0,
            xld[0:86, CH2:2 * CH2], ALU.mult, ALU.mult), sem="dv")
        P.wait("gpsimd", "dv", t2)
        P.dma("gpsimd", env["zsd_d"][:, sl], xld[0:86, 0:CH2], sem="pw")
        prev = P.dma("gpsimd", env["sdeff_d"][:, sl], xld[0:86, CH2:2 * CH2], sem="pw")
    P.wait("sync", prev)
    env["_prep_tok"] = P.dma("sync", env["sdef_sb"][:, :], env["sdeff_d"][:, :], sem="ps")


def emit_ln(P, env, src_d, write_xbase):
    xld, xn, lnst = env["xld"], env["xn_sb"], env["lnst"]
    xs_sb, xsT, xnT = env["xs_sb"], env["xsT"], env["xnT"]
    xsf = env["xsf"]
    ident, bank = env["ident"], env["bank"]
    s1, s2, m, var, sd_, rstd, bias = (lnst[:, i:i + 1] for i in range(7))

    P.wait("sync", "xo0", P.tok("xo0"))
    P.wait("sync", "xo1", P.tok("xo1"))
    P.wait("sync", "ac", P.tok("ac"))
    P.wait("tensor", "dv", P.tok("dv"))
    P.wait("tensor", "ac", P.tok("ac"))

    tr_ev = []
    for tt in range(NTT):
        sl = slice(tt * 128, (tt + 1) * 128)
        if tt >= 1:
            P.wait("sync", "ac", env["_xld_free"])
        tld = P.dma("sync", xld[:, :], src_d[sl, :], sem="xli")
        if write_xbase:
            P.dma("sync", env["xbase_d"][sl, :], src_d[sl, :], sem="xbw")
        P.wait("scalar", tld)
        if tr_ev:
            P.wait("scalar", "pe", env["_ln_trmm"])
            P.wait("scalar", "dv", env["_xsf_cp"])
        ts1 = P.do("scalar", lambda s: s.activation(
            out=xn[:, :], in_=xld[:, :], func=AF.Copy, accum_out=s1), sem="ac")
        P.wait("scalar", "ac", ts1)
        ts2 = P.do("scalar", lambda s: s.activation(
            out=xn[:, :], in_=xld[:, :], func=AF.Square, accum_out=s2), sem="ac")
        P.wait("vector", "ac", ts2)
        P.do("vector", lambda v: v.tensor_scalar(m, s1, 1.0 / D, None, ALU.mult), sem="dv")
        t_mv = P.do("vector", lambda v: v.tensor_scalar(var, s2, 1.0 / D, EPS, ALU.mult, ALU.add), sem="dv")
        P.wait("vector", "dv", t_mv)
        tm = P.do("vector", lambda v: v.scalar_tensor_tensor(
            s1, s1, -1.0 / (float(D) * float(D)), s1, ALU.mult, ALU.mult), sem="dv")
        P.wait("vector", "dv", tm)
        tv = P.do("vector", lambda v: v.tensor_add(var, var, s1), sem="dv")
        P.wait("scalar", "dv", tv)
        tsq = P.do("scalar", lambda s: s.activation(out=sd_, in_=var, func=AF.Sqrt), sem="ac")
        P.wait("vector", "ac", tsq)
        tr = P.do("vector", lambda v: v.reciprocal(rstd, sd_), sem="dv")
        P.wait("vector", "dv", tr)
        tb = P.do("vector", lambda v: v.scalar_tensor_tensor(
            bias, m, -1.0, rstd, ALU.mult, ALU.mult), sem="dv")
        P.wait("scalar", "dv", tb)
        P.wait("scalar", "ac", ts2)
        txn = P.do("scalar", lambda s: s.activation(
            out=xn[:, :], in_=xld[:, :], func=AF.Identity,
            scale=rstd, bias=bias), sem="ac")
        env["_xld_free"] = txn
        P.wait("vector", "ac", txn)
        if "_xsf_cp" in env:
            P.wait("vector", "dv", env["_xsf_cp"])
        trd = None
        for gi in range(GI):
            trd = P.do("vector", lambda v, gi=gi: v.reduce_sum(
                xsf[:, gi:gi + 1],
                xn[:, gi * 128:(gi + 1) * 128], axis=AX.X), sem="dv")
        P.wait("vector", "dv", trd)
        env["_xsf_cp"] = P.do("vector", lambda v, tt=tt: v.tensor_copy(
            xs_sb[:, tt * GI:(tt + 1) * GI], xsf[:, :]), sem="dv")
        P.wait("tensor", "ac", txn)
        for gi in range(GI):
            ps = bank[gi % 4]
            if len(tr_ev) >= 4:
                P.wait("tensor", "ac", tr_ev[-4])
            tp = P.do("tensor", lambda t, gi=gi, ps=ps: t.matmul(
                ps[:, 0:128], xn[:, gi * 128:(gi + 1) * 128],
                ident[:], start=True, stop=True), sem="pe")
            if gi == GI - 1:
                env["_ln_trmm"] = tp
            P.wait("scalar", "pe", tp)
            te = P.do("scalar", lambda s, gi=gi, tt=tt, ps=ps: s.activation(
                out=xnT[:, gi * TOK + tt * 128: gi * TOK + (tt + 1) * 128],
                in_=ps[:, 0:128], func=AF.Copy), sem="ac")
            tr_ev.append(te)
    P.wait("tensor", "dv", P.tok("dv"))
    P.wait("tensor", "ac", tr_ev[-1])
    xst_dv = {}
    for tc in range(NTT):
        ps = bank[tc % 2]
        if tc >= 2:
            P.wait("tensor", "dv", xst_dv[tc - 2])
        tp = P.do("tensor", lambda t, tc=tc, ps=ps: t.matmul(
            ps[0:GI, 0:128], xs_sb[:, tc * GI:(tc + 1) * GI], ident[:],
            start=True, stop=True), sem="pe")
        P.wait("vector", "pe", tp)
        xst_dv[tc] = P.do("vector", lambda v, tc=tc, ps=ps: v.tensor_copy(
            xsT[:, tc * 128:(tc + 1) * 128], ps[0:GI, 0:128]),
            sem=("dv" if tc < NTT - 1 else "ln"))
    env["_ln_done"] = xst_dv[NTT - 1]
    env["_lnxst_dv"] = P.tok("dv")


def emit_gateup(P, env):
    wg_sb, wu_sb, scb, zscb = env["wg_sb"], env["wu_sb"], env["scb"], env["zscb"]
    qg_i8, qu_i8 = env["qg_i8"], env["qu_i8"]
    xnT, xsT, indg = env["xnT"], env["xsT"], env["indg"]
    gsil, hct, hs_sb = env["gsil"], env["hct"], env["hs_sb"]
    bank = env["bank"]
    onescol, ident, hsT = env["onescol"], env["ident"], env["hsT"]
    ht_d, wd_d, wdq = env["ht_d"], env["wd_d"], env["wdq"]
    sdef_sb = env["sdef_sb"]

    P.wait("tensor", "ln", env["_ln_done"])
    P.wait("gpsimd", "ln", env["_ln_done"])
    P.wait("sync", "ln", env["_ln_done"])
    P.wait("tensor", env["_prep_tok"])
    P.wait("tensor", "ac", P.tok("ac"))
    P.wait("vector", "pe", P.tok("pe"))

    qwg_v = env["qwg"].rearrange("(g p) c -> p g c", p=128)
    qwu_v = env["qwu"].rearrange("(g p) c -> p g c", p=128)
    qwd_v = env["qwd"].rearrange("(ct p) c -> ct p c", p=128)

    wtok, utok, dtok = {}, {}, {}
    ht_wtok = [None] * GO
    wd_wtok = [None] * GO
    env["_ht_wtok"] = ht_wtok
    env["_wd_wtok"] = wd_wtok

    dqend = {}
    bc_pe = {}

    def load_cb(mat, cb):
        src = qwg_v if mat == 0 else qwu_v
        dst = qg_i8 if mat == 0 else qu_i8
        s_in = env["sg"] if mat == 0 else env["su"]
        zs_in = env["zsg_d"] if mat == 0 else env["zsu_d"]
        bb = cb % 2
        eng = "sync" if mat == 0 else "scalar"
        if (mat, cb - 2) in dqend:
            P.wait(eng, "dv", dqend[(mat, cb - 2)])
        if (mat, cb - 1) in bc_pe:
            P.wait("gpsimd", "pe", bc_pe[(mat, cb - 1)])
        if (mat, cb - 2) in utok:
            P.wait("sync", "pe", utok[(mat, cb - 2)])
        if env.get("_zs_prev") is not None:
            P.wait("sync", env["_zs_prev"])
        sl = slice(cb * 256, (cb + 1) * 256)
        t1 = P.dma(eng, dst[:, bb, :].rearrange("p (g c) -> p g c", c=256),
                   src[:, :, sl], sem=("qg%d" % bb if mat == 0 else "qu%d" % bb))
        t2 = P.dma("gpsimd", scb[:, mat * 256:(mat + 1) * 256], s_in[:, sl],
                   sem=("scg" if mat == 0 else "scu"))
        t3 = P.dma("sync", zscb[:, cb % 2, mat * 256:(mat + 1) * 256],
                   zs_in[:, sl], sem=("zsgq" if mat == 0 else "zsuq"))
        env["_zs_prev"] = t3
        wtok[(mat, cb)] = (t1, t2, t3)

    def load_dct(ct):
        if ct >= 1 and wd_wtok[ct - 1] is not None:
            P.wait("gpsimd", wd_wtok[ct - 1])
        dtok[ct] = P.dma("gpsimd", wdq[:, :], qwd_v[ct, :, :], sem="wdqi")

    bc_state = [0, None, None]

    def bcast(lhs_lo, lhs_n, lhs_col, rhs, ncol):
        i = bc_state[0] % 2
        bc_state[0] += 1
        bkx = bank[4 + i]
        if bc_state[1 + i] is not None:
            P.wait("tensor", "dv", bc_state[1 + i])
        tp = P.do("tensor", lambda t: t.matmul(
            bkx[:, 0:ncol], indg[lhs_lo:lhs_lo + lhs_n,
                                 lhs_col:lhs_col + 128], rhs,
            start=True, stop=True), sem="pe")
        return i, bkx, tp

    def bcast_done(i, tok):
        bc_state[1 + i] = tok

    def dq(mat, cb):
        Wt = wg_sb if mat == 0 else wu_sb
        src8 = (qg_i8 if mat == 0 else qu_i8)[:, cb % 2, :]
        t1, t2, t3 = wtok[(mat, cb)]
        P.wait("tensor", t2)
        P.wait("vector", t1)
        if (mat, cb - 1) in utok:
            P.wait("vector", "pe", utok[(mat, cb - 1)])
        for gp in range(GI // 2):
            gi0, gi1 = 2 * gp, 2 * gp + 1
            i = bc_state[0] % 2
            bc_state[0] += 1
            bkx = bank[4 + i]
            if bc_state[1 + i] is not None:
                P.wait("tensor", "dv", bc_state[1 + i])
            P.do("tensor", lambda t, gi0=gi0, bkx=bkx, mat=mat: t.matmul(
                bkx[:, 0:256], indg[0:GI, gi0 * 128:gi0 * 128 + 128],
                scb[:, mat * 256:(mat + 1) * 256],
                start=True, stop=True), sem="pe")
            tp = P.do("tensor", lambda t, gi1=gi1, bkx=bkx, mat=mat: t.matmul(
                bkx[:, 256:512], indg[0:GI, gi1 * 128:gi1 * 128 + 128],
                scb[:, mat * 256:(mat + 1) * 256],
                start=True, stop=True), sem="pe")
            P.wait("vector", "pe", tp)
            tv = P.do("vector", lambda v, gp=gp, Wt=Wt, src8=src8, bkx=bkx:
                      v.tensor_mul(
                          Wt[:, gp * 512:(gp + 1) * 512],
                          src8[:, gp * 512:(gp + 1) * 512],
                          bkx[:, 0:512]), sem="dv")
            bc_state[1 + i] = tv
        dqend[(mat, cb)] = P.tok("dv")
        bc_pe[(mat, cb)] = P.tok("pe")

    load_cb(0, 0)
    load_cb(1, 0)
    load_dct(0)
    dq(0, 0)
    dq(1, 0)
    load_cb(0, 1)
    load_cb(1, 1)

    hs_ev = {}
    acc_free = {}
    hct_free = None
    hs_pe_tok = None

    for cb in range(NCB):
        env["_dqg"] = dqend[(0, cb)]
        env["_dqu"] = dqend[(1, cb)]
        for ct in range(2):
            gct = 2 * cb + ct
            W0 = wg_sb[:, :]
            W1 = wu_sb[:, :]
            _, _, t3g = wtok[(0, cb)]
            _, _, t3u = wtok[(1, cb)]
            P.wait("tensor", t3g)
            P.wait("tensor", t3u)
            P.wait("tensor", "dv", env["_dqg"])
            P.wait("tensor", "dv", env["_dqu"])
            for tb in range(2):
                bk = bank[tb]
                for gi in range(GI):
                    if gi == 0 and (0, tb) in acc_free:
                        P.wait("tensor", "ac", acc_free[(0, tb)])
                    P.do("tensor", lambda t, gi=gi, ct=ct, tb=tb, bk=bk, W0=W0: t.matmul(
                        bk[:], W0[:, gi * 256 + ct * 128: gi * 256 + ct * 128 + 128],
                        xnT[:, gi * TOK + tb * 512: gi * TOK + (tb + 1) * 512],
                        start=(gi == 0), stop=False), sem="pe")
                tzg = P.do("tensor", lambda t, ct=ct, tb=tb, bk=bk, cb=cb: t.matmul(
                    bk[:], zscb[0:GI, cb % 2, ct * 128: ct * 128 + 128],
                    xsT[:, tb * 512:(tb + 1) * 512],
                    start=False, stop=True), sem="pe")
                env["_lastg_pe"] = tzg
                P.wait("scalar", "pe", tzg)
                tsl = P.do("scalar", lambda s, tb=tb, bk=bk: s.activation(
                    out=gsil[:, tb * 512:(tb + 1) * 512], in_=bk[:],
                    func=AF.Silu), sem="ac")
                acc_free[(0, tb)] = tsl
            for tb in range(2):
                bk = bank[2 + tb]
                for gi in range(GI):
                    if gi == 0 and (1, tb) in acc_free:
                        P.wait("tensor", "dv", acc_free[(1, tb)])
                    P.do("tensor", lambda t, gi=gi, ct=ct, tb=tb, bk=bk, W1=W1: t.matmul(
                        bk[:], W1[:, gi * 256 + ct * 128: gi * 256 + ct * 128 + 128],
                        xnT[:, gi * TOK + tb * 512: gi * TOK + (tb + 1) * 512],
                        start=(gi == 0), stop=False), sem="pe")
                tzu = P.do("tensor", lambda t, ct=ct, tb=tb, bk=bk, cb=cb: t.matmul(
                    bk[:], zscb[0:GI, cb % 2, 256 + ct * 128: 256 + ct * 128 + 128],
                    xsT[:, tb * 512:(tb + 1) * 512],
                    start=False, stop=True), sem="pe")
                env["_lastu_pe"] = tzu
                P.wait("vector", "pe", tzu)
                P.wait("vector", "ac", acc_free[(0, tb)])
                if hct_free is not None:
                    P.wait("vector", hct_free)
                if hs_pe_tok is not None:
                    P.wait("vector", "pe", hs_pe_tok)
                th = P.do("vector", lambda v, tb=tb, bk=bk: v.tensor_mul(
                    hct[:, tb * 512:(tb + 1) * 512], bk[:],
                    gsil[:, tb * 512:(tb + 1) * 512]), sem="dv")
                acc_free[(1, tb)] = th
            th_all = P.tok("dv")
            hb = bank[6 + gct % 2]
            P.wait("tensor", "dv", th_all)
            if gct - 2 in hs_ev:
                P.wait("tensor", "dv", hs_ev[gct - 2])
            for tc in range(NTT):
                P.do("tensor", lambda t, tc=tc, hb=hb: t.matmul(
                    hb[:, tc:tc + 1], hct[:, tc * 128:(tc + 1) * 128],
                    onescol[:], start=True, stop=True), sem="pe")
            hs_pe_tok = P.tok("pe")
            P.wait("vector", "pe", hs_pe_tok)
            tev = P.do("vector", lambda v, gct=gct, hb=hb: v.tensor_copy(
                hs_sb[:, gct * NTT:(gct + 1) * NTT], hb[:, 0:NTT]), sem="dv")
            hs_ev[gct] = tev
            P.wait("scalar", "dv", th_all)
            hct_free = P.dma("scalar", ht_d[gct, :, :], hct[:, :], sem="htw")
            ht_wtok[gct] = hct_free
            ct2 = gct
            P.wait("vector", dtok[ct2])
            base = (ct2 // 32) * 32
            nrow = min(32, GO - base)
            last_dq = None
            for mb in range(8):
                i, bkx, tp = bcast(base, nrow, (ct2 - base) * 128,
                                   sdef_sb[base:base + nrow, mb * 512:(mb + 1) * 512],
                                   512)
                P.wait("vector", "pe", tp)
                tv = P.do("vector", lambda v, mb=mb, bkx=bkx: v.tensor_mul(
                    wdq[:, mb * 512:(mb + 1) * 512], wdq[:, mb * 512:(mb + 1) * 512],
                    bkx[:, 0:512]), sem="dv")
                bcast_done(i, tv)
                last_dq = tv
            P.wait("scalar", "dv", last_dq)
            wd_wtok[ct2] = P.dma("scalar", wd_d[ct2, :, :], wdq[:, :], sem="wdw")
            if ct2 + 1 < GO:
                load_dct(ct2 + 1)
        utok[(0, cb)] = env["_lastg_pe"]
        utok[(1, cb)] = env["_lastu_pe"]
        if cb + 1 < NCB:
            dq(0, cb + 1)
            dq(1, cb + 1)
        if cb + 2 < NCB:
            load_cb(0, cb + 2)
            load_cb(1, cb + 2)
    P.wait("tensor", "dv", P.tok("dv"))
    hst_dv = {}
    for tc in range(NTT):
        ps = bank[tc % 2]
        if tc >= 2:
            P.wait("tensor", "dv", hst_dv[tc - 2])
        tp = P.do("tensor", lambda t, tc=tc, ps=ps: t.matmul(
            ps[0:GO, 0:128], hs_sb.rearrange("p (ct tc) -> p tc ct", tc=NTT)[:, tc, :],
            ident[:], start=True, stop=True), sem="pe")
        P.wait("vector", "pe", tp)
        hst_dv[tc] = P.do("vector", lambda v, tc=tc, ps=ps: v.tensor_copy(
            hsT[:, tc * 128:(tc + 1) * 128], ps[0:GO, 0:128]),
            sem=("dv" if tc < NTT - 1 else "ln"))
    env["_gu_done"] = hst_dv[NTT - 1]


def emit_down(P, env):
    htin, wdt, xacc = env["htin"], env["wdt"], env["xacc"]
    zsdmb, hsT = env["zsdmb"], env["hsT"]
    bank = env["bank"]
    ht_d, wd_d, xbase_d, xout = env["ht_d"], env["wd_d"], env["xbase_d"], env["xout"]
    ht_wtok, wd_wtok = env["_ht_wtok"], env["_wd_wtok"]

    P.wait("tensor", "ln", env["_gu_done"])
    P.wait("vector", "ln", env["_gu_done"])
    P.wait("sync", "xbw", P.tok("xbw"))

    htok, wtok2 = {}, {}
    dn_use = {}
    env["_dn_use"] = dn_use
    wg_sb, wu_sb, xnT = env["wg_sb"], env["wu_sb"], env["xnT"]
    chunks = []
    c0 = 0
    while c0 < GO:
        chunks.append((c0, min(4, GO - c0)))
        c0 += 4
    NCH = len(chunks)

    def ht4(s):
        base = wg_sb if s < 2 else wu_sb
        return base[:, (s % 2) * 4 * TOK:(s % 2 + 1) * 4 * TOK] \
            .rearrange("p (b c) -> p b c", c=TOK)

    def wd4(s):
        return xnT[:, s * 4 * 512:(s + 1) * 4 * 512] \
            .rearrange("p (b c) -> p b c", c=512)

    P.wait("scalar", "pe", P.tok("pe"))
    P.wait("sync", "pe", P.tok("pe"))

    def load_chunk(k):
        mb, kc = divmod(k, NCH)
        ct0, nct = chunks[kc]
        s = k % 4
        if k >= 4:
            P.wait("scalar", "pe", dn_use[k - 4])
            P.wait("sync", "pe", dn_use[k - 4])
        P.wait("scalar", ht_wtok[ct0 + nct - 1])
        P.wait("sync", wd_wtok[ct0 + nct - 1])
        htok[k] = P.dma(
            "scalar", ht4(s)[:, 0:nct, :],
            ht_d[ct0:ct0 + nct, :, :].rearrange("ct p t -> p ct t"),
            sem="ht%d" % s)
        wtok2[k] = P.dma(
            "sync", wd4(s)[:, 0:nct, :],
            wd_d[ct0:ct0 + nct, :, mb * 512:(mb + 1) * 512]
            .rearrange("ct p t -> p ct t"),
            sem="wd%d" % s)

    for _k in range(4):
        load_chunk(_k)
    zt_prev = None
    evac_toks = [None] * NTT
    xw_tok = {}
    for mb in range(8):
        if zt_prev is not None:
            P.wait("sync", "pe", zt_prev)
        tz = P.dma("sync", zsdmb[:, :], env["zsd_d"][:, mb * 512:(mb + 1) * 512],
                   sem="zsm")
        for kc in range(NCH):
            k = mb * NCH + kc
            ct0, nct = chunks[kc]
            s = k % 4
            P.wait("tensor", htok[k])
            P.wait("tensor", wtok2[k])
            for j in range(nct):
                for tt in range(NTT):
                    if ct0 + j == 0 and evac_toks[tt] is not None:
                        P.wait("tensor", "dv", evac_toks[tt])
                    P.do("tensor", lambda t, tt=tt, j=j, s=s: t.matmul(
                        bank[tt][:], ht4(s)[:, j, tt * 128:(tt + 1) * 128],
                        wd4(s)[:, j, :], start=(ct0 + j == 0), stop=False),
                        sem="pe")
            dn_use[k] = P.tok("pe")
            if k + 4 < 8 * NCH:
                load_chunk(k + 4)
        P.wait("tensor", tz)
        for tt in range(NTT):
            tzc = P.do("tensor", lambda t, tt=tt: t.matmul(
                bank[tt][:], hsT[:, tt * 128:(tt + 1) * 128], zsdmb[:, :],
                start=False, stop=True), sem="pe")
            if tt == NTT - 1:
                zt_prev = tzc
            if (tt % 2) in xw_tok:
                P.wait("sync", xw_tok[tt % 2])
            txa = P.dma("sync", xacc[:, tt % 2, :],
                        xbase_d[tt * 128:(tt + 1) * 128, mb * 512:(mb + 1) * 512],
                        sem="xa%d" % (tt % 2))
            P.wait("vector", "pe", tzc)
            P.wait("vector", txa)
            ta = P.do("vector", lambda v, tt=tt: v.tensor_add(
                xacc[:, tt % 2, :], xacc[:, tt % 2, :], bank[tt][:]), sem="dv")
            evac_toks[tt] = ta
            P.wait("sync", "dv", ta)
            xw_tok[tt % 2] = P.dma(
                "sync", xout[tt * 128:(tt + 1) * 128, mb * 512:(mb + 1) * 512],
                xacc[:, tt % 2, :], sem="xo%d" % (tt % 2))
    env["_down_done"] = P.tok("dv")


def emit_projections(P, env):
    wg_sb, wu_sb = env["wg_sb"], env["wu_sb"]
    xnT, pstg = env["xnT"], env["pstg"]
    bank = env["bank"]

    jobs = [(env["wq"], env["qt"], D, QSC),
            (env["wk"], env["kt"], NKV * HD, 1.0),
            (env["wv"], env["vt"], NKV * HD, 1.0)]

    P.wait("gpsimd", "dv", env["_down_done"])
    P.wait("gpsimd", "pe", P.tok("pe"))
    P.wait("tensor", "ln", env["_ln_done"])
    P.wait("tensor", "ac", P.tok("ac"))
    P.wait("scalar", "dv", env["_down_done"])

    seq = []
    for (w_in, out_d, ncols, scale) in jobs:
        w_v = w_in.rearrange("(g p) c -> p g c", p=128)
        for db in range(ncols // 256):
            seq.append((w_v, out_d, db, scale))

    ltoks, used, bufs = {}, {}, {}

    def load_j(j):
        w_v, out_d, db, scale = seq[j]
        buf = wg_sb[:, :] if j % 2 == 0 else wu_sb[:, :]
        if (j - 2) in used:
            P.wait("gpsimd", "pe", used[j - 2])
        sl = slice(db * 256, (db + 1) * 256)
        ltoks[j] = P.dma("gpsimd", buf.rearrange("p (g c) -> p g c", c=256),
                         w_v[:, :, sl], sem="pj%d" % (j % 2))
        return buf

    for j in range(min(2, len(seq))):
        bufs[j] = load_j(j)
    ev_free = {}
    pj_w = {}
    for j in range(len(seq)):
        w_v, out_d, db, scale = seq[j]
        buf = bufs[j]
        P.wait("tensor", ltoks[j])
        for ct in range(2):
            for tb in range(2):
                bki = ct * 2 + tb
                bk = bank[bki]
                if bki in ev_free:
                    P.wait("tensor", "ac", ev_free[bki])
                for gi in range(GI):
                    P.do("tensor", lambda t, gi=gi, ct=ct, tb=tb, bk=bk, buf=buf: t.matmul(
                        bk[:], buf[:, gi * 256 + ct * 128: gi * 256 + ct * 128 + 128],
                        xnT[:, gi * TOK + tb * 512: gi * TOK + (tb + 1) * 512],
                        start=(gi == 0), stop=(gi == GI - 1)), sem="pe")
                tmm = P.tok("pe")
                P.wait("scalar", "pe", tmm)
                if (bki % 2) in pj_w:
                    P.wait("scalar", pj_w[bki % 2])
                tev = P.do("scalar", lambda s, bk=bk, bki=bki, scale=scale: s.activation(
                    out=pstg[:, bki % 2, :], in_=bk[:], func=AF.Copy, scale=scale),
                    sem="ac")
                ev_free[bki] = tev
                P.wait("scalar", "ac", tev)
                r0 = db * 256 + ct * 128
                pj_w[bki % 2] = P.dma(
                    "scalar", out_d[r0:r0 + 128, tb * 512:(tb + 1) * 512],
                    pstg[:, bki % 2, :], sem="pw%d" % (bki % 2))
        used[j] = P.tok("pe")
        if j + 2 < len(seq):
            bufs[j + 2] = load_j(j + 2)


def emit_oproj(P, env):
    wg_sb, wu_sb = env["wg_sb"], env["wu_sb"]
    xnT, xacc, xld = env["xnT"], env["xacc"], env["xld"]
    bank = env["bank"]
    attnT, wo, xin, xbase_d = env["attnT"], env["wo"], env["xin"], env["xbase_d"]

    at_v = attnT.rearrange("(g p) c -> p g c", p=128)
    t_at = P.dma("sync", xnT.rearrange("p (g c) -> p g c", c=TOK), at_v, sem="ati")
    wo_v = wo.rearrange("(g p) c -> p g c", p=128)

    P.wait("sync", "dv", P.tok("dv"))
    P.wait("scalar", "dv", P.tok("dv"))
    for s_ in ("pwa", "pwb", "pw"):
        P.wait("sync", s_, P.tok(s_))
        P.wait("scalar", s_, P.tok(s_))

    ldt, castt, wuse = {}, {}, {}
    NQ = 4 * (D // 256)

    def stage(q):
        mbq, qh = divmod(q, 4)
        eng = "sync" if q % 2 == 0 else "scalar"
        if q >= 2:
            P.wait(eng, "ac", castt[q - 2])
        sl = slice(mbq * 256, (mbq + 1) * 256)
        ldt[q] = P.dma(
            eng, xld[:, (q % 2) * 2048:(q % 2) * 2048 + 2048]
            .rearrange("p (g c) -> p g c", c=256),
            wo_v[:, qh * 8:(qh + 1) * 8, sl], sem="pj%d" % (q % 4))

    def docast(q):
        mbq, qh = divmod(q, 4)
        buf = wg_sb[:, :] if mbq % 2 == 0 else wu_sb[:, :]
        P.wait("scalar", ldt[q])
        if qh == 0 and (mbq - 2) in wuse:
            P.wait("scalar", "pe", wuse[mbq - 2])
        castt[q] = P.do("scalar", lambda s, buf=buf, qh=qh, q=q: s.activation(
            out=buf[:, qh * 2048:(qh + 1) * 2048],
            in_=xld[:, (q % 2) * 2048:(q % 2) * 2048 + 2048],
            func=AF.Copy), sem="ac")

    stage(0)
    stage(1)
    P.wait("tensor", t_at)
    evd = {}
    xw = {}
    for mbq in range(D // 256):
        buf = wg_sb[:, :] if mbq % 2 == 0 else wu_sb[:, :]
        for q in range(4 * mbq, 4 * mbq + 4):
            docast(q)
            if q + 2 < NQ:
                stage(q + 2)
        P.wait("tensor", "ac", castt[4 * mbq + 3])
        for gi in range(GI):
            for tt in range(NTT):
                if gi == 0 and tt in evd:
                    P.wait("tensor", "dv", evd[tt])
                P.do("tensor", lambda t, gi=gi, tt=tt, buf=buf: t.matmul(
                    bank[tt][:, 0:256],
                    xnT[:, gi * TOK + tt * 128: gi * TOK + (tt + 1) * 128],
                    buf[:, gi * 256:(gi + 1) * 256],
                    start=(gi == 0), stop=(gi == GI - 1)), sem="pe")
        tmm = P.tok("pe")
        for tt in range(NTT):
            if (tt % 2) in xw:
                P.wait("sync", xw[tt % 2])
            txa = P.dma("sync", xacc[:, tt % 2, 0:256],
                        xin[tt * 128:(tt + 1) * 128, mbq * 256:(mbq + 1) * 256],
                        sem="xa%d" % (tt % 2))
            P.wait("vector", "pe", tmm)
            P.wait("vector", txa)
            ta = P.do("vector", lambda v, tt=tt: v.tensor_add(
                xacc[:, tt % 2, 0:256], xacc[:, tt % 2, 0:256],
                bank[tt][:, 0:256]), sem="dv")
            evd[tt] = ta
            P.wait("sync", "dv", ta)
            xw[tt % 2] = P.dma(
                "sync", xbase_d[tt * 128:(tt + 1) * 128, mbq * 256:(mbq + 1) * 256],
                xacc[:, tt % 2, 0:256], sem="xo%d" % (tt % 2))
        wuse[mbq] = P.tok("pe")
    env["_op_done"] = P.do("vector", lambda v: v.nop(), sem="ln")



def build_attention():
    nc = bass.Bass("TRN2", target_bir_lowering=False, debug=False)
    ap = lambda t: t.ap()
    qtp = ap(nc.dram_tensor("qtp", [2, 4 * HD, S], BF16, kind="ExternalInput"))
    ktp = ap(nc.dram_tensor("ktp", [2, HD, S], BF16, kind="ExternalInput"))
    vtp = ap(nc.dram_tensor("vtp", [2, HD, S], BF16, kind="ExternalInput"))
    ao = ap(nc.dram_tensor("ao", [2 * 4 * HD, S], BF16, kind="ExternalOutput"))

    import contextlib
    with contextlib.ExitStack() as _es:
        _e = _es.enter_context
        qts = _e(nc.sbuf_tensor("qts", [128, 4 * S], BF16))
        kts = _e(nc.sbuf_tensor("kts", [128, S], BF16))
        vtq = _e(nc.sbuf_tensor("vtq", [128, S], BF16))
        vts = _e(nc.sbuf_tensor("vts", [128, S], BF16))
        et = _e(nc.sbuf_tensor("et", [128, 4, 512], BF16))
        eacc = _e(nc.sbuf_tensor("eacc", [128, 512], F32))
        eaccb = _e(nc.sbuf_tensor("eaccb", [128, 512], BF16))
        lrec = _e(nc.sbuf_tensor("lrec", [128, 512], F32))
        masks = _e(nc.sbuf_tensor("masks", [128, 4 * 512], BF16))
        ones128 = _e(nc.sbuf_tensor("ones128", [128, 128], BF16))
        ident = _e(nc.sbuf_tensor("ident", [128, 128], BF16))
        astg = _e(nc.sbuf_tensor("astg", [128, 2, 512], BF16))
        expb = _e(nc.sbuf_tensor("expb", [128, 1], F32))
        bk0 = _e(nc.psum_tensor("bk0", [128, 512], F32))
        bk1 = _e(nc.psum_tensor("bk1", [128, 512], F32))
        bk2 = _e(nc.psum_tensor("bk2", [128, 512], F32))
        bk3 = _e(nc.psum_tensor("bk3", [128, 512], F32))
        bk4 = _e(nc.psum_tensor("bk4", [128, 512], F32))
        bk5 = _e(nc.psum_tensor("bk5", [128, 512], F32))
        semnames = ["pe", "dv", "ac", "use", "gp", "qld", "kld", "vld",
                    "ao0", "ao1"]
        handles = {n: _e(nc.semaphore(n)) for n in semnames}
        block = _e(nc.Block())
        bank = [bk0, bk1, bk2, bk3, bk4, bk5]

        def body(P):
            P.gp(lambda g: g.memset(ones128[:], 1.0), sem="use")
            P.gp(lambda g: g.memset(expb[:], EXPB), sem="use")
            P.gp(lambda g: g.memset(ident[:], 1.0), sem="use")
            P.gp(lambda g: g.affine_select(
                ident[:], ident[:], pattern=[[1, 128]], compare_op=ALU.is_equal,
                fill=0.0, channel_multiplier=-1), sem="use")
            P.gp(lambda g: g.memset(masks[:], 1.0), sem="use")
            tmask = None
            for j in range(4):
                tmask = P.gp(lambda g, j=j: g.affine_select(
                    masks[:, j * 512:(j + 1) * 512], masks[:, j * 512:(j + 1) * 512],
                    pattern=[[1, 512]], compare_op=ALU.is_ge, fill=0.0,
                    base=-128 * j, channel_multiplier=-1), sem="use")
            ao_last = {}
            for pp in range(2):
                if pp == 1:
                    P.wait("sync", "ao0", P.tok("ao0"))
                    P.wait("sync", "ao1", P.tok("ao1"))
                    P.wait("sync", "ac", P.tok("ac"))
                    P.wait("sync", "pe", P.tok("pe"))
                tq = P.dma("sync", qts.rearrange("p (h c) -> p h c", c=S),
                           qtp[pp].rearrange("(h p) c -> p h c", p=128), sem="qld")
                tk = P.dma("sync", kts[:, :], ktp[pp, :, :], sem="kld")
                tv = P.dma("sync", vtq[:, :], vtp[pp, :, :], sem="vld")
                P.wait("tensor", tv)
                if pp == 0:
                    P.wait("tensor", "use", tmask)
                vt_ev = []
                for kt in range(32):
                    ps = bank[4 + kt % 2]
                    if len(vt_ev) >= 2:
                        P.wait("tensor", "dv", vt_ev[-2])
                    tp = P.do("tensor", lambda t, kt=kt, ps=ps: t.matmul(
                        ps[:, 0:128], vtq[:, kt * 128:(kt + 1) * 128], ident[:],
                        start=True, stop=True), sem="pe")
                    P.wait("vector", "pe", tp)
                    te = P.do("vector", lambda v, kt=kt, ps=ps: v.tensor_copy(
                        vts[:, kt * 128:(kt + 1) * 128], ps[:, 0:128]), sem="dv")
                    vt_ev.append(te)
                P.wait("tensor", tq)
                P.wait("tensor", tk)
                P.wait("tensor", "dv", vt_ev[-1])
                if pp == 0:
                    P.wait("scalar", "use", tmask)
                    P.wait("vector", "use", tmask)
                sc_free = [None, None, None]
                et_use = [None] * 4
                av_free = None
                l_free = None
                lm_tok = None
                for hh in range(4):
                    for qb in range(8):
                        nkt = 4 * (qb + 1)
                        av_tok = None
                        prev_tla = None
                        for kt in range(nkt):
                            sb = bank[(0, 1, 5)[kt % 3]]
                            if sc_free[kt % 3] is not None:
                                P.wait("tensor", "ac", sc_free[kt % 3])
                            tmm = P.do("tensor", lambda t, kt=kt, hh=hh, qb=qb, sb=sb: t.matmul(
                                sb[:], kts[:, kt * 128:(kt + 1) * 128],
                                qts[:, hh * S + qb * 512: hh * S + (qb + 1) * 512],
                                start=True, stop=True), sem="pe")
                            P.wait("scalar", "pe", tmm)
                            if et_use[kt % 4] is not None:
                                pe_t, dv_t = et_use[kt % 4]
                                P.wait("scalar", "pe", pe_t)
                                P.wait("scalar", "dv", dv_t)
                            tex = P.do("scalar", lambda s, kt=kt, sb=sb: s.activation(
                                out=et[:, kt % 4, :], in_=sb[:], func=AF.Exp,
                                bias=expb[:, 0:1]), sem="ac")
                            sc_free[kt % 3] = tex
                            dj = kt - 4 * qb
                            P.wait("vector", "ac", tex)
                            if dj >= 0:
                                tmk = P.do("vector", lambda v, kt=kt, dj=dj: v.tensor_mul(
                                    et[:, kt % 4, :], et[:, kt % 4, :],
                                    masks[:, dj * 512:(dj + 1) * 512]), sem="dv")
                            else:
                                tmk = None
                            if prev_tla is not None:
                                P.wait("vector", "dv", prev_tla)
                            if kt == 0:
                                tla = P.do("vector", lambda v, kt=kt: v.tensor_copy(
                                    eacc[:, :], et[:, kt % 4, :]), sem="dv")
                            else:
                                tla = P.do("vector", lambda v, kt=kt: v.tensor_add(
                                    eacc[:, :], eacc[:, :], et[:, kt % 4, :]), sem="dv")
                            prev_tla = tla
                            if tmk is not None:
                                P.wait("tensor", "dv", tmk)
                            else:
                                P.wait("tensor", "ac", tex)
                            if kt == 0 and av_free is not None:
                                P.wait("tensor", "dv", av_free)
                            tav = P.do("tensor", lambda t, kt=kt, nkt=nkt: t.matmul(
                                bank[2][:], vts[:, kt * 128:(kt + 1) * 128],
                                et[:, kt % 4, :], start=(kt == 0),
                                stop=(kt == nkt - 1)), sem="pe")
                            et_use[kt % 4] = (tav, tla)
                            av_tok = tav
                        if lm_tok is not None:
                            P.wait("vector", "pe", lm_tok)
                        if prev_tla is not None:
                            P.wait("vector", "dv", prev_tla)
                        tcb = P.do("vector", lambda v: v.tensor_copy(
                            eaccb[:, :], eacc[:, :]), sem="dv")
                        P.wait("tensor", "dv", tcb)
                        if l_free is not None:
                            P.wait("tensor", "dv", l_free)
                        tlm = P.do("tensor", lambda t: t.matmul(
                            bank[3][:], ones128[:], eaccb[:, :],
                            start=True, stop=True), sem="pe")
                        lm_tok = tlm
                        P.wait("vector", "pe", tlm)
                        trc = P.do("vector", lambda v: v.reciprocal(
                            lrec[:, :], bank[3][:]), sem="dv")
                        l_free = trc
                        P.wait("vector", "pe", av_tok)
                        P.wait("vector", "dv", trc)
                        if ((hh * 8 + qb) % 2) in ao_last:
                            P.wait("vector", ao_last[(hh * 8 + qb) % 2])
                        tnm = P.do("vector", lambda v, hh=hh, qb=qb: v.tensor_mul(
                            astg[:, (hh * 8 + qb) % 2, :], bank[2][:], lrec[:, :]),
                            sem="dv")
                        av_free = tnm
                        P.wait("sync", "dv", tnm)
                        ao_last[(hh * 8 + qb) % 2] = P.dma(
                            "sync",
                            ao[pp * 512 + hh * 128: pp * 512 + (hh + 1) * 128,
                               qb * 512:(qb + 1) * 512],
                            astg[:, (hh * 8 + qb) % 2, :],
                            sem="ao%d" % ((hh * 8 + qb) % 2))
            P.wait("sync", "ao0")
            P.wait("sync", "ao1")

        run_spmd_program(nc, block, handles, body)
    return nc



_NC_A = None
_NC_B = None
_NC_C = None


def _get_programs():
    global _NC_A, _NC_B, _NC_C
    if _NC_A is None:
        _NC_A = build_tokenside(with_proj=True, with_oproj=False)
        _NC_B = build_attention()
        _NC_C = build_tokenside(with_proj=False, with_oproj=True)
    return _NC_A, _NC_B, _NC_C


def kernel(L0_attn_hidden_states, L0_attn_w_q, L0_attn_w_k, L0_attn_w_v,
           L0_attn_w_o, L0_ffn_qw_gate, L0_ffn_z_gate, L0_ffn_s_gate,
           L0_ffn_qw_up, L0_ffn_z_up, L0_ffn_s_up, L0_ffn_qw_down,
           L0_ffn_z_down, L0_ffn_s_down):
    nc_a, nc_b, nc_c = _get_programs()

    x0 = np.ascontiguousarray(np.asarray(L0_attn_hidden_states, dtype=np.float32)
                              .reshape(B * S, D))
    qwg = np.ascontiguousarray(np.asarray(L0_ffn_qw_gate).astype(np.int8))
    qwu = np.ascontiguousarray(np.asarray(L0_ffn_qw_up).astype(np.int8))
    qwd = np.ascontiguousarray(np.asarray(L0_ffn_qw_down).astype(np.int8))
    zg = np.ascontiguousarray(np.asarray(L0_ffn_z_gate).astype(np.int8))
    sg = np.ascontiguousarray(np.asarray(L0_ffn_s_gate, dtype=np.float32))
    zu = np.ascontiguousarray(np.asarray(L0_ffn_z_up).astype(np.int8))
    su = np.ascontiguousarray(np.asarray(L0_ffn_s_up, dtype=np.float32))
    zd = np.ascontiguousarray(np.asarray(L0_ffn_z_down).astype(np.int8))
    sd = np.ascontiguousarray(np.asarray(L0_ffn_s_down, dtype=np.float32))
    wq = np.ascontiguousarray(np.asarray(L0_attn_w_q, dtype=np.float32))
    wk = np.ascontiguousarray(np.asarray(L0_attn_w_k, dtype=np.float32))
    wv = np.ascontiguousarray(np.asarray(L0_attn_w_v, dtype=np.float32))
    wo = np.ascontiguousarray(np.asarray(L0_attn_w_o, dtype=np.float32))

    common = dict(qwg=qwg, qwu=qwu, qwd=qwd, zg=zg, sg=sg, zu=zu, su=su,
                  zd=zd, sd=sd)

    in_a = []
    for c in range(NC):
        m = dict(common)
        m.update(dict(xin=x0[c * TOK:(c + 1) * TOK], wq=wq, wk=wk, wv=wv))
        in_a.append(m)
    res_a = run_bass_kernel_spmd(nc_a, in_a, core_ids=list(range(NC))).results
    x1 = [res_a[c]["xout"] for c in range(NC)]
    qts = [res_a[c]["qt"] for c in range(NC)]
    kts = [res_a[c]["kt"] for c in range(NC)]
    vts = [res_a[c]["vt"] for c in range(NC)]

    in_b = []
    for c in range(NC):
        b = (2 * c) // 8
        owners = [4 * b + o for o in range(4)]
        qtp = np.empty((2, 4 * HD, S), dtype=BF16NP)
        ktp = np.empty((2, HD, S), dtype=BF16NP)
        vtp = np.empty((2, HD, S), dtype=BF16NP)
        for k in range(2):
            g = (2 * c + k) % 8
            qtp[k] = np.concatenate([qts[o][g * 512:(g + 1) * 512, :] for o in owners], axis=1)
            ktp[k] = np.concatenate([kts[o][g * 128:(g + 1) * 128, :] for o in owners], axis=1)
            vtp[k] = np.concatenate([vts[o][g * 128:(g + 1) * 128, :] for o in owners], axis=1)
        in_b.append(dict(qtp=qtp, ktp=ktp, vtp=vtp))
    res_b = run_bass_kernel_spmd(nc_b, in_b, core_ids=list(range(NC))).results
    aos = [res_b[c]["ao"] for c in range(NC)]

    attnT_full = []
    for b in range(B):
        blocks = []
        for g in range(8):
            oc = 4 * b + g // 2
            slot = g % 2
            blocks.append(aos[oc][slot * 512:(slot + 1) * 512, :])
        attnT_full.append(np.concatenate(blocks, axis=0))
    in_c = []
    for c in range(NC):
        b = c // 4
        cs = (c % 4) * TOK
        m = dict(common)
        m.update(dict(
            xin=np.ascontiguousarray(x1[c]),
            attnT=np.ascontiguousarray(attnT_full[b][:, cs:cs + TOK]),
            wo=wo,
        ))
        in_c.append(m)
    res_c = run_bass_kernel_spmd(nc_c, in_c, core_ids=list(range(NC))).results
    out = np.concatenate([res_c[c]["xout"] for c in range(NC)], axis=0)
    return out.reshape(B, S, D).astype(np.float32)


# version 05 (final)
# speedup vs baseline: 1.3866x; 1.1119x over previous
import numpy as np
import ml_dtypes
import concourse.bass as bass
import concourse.mybir as mybir
from concourse.bass_utils import run_bass_kernel_spmd

F32 = mybir.dt.float32
BF16 = mybir.dt.bfloat16
I32 = mybir.dt.int32
I8 = mybir.dt.int8
AF = mybir.ActivationFunctionType
ALU = mybir.AluOpType
AX = mybir.AxisListType
BF16NP = ml_dtypes.bfloat16

D = 4096
HID = 11008
S = 4096
B = 2
NC = 8
TOK = (B * S) // NC
GI = D // 128
GO = HID // 128
NH = 32
NKV = 8
HD = 128
EPS = 1e-5
EXPB = -20.0
NTT = TOK // 128
NCB = HID // 256
QSC = float(HD) ** -0.5


class Prog:

    def __init__(self, me, eng, handles):
        self.me = me
        self.eng = eng
        self.h = handles
        self.v = {k: 0 for k in handles}

    def do(self, engname, fn, sem=None, n=1):
        tok = None
        if sem is not None:
            self.v[sem] += n
            tok = self.v[sem]
        if engname == self.me:
            ins = fn(self.eng)
            if sem is not None:
                ins.then_inc(self.h[sem], n)
        return tok

    def dma(self, engname, out, in_, sem, **kw):
        t = self.do(engname, lambda e: e.dma_start(out=out, in_=in_, **kw),
                    sem=sem, n=16)
        return (sem, t)

    def wait(self, engname, sem, tok=None):
        if isinstance(sem, tuple):
            sem, tok = sem
        elif isinstance(tok, tuple):
            sem, tok = tok
        if engname == self.me:
            self.eng.wait_ge(self.h[sem], self.v[sem] if tok is None else tok)

    def tok(self, sem):
        return self.v[sem]

    def gp(self, fn, sem="gp"):
        t = self.do("gpsimd", fn, sem=sem)
        self.wait("gpsimd", sem, t)
        return t


def run_spmd_program(nc, block, handles, body):
    def mk(engname):
        def walk(eng):
            body(Prog(engname, eng, handles))
        return walk
    block.sync(mk("sync"))
    block.gpsimd(mk("gpsimd"))
    block.tensor(mk("tensor"))
    block.vector(mk("vector"))
    block.scalar(mk("scalar"))



def build_tokenside(with_proj, with_oproj):
    nc = bass.Bass("TRN2", target_bir_lowering=False, debug=False)
    ap = lambda t: t.ap()

    E = {}
    E["xin"] = ap(nc.dram_tensor("xin", [TOK, D], F32, kind="ExternalInput"))
    E["qwg"] = ap(nc.dram_tensor("qwg", [D, HID], I8, kind="ExternalInput"))
    E["qwu"] = ap(nc.dram_tensor("qwu", [D, HID], I8, kind="ExternalInput"))
    E["qwd"] = ap(nc.dram_tensor("qwd", [HID, D], I8, kind="ExternalInput"))
    E["zg"] = ap(nc.dram_tensor("zg", [GI, HID], I8, kind="ExternalInput"))
    E["sg"] = ap(nc.dram_tensor("sg", [GI, HID], F32, kind="ExternalInput"))
    E["zu"] = ap(nc.dram_tensor("zu", [GI, HID], I8, kind="ExternalInput"))
    E["su"] = ap(nc.dram_tensor("su", [GI, HID], F32, kind="ExternalInput"))
    E["zd"] = ap(nc.dram_tensor("zd", [GO, D], I8, kind="ExternalInput"))
    E["sd"] = ap(nc.dram_tensor("sd", [GO, D], F32, kind="ExternalInput"))
    if with_proj:
        E["wq"] = ap(nc.dram_tensor("wq", [D, D], F32, kind="ExternalInput"))
        E["wk"] = ap(nc.dram_tensor("wk", [D, NKV * HD], F32, kind="ExternalInput"))
        E["wv"] = ap(nc.dram_tensor("wv", [D, NKV * HD], F32, kind="ExternalInput"))
        E["qt"] = ap(nc.dram_tensor("qt", [D, TOK], BF16, kind="ExternalOutput"))
        E["kt"] = ap(nc.dram_tensor("kt", [NKV * HD, TOK], BF16, kind="ExternalOutput"))
        E["vt"] = ap(nc.dram_tensor("vt", [NKV * HD, TOK], BF16, kind="ExternalOutput"))
    if with_oproj:
        E["attnT"] = ap(nc.dram_tensor("attnT", [D, TOK], BF16, kind="ExternalInput"))
        E["wo"] = ap(nc.dram_tensor("wo", [D, D], F32, kind="ExternalInput"))
    E["xout"] = ap(nc.dram_tensor("xout", [TOK, D], F32, kind="ExternalOutput"))
    E["xbase_d"] = ap(nc.dram_tensor("xbase_d", [TOK, D], F32))
    E["zsg_d"] = ap(nc.dram_tensor("zsg_d", [GI, HID], BF16))
    E["zsu_d"] = ap(nc.dram_tensor("zsu_d", [GI, HID], BF16))
    E["zsd_d"] = ap(nc.dram_tensor("zsd_d", [GO, D], BF16))
    E["sdeff_d"] = ap(nc.dram_tensor("sdeff_d", [GO, D], BF16))
    E["ht_d"] = ap(nc.dram_tensor("ht_d", [GO, 128, TOK], BF16))
    E["wd_d"] = ap(nc.dram_tensor("wd_d", [GO, 128, D], BF16))

    import contextlib
    with contextlib.ExitStack() as _es:
        _e = _es.enter_context
        xnT = _e(nc.sbuf_tensor("xnT", [128, GI * TOK], BF16))
        wg_sb = _e(nc.sbuf_tensor("wg", [128, GI * 256], BF16))
        wu_sb = _e(nc.sbuf_tensor("wu", [128, GI * 256], BF16))
        qg_i8 = _e(nc.sbuf_tensor("qgi", [128, 2, GI * 256], I8))
        qu_i8 = _e(nc.sbuf_tensor("qui", [128, 2, GI * 256], I8))
        wdq = _e(nc.sbuf_tensor("wdq", [128, D], BF16))
        scb = _e(nc.sbuf_tensor("scb", [GI, 2 * 256], BF16))
        zscb = _e(nc.sbuf_tensor("zscb", [GI, 2, 2 * 256], BF16))
        sdef_sb = _e(nc.sbuf_tensor("sdef", [GO, D], BF16))
        zsdmb = _e(nc.sbuf_tensor("zsdmb", [GO, 512], BF16))
        xsT = _e(nc.sbuf_tensor("xsT", [GI, TOK], BF16))
        hs_sb = _e(nc.sbuf_tensor("hs", [128, NTT * GO], BF16))
        hsT = _e(nc.sbuf_tensor("hsT", [GO, TOK], BF16))
        hct = _e(nc.sbuf_tensor("hct", [128, TOK], BF16))
        htin = _e(nc.sbuf_tensor("htin", [128, 2, TOK], BF16))
        wdt = _e(nc.sbuf_tensor("wdt", [128, 2, 512], BF16))
        ones1 = _e(nc.sbuf_tensor("ones1", [1, 128], BF16))
        onescol = _e(nc.sbuf_tensor("onescol", [128, 1], BF16))
        ident = _e(nc.sbuf_tensor("ident", [128, 128], BF16))
        indg = _e(nc.sbuf_tensor("indg", [128, GI * 128], BF16))
        rowsel = _e(nc.sbuf_tensor("rowsel", [128, GI], F32))
        indone = _e(nc.sbuf_tensor("indone", [128, 128], BF16))
        xld = _e(nc.sbuf_tensor("xld", [128, D], F32))
        xn_sb = _e(nc.sbuf_tensor("xn", [128, D], BF16))
        xs_sb = _e(nc.sbuf_tensor("xs", [128, NTT * GI], BF16))
        xsf = _e(nc.sbuf_tensor("xsf", [128, GI], F32))
        lnst = _e(nc.sbuf_tensor("lnst", [128, 8], F32))
        gsil = _e(nc.sbuf_tensor("gsil", [128, TOK], BF16))
        xacc = _e(nc.sbuf_tensor("xacc", [128, 2, 512], F32))
        pstg = _e(nc.sbuf_tensor("pstg", [128, 2, 512], BF16))
        bk0 = _e(nc.psum_tensor("bk0", [128, 512], F32))
        bk1 = _e(nc.psum_tensor("bk1", [128, 512], F32))
        bk2 = _e(nc.psum_tensor("bk2", [128, 512], F32))
        bk3 = _e(nc.psum_tensor("bk3", [128, 512], F32))
        bk4 = _e(nc.psum_tensor("bk4", [128, 512], F32))
        bk5 = _e(nc.psum_tensor("bk5", [128, 512], F32))
        bk6 = _e(nc.psum_tensor("bk6", [128, 512], F32))
        bk7 = _e(nc.psum_tensor("bk7", [128, 512], F32))
        semnames = ["pe", "dv", "ac", "gp", "pre", "ln", "pz", "ps", "pz0",
                    "pz1", "ps0", "ps1", "pwa", "pwb", "pw", "xli",
                    "xbw", "qg0", "qg1", "qu0", "qu1", "scg", "scu", "zsgq",
                    "zsuq", "wdqi", "wdw", "htw", "ht0", "ht1", "ht2", "ht3",
                    "wd0", "wd1", "wd2", "wd3",
                    "zsm", "xa0", "xa1", "xo0", "xo1", "pj0", "pj1", "pj2",
                    "pj3", "pw0", "pw1", "ati"]
        handles = {n: _e(nc.semaphore(n)) for n in semnames}
        block = _e(nc.Block())
        env = dict(E)
        env.update(dict(
            xnT=xnT, wg_sb=wg_sb, wu_sb=wu_sb, qg_i8=qg_i8, qu_i8=qu_i8,
            wdq=wdq, scb=scb, zscb=zscb,
            sdef_sb=sdef_sb, zsdmb=zsdmb, xsT=xsT, hs_sb=hs_sb, hsT=hsT,
            hct=hct, htin=htin, wdt=wdt, ones1=ones1, onescol=onescol,
            ident=ident, indg=indg, xld=xld, xn_sb=xn_sb, xs_sb=xs_sb,
            lnst=lnst, gsil=gsil, xacc=xacc, pstg=pstg, rowsel=rowsel,
            indone=indone, xsf=xsf,
            bank=[bk0, bk1, bk2, bk3, bk4, bk5, bk6, bk7],
        ))

        def body(P):
            for k in [k for k in env if k.startswith("_")]:
                del env[k]
            emit_prep(P, env)
            if with_oproj:
                emit_oproj(P, env)
                base = env["xbase_d"]
            else:
                base = env["xin"]
            emit_ln(P, env, base, write_xbase=(not with_oproj))
            emit_gateup(P, env)
            emit_down(P, env)
            if with_proj:
                emit_ln(P, env, env["xout"], write_xbase=False)
                emit_projections(P, env)
            for s_ in ("xo0", "xo1", "xbw", "pw0", "pw1", "wdw", "htw", "pw"):
                P.wait("sync", s_)

        run_spmd_program(nc, block, handles, body)
    return nc


def emit_prep(P, env):
    ones1, onescol, ident, indg = env["ones1"], env["onescol"], env["ident"], env["indg"]
    xld = env["xld"]
    CH = HID // 16

    P.gp(lambda g: g.memset(ones1[:], 1.0))
    P.gp(lambda g: g.memset(onescol[:], 1.0))
    P.gp(lambda g: g.memset(ident[:], 1.0))
    P.gp(lambda g: g.affine_select(
        ident[:], ident[:], pattern=[[1, 128]], compare_op=ALU.is_equal,
        fill=0.0, channel_multiplier=-1))
    rowsel, indone = env["rowsel"], env["indone"]
    P.gp(lambda g: g.memset(rowsel[:], 0.0))
    P.gp(lambda g: g.memset(indone[:], 1.0))
    for base_r in (0, 32, 64):
        P.gp(lambda g, base_r=base_r: g.affine_select(
            rowsel[:], rowsel[:], pattern=[[-1, GI]], compare_op=ALU.not_equal,
            fill=1.0, base=-base_r, channel_multiplier=1))
    tsel = P.do("gpsimd", lambda g: g.nop(), sem="gp")
    P.wait("vector", "gp", tsel)
    for gi in range(GI):
        P.do("vector", lambda v, gi=gi: v.tensor_scalar(
            indg[:, gi * 128:(gi + 1) * 128], indone[:],
            rowsel[:, gi:gi + 1], None, ALU.mult), sem="pre")

    items = []
    for z_, s_, out_ in ((env["zg"], env["sg"], env["zsg_d"]),
                         (env["zu"], env["su"], env["zsu_d"])):
        for c in range(16):
            items.append((z_, s_, out_, slice(c * CH, (c + 1) * CH)))
    ztk, stk, dvt, wtk = {}, {}, {}, {}

    def prow(i):
        return slice(32 * (i % 2), 32 * (i % 2) + GI)

    def pload(i):
        z_, s_, out_, sl = items[i]
        r = prow(i)
        if i >= 2:
            P.wait("gpsimd", "dv", dvt[i - 2])
            P.wait("sync", "dv", dvt[i - 2])
            P.wait("gpsimd", wtk[i - 2])
            P.wait("sync", wtk[i - 2])
        ztk[i] = P.dma("gpsimd", xld[r, 0:CH], z_[:, sl], sem="pz%d" % (i % 2))
        stk[i] = P.dma("sync", xld[r, CH:2 * CH], s_[:, sl], sem="ps%d" % (i % 2))

    pload(0)
    pload(1)
    prev = None
    for i, (z_, s_, out_, sl) in enumerate(items):
        r = prow(i)
        P.wait("vector", ztk[i])
        P.wait("vector", stk[i])
        tv = P.do("vector", lambda v, r=r: v.scalar_tensor_tensor(
            xld[r, 2 * CH:3 * CH], xld[r, 0:CH], -1.0,
            xld[r, CH:2 * CH], ALU.mult, ALU.mult), sem="dv")
        dvt[i] = tv
        P.wait("gpsimd", "dv", tv)
        prev = P.dma("gpsimd", out_[:, sl], xld[r, 2 * CH:3 * CH],
                     sem="pw%d" % (i % 2) if False else "pwa" if i % 2 == 0 else "pwb")
        wtk[i] = prev
        if i + 2 < len(items):
            pload(i + 2)
    P.wait("gpsimd", wtk[len(items) - 2])
    P.wait("sync", wtk[len(items) - 2])
    P.wait("gpsimd", wtk[len(items) - 1])
    P.wait("sync", wtk[len(items) - 1])
    P.wait("sync", "dv", dvt[len(items) - 1])
    P.wait("gpsimd", "dv", dvt[len(items) - 1])
    CH2 = D // 2
    for c in range(2):
        sl = slice(c * CH2, (c + 1) * CH2)
        P.wait("gpsimd", prev)
        P.wait("sync", prev)
        tz = P.dma("gpsimd", xld[0:86, 0:CH2], env["zd"][:, sl], sem="pz")
        ts = P.dma("sync", xld[0:86, CH2:2 * CH2], env["sd"][:, sl], sem="ps")
        P.wait("vector", tz)
        P.wait("vector", ts)
        t1 = P.do("vector", lambda v: v.tensor_scalar_mul(
            xld[0:86, CH2:2 * CH2], xld[0:86, CH2:2 * CH2], 0.5), sem="dv")
        P.wait("vector", "dv", t1)
        t2 = P.do("vector", lambda v: v.scalar_tensor_tensor(
            xld[0:86, 0:CH2], xld[0:86, 0:CH2], -1.0,
            xld[0:86, CH2:2 * CH2], ALU.mult, ALU.mult), sem="dv")
        P.wait("gpsimd", "dv", t2)
        P.dma("gpsimd", env["zsd_d"][:, sl], xld[0:86, 0:CH2], sem="pw")
        prev = P.dma("gpsimd", env["sdeff_d"][:, sl], xld[0:86, CH2:2 * CH2], sem="pw")
    P.wait("sync", prev)
    env["_prep_tok"] = P.dma("sync", env["sdef_sb"][:, :], env["sdeff_d"][:, :], sem="ps")


def emit_ln(P, env, src_d, write_xbase):
    xld, xn, lnst = env["xld"], env["xn_sb"], env["lnst"]
    xs_sb, xsT, xnT = env["xs_sb"], env["xsT"], env["xnT"]
    xsf = env["xsf"]
    ident, bank = env["ident"], env["bank"]
    s1, s2, m, var, sd_, rstd, bias = (lnst[:, i:i + 1] for i in range(7))

    P.wait("sync", "xo0", P.tok("xo0"))
    P.wait("sync", "xo1", P.tok("xo1"))
    P.wait("sync", "ac", P.tok("ac"))
    P.wait("tensor", "dv", P.tok("dv"))
    P.wait("tensor", "ac", P.tok("ac"))

    tr_ev = []
    for tt in range(NTT):
        sl = slice(tt * 128, (tt + 1) * 128)
        if tt >= 1:
            P.wait("sync", "ac", env["_xld_free"])
        tld = P.dma("sync", xld[:, :], src_d[sl, :], sem="xli")
        if write_xbase:
            P.dma("sync", env["xbase_d"][sl, :], src_d[sl, :], sem="xbw")
        P.wait("scalar", tld)
        if tr_ev:
            P.wait("scalar", "pe", env["_ln_trmm"])
            P.wait("scalar", "dv", env["_xsf_cp"])
        ts1 = P.do("scalar", lambda s: s.activation(
            out=xn[:, :], in_=xld[:, :], func=AF.Copy, accum_out=s1), sem="ac")
        P.wait("scalar", "ac", ts1)
        ts2 = P.do("scalar", lambda s: s.activation(
            out=xn[:, :], in_=xld[:, :], func=AF.Square, accum_out=s2), sem="ac")
        P.wait("vector", "ac", ts2)
        P.do("vector", lambda v: v.tensor_scalar(m, s1, 1.0 / D, None, ALU.mult), sem="dv")
        t_mv = P.do("vector", lambda v: v.tensor_scalar(var, s2, 1.0 / D, EPS, ALU.mult, ALU.add), sem="dv")
        P.wait("vector", "dv", t_mv)
        tm = P.do("vector", lambda v: v.scalar_tensor_tensor(
            s1, s1, -1.0 / (float(D) * float(D)), s1, ALU.mult, ALU.mult), sem="dv")
        P.wait("vector", "dv", tm)
        tv = P.do("vector", lambda v: v.tensor_add(var, var, s1), sem="dv")
        P.wait("scalar", "dv", tv)
        tsq = P.do("scalar", lambda s: s.activation(out=sd_, in_=var, func=AF.Sqrt), sem="ac")
        P.wait("vector", "ac", tsq)
        tr = P.do("vector", lambda v: v.reciprocal(rstd, sd_), sem="dv")
        P.wait("vector", "dv", tr)
        tb = P.do("vector", lambda v: v.scalar_tensor_tensor(
            bias, m, -1.0, rstd, ALU.mult, ALU.mult), sem="dv")
        P.wait("scalar", "dv", tb)
        P.wait("scalar", "ac", ts2)
        txn = P.do("scalar", lambda s: s.activation(
            out=xn[:, :], in_=xld[:, :], func=AF.Identity,
            scale=rstd, bias=bias), sem="ac")
        env["_xld_free"] = txn
        P.wait("vector", "ac", txn)
        if "_xsf_cp" in env:
            P.wait("vector", "dv", env["_xsf_cp"])
        trd = None
        for gi in range(GI):
            trd = P.do("vector", lambda v, gi=gi: v.reduce_sum(
                xsf[:, gi:gi + 1],
                xn[:, gi * 128:(gi + 1) * 128], axis=AX.X), sem="dv")
        P.wait("vector", "dv", trd)
        env["_xsf_cp"] = P.do("vector", lambda v, tt=tt: v.tensor_copy(
            xs_sb[:, tt * GI:(tt + 1) * GI], xsf[:, :]), sem="dv")
        P.wait("tensor", "ac", txn)
        for gi in range(GI):
            ps = bank[gi % 4]
            if len(tr_ev) >= 4:
                P.wait("tensor", "ac", tr_ev[-4])
            tp = P.do("tensor", lambda t, gi=gi, ps=ps: t.matmul(
                ps[:, 0:128], xn[:, gi * 128:(gi + 1) * 128],
                ident[:], start=True, stop=True), sem="pe")
            if gi == GI - 1:
                env["_ln_trmm"] = tp
            P.wait("scalar", "pe", tp)
            te = P.do("scalar", lambda s, gi=gi, tt=tt, ps=ps: s.activation(
                out=xnT[:, gi * TOK + tt * 128: gi * TOK + (tt + 1) * 128],
                in_=ps[:, 0:128], func=AF.Copy), sem="ac")
            tr_ev.append(te)
    P.wait("tensor", "dv", P.tok("dv"))
    P.wait("tensor", "ac", tr_ev[-1])
    xst_dv = {}
    for tc in range(NTT):
        ps = bank[tc % 2]
        if tc >= 2:
            P.wait("tensor", "dv", xst_dv[tc - 2])
        tp = P.do("tensor", lambda t, tc=tc, ps=ps: t.matmul(
            ps[0:GI, 0:128], xs_sb[:, tc * GI:(tc + 1) * GI], ident[:],
            start=True, stop=True), sem="pe")
        P.wait("vector", "pe", tp)
        xst_dv[tc] = P.do("vector", lambda v, tc=tc, ps=ps: v.tensor_copy(
            xsT[:, tc * 128:(tc + 1) * 128], ps[0:GI, 0:128]),
            sem=("dv" if tc < NTT - 1 else "ln"))
    env["_ln_done"] = xst_dv[NTT - 1]
    env["_lnxst_dv"] = P.tok("dv")


def emit_gateup(P, env):
    wg_sb, wu_sb, scb, zscb = env["wg_sb"], env["wu_sb"], env["scb"], env["zscb"]
    qg_i8, qu_i8 = env["qg_i8"], env["qu_i8"]
    xnT, xsT, indg = env["xnT"], env["xsT"], env["indg"]
    gsil, hct, hs_sb = env["gsil"], env["hct"], env["hs_sb"]
    bank = env["bank"]
    onescol, ident, hsT = env["onescol"], env["ident"], env["hsT"]
    ht_d, wd_d, wdq = env["ht_d"], env["wd_d"], env["wdq"]
    sdef_sb = env["sdef_sb"]

    P.wait("tensor", "ln", env["_ln_done"])
    P.wait("gpsimd", "ln", env["_ln_done"])
    P.wait("sync", "ln", env["_ln_done"])
    P.wait("tensor", env["_prep_tok"])
    P.wait("tensor", "ac", P.tok("ac"))
    P.wait("vector", "pe", P.tok("pe"))

    qwg_v = env["qwg"].rearrange("(g p) c -> p g c", p=128)
    qwu_v = env["qwu"].rearrange("(g p) c -> p g c", p=128)
    qwd_v = env["qwd"].rearrange("(ct p) c -> ct p c", p=128)

    wtok, utok, dtok = {}, {}, {}
    ht_wtok = [None] * GO
    wd_wtok = [None] * GO
    env["_ht_wtok"] = ht_wtok
    env["_wd_wtok"] = wd_wtok

    dqend = {}
    bc_pe = {}

    def load_cb(mat, cb):
        src = qwg_v if mat == 0 else qwu_v
        dst = qg_i8 if mat == 0 else qu_i8
        s_in = env["sg"] if mat == 0 else env["su"]
        zs_in = env["zsg_d"] if mat == 0 else env["zsu_d"]
        bb = cb % 2
        eng = "sync" if mat == 0 else "scalar"
        if (mat, cb - 2) in dqend:
            P.wait(eng, "dv", dqend[(mat, cb - 2)])
        if (mat, cb - 1) in bc_pe:
            P.wait("gpsimd", "pe", bc_pe[(mat, cb - 1)])
        if (mat, cb - 2) in utok:
            P.wait("sync", "pe", utok[(mat, cb - 2)])
        if env.get("_zs_prev") is not None:
            P.wait("sync", env["_zs_prev"])
        sl = slice(cb * 256, (cb + 1) * 256)
        t1 = P.dma(eng, dst[:, bb, :].rearrange("p (g c) -> p g c", c=256),
                   src[:, :, sl], sem=("qg%d" % bb if mat == 0 else "qu%d" % bb))
        t2 = P.dma("gpsimd", scb[:, mat * 256:(mat + 1) * 256], s_in[:, sl],
                   sem=("scg" if mat == 0 else "scu"))
        t3 = P.dma("sync", zscb[:, cb % 2, mat * 256:(mat + 1) * 256],
                   zs_in[:, sl], sem=("zsgq" if mat == 0 else "zsuq"))
        env["_zs_prev"] = t3
        wtok[(mat, cb)] = (t1, t2, t3)

    def load_dct(ct):
        if ct >= 1 and wd_wtok[ct - 1] is not None:
            P.wait("gpsimd", wd_wtok[ct - 1])
        dtok[ct] = P.dma("gpsimd", wdq[:, :], qwd_v[ct, :, :], sem="wdqi")

    bc_state = [0, None, None]

    def bcast(lhs_lo, lhs_n, lhs_col, rhs, ncol):
        i = bc_state[0] % 2
        bc_state[0] += 1
        bkx = bank[4 + i]
        if bc_state[1 + i] is not None:
            P.wait("tensor", "dv", bc_state[1 + i])
        tp = P.do("tensor", lambda t: t.matmul(
            bkx[:, 0:ncol], indg[lhs_lo:lhs_lo + lhs_n,
                                 lhs_col:lhs_col + 128], rhs,
            start=True, stop=True), sem="pe")
        return i, bkx, tp

    def bcast_done(i, tok):
        bc_state[1 + i] = tok

    def dq(mat, cb):
        Wt = wg_sb if mat == 0 else wu_sb
        src8 = (qg_i8 if mat == 0 else qu_i8)[:, cb % 2, :]
        t1, t2, t3 = wtok[(mat, cb)]
        P.wait("tensor", t2)
        P.wait("vector", t1)
        if (mat, cb - 1) in utok:
            P.wait("vector", "pe", utok[(mat, cb - 1)])
        for gp in range(GI // 2):
            gi0, gi1 = 2 * gp, 2 * gp + 1
            i = bc_state[0] % 2
            bc_state[0] += 1
            bkx = bank[4 + i]
            if bc_state[1 + i] is not None:
                P.wait("tensor", "dv", bc_state[1 + i])
            P.do("tensor", lambda t, gi0=gi0, bkx=bkx, mat=mat: t.matmul(
                bkx[:, 0:256], indg[0:GI, gi0 * 128:gi0 * 128 + 128],
                scb[:, mat * 256:(mat + 1) * 256],
                start=True, stop=True), sem="pe")
            tp = P.do("tensor", lambda t, gi1=gi1, bkx=bkx, mat=mat: t.matmul(
                bkx[:, 256:512], indg[0:GI, gi1 * 128:gi1 * 128 + 128],
                scb[:, mat * 256:(mat + 1) * 256],
                start=True, stop=True), sem="pe")
            P.wait("vector", "pe", tp)
            tv = P.do("vector", lambda v, gp=gp, Wt=Wt, src8=src8, bkx=bkx:
                      v.tensor_mul(
                          Wt[:, gp * 512:(gp + 1) * 512],
                          src8[:, gp * 512:(gp + 1) * 512],
                          bkx[:, 0:512]), sem="dv")
            bc_state[1 + i] = tv
        dqend[(mat, cb)] = P.tok("dv")
        bc_pe[(mat, cb)] = P.tok("pe")

    load_cb(0, 0)
    load_cb(1, 0)
    load_dct(0)
    dq(0, 0)
    dq(1, 0)
    load_cb(0, 1)
    load_cb(1, 1)

    hs_ev = {}
    acc_free = {}
    hct_free = None
    hs_pe_tok = None

    for cb in range(NCB):
        env["_dqg"] = dqend[(0, cb)]
        env["_dqu"] = dqend[(1, cb)]
        for ct in range(2):
            gct = 2 * cb + ct
            W0 = wg_sb[:, :]
            W1 = wu_sb[:, :]
            _, _, t3g = wtok[(0, cb)]
            _, _, t3u = wtok[(1, cb)]
            P.wait("tensor", t3g)
            P.wait("tensor", t3u)
            P.wait("tensor", "dv", env["_dqg"])
            P.wait("tensor", "dv", env["_dqu"])
            for tb in range(2):
                bk = bank[tb]
                for gi in range(GI):
                    if gi == 0 and (0, tb) in acc_free:
                        P.wait("tensor", "ac", acc_free[(0, tb)])
                    P.do("tensor", lambda t, gi=gi, ct=ct, tb=tb, bk=bk, W0=W0: t.matmul(
                        bk[:], W0[:, gi * 256 + ct * 128: gi * 256 + ct * 128 + 128],
                        xnT[:, gi * TOK + tb * 512: gi * TOK + (tb + 1) * 512],
                        start=(gi == 0), stop=False), sem="pe")
                tzg = P.do("tensor", lambda t, ct=ct, tb=tb, bk=bk, cb=cb: t.matmul(
                    bk[:], zscb[0:GI, cb % 2, ct * 128: ct * 128 + 128],
                    xsT[:, tb * 512:(tb + 1) * 512],
                    start=False, stop=True), sem="pe")
                env["_lastg_pe"] = tzg
                P.wait("scalar", "pe", tzg)
                tsl = P.do("scalar", lambda s, tb=tb, bk=bk: s.activation(
                    out=gsil[:, tb * 512:(tb + 1) * 512], in_=bk[:],
                    func=AF.Silu), sem="ac")
                acc_free[(0, tb)] = tsl
            for tb in range(2):
                bk = bank[2 + tb]
                for gi in range(GI):
                    if gi == 0 and (1, tb) in acc_free:
                        P.wait("tensor", "dv", acc_free[(1, tb)])
                    P.do("tensor", lambda t, gi=gi, ct=ct, tb=tb, bk=bk, W1=W1: t.matmul(
                        bk[:], W1[:, gi * 256 + ct * 128: gi * 256 + ct * 128 + 128],
                        xnT[:, gi * TOK + tb * 512: gi * TOK + (tb + 1) * 512],
                        start=(gi == 0), stop=False), sem="pe")
                tzu = P.do("tensor", lambda t, ct=ct, tb=tb, bk=bk, cb=cb: t.matmul(
                    bk[:], zscb[0:GI, cb % 2, 256 + ct * 128: 256 + ct * 128 + 128],
                    xsT[:, tb * 512:(tb + 1) * 512],
                    start=False, stop=True), sem="pe")
                env["_lastu_pe"] = tzu
                P.wait("vector", "pe", tzu)
                P.wait("vector", "ac", acc_free[(0, tb)])
                if hct_free is not None:
                    P.wait("vector", hct_free)
                if hs_pe_tok is not None:
                    P.wait("vector", "pe", hs_pe_tok)
                th = P.do("vector", lambda v, tb=tb, bk=bk: v.tensor_mul(
                    hct[:, tb * 512:(tb + 1) * 512], bk[:],
                    gsil[:, tb * 512:(tb + 1) * 512]), sem="dv")
                acc_free[(1, tb)] = th
            th_all = P.tok("dv")
            hb = bank[6 + gct % 2]
            P.wait("tensor", "dv", th_all)
            if gct - 2 in hs_ev:
                P.wait("tensor", "dv", hs_ev[gct - 2])
            for tc in range(NTT):
                P.do("tensor", lambda t, tc=tc, hb=hb: t.matmul(
                    hb[:, tc:tc + 1], hct[:, tc * 128:(tc + 1) * 128],
                    onescol[:], start=True, stop=True), sem="pe")
            hs_pe_tok = P.tok("pe")
            P.wait("vector", "pe", hs_pe_tok)
            tev = P.do("vector", lambda v, gct=gct, hb=hb: v.tensor_copy(
                hs_sb[:, gct * NTT:(gct + 1) * NTT], hb[:, 0:NTT]), sem="dv")
            hs_ev[gct] = tev
            P.wait("scalar", "dv", th_all)
            hct_free = P.dma("scalar", ht_d[gct, :, :], hct[:, :], sem="htw")
            ht_wtok[gct] = hct_free
            ct2 = gct
            P.wait("vector", dtok[ct2])
            base = (ct2 // 32) * 32
            nrow = min(32, GO - base)
            last_dq = None
            for mb in range(8):
                i, bkx, tp = bcast(base, nrow, (ct2 - base) * 128,
                                   sdef_sb[base:base + nrow, mb * 512:(mb + 1) * 512],
                                   512)
                P.wait("vector", "pe", tp)
                tv = P.do("vector", lambda v, mb=mb, bkx=bkx: v.tensor_mul(
                    wdq[:, mb * 512:(mb + 1) * 512], wdq[:, mb * 512:(mb + 1) * 512],
                    bkx[:, 0:512]), sem="dv")
                bcast_done(i, tv)
                last_dq = tv
            P.wait("scalar", "dv", last_dq)
            wd_wtok[ct2] = P.dma("scalar", wd_d[ct2, :, :], wdq[:, :], sem="wdw")
            if ct2 + 1 < GO:
                load_dct(ct2 + 1)
        utok[(0, cb)] = env["_lastg_pe"]
        utok[(1, cb)] = env["_lastu_pe"]
        if cb + 1 < NCB:
            dq(0, cb + 1)
            dq(1, cb + 1)
        if cb + 2 < NCB:
            load_cb(0, cb + 2)
            load_cb(1, cb + 2)
    P.wait("tensor", "dv", P.tok("dv"))
    hst_dv = {}
    for tc in range(NTT):
        ps = bank[tc % 2]
        if tc >= 2:
            P.wait("tensor", "dv", hst_dv[tc - 2])
        tp = P.do("tensor", lambda t, tc=tc, ps=ps: t.matmul(
            ps[0:GO, 0:128], hs_sb.rearrange("p (ct tc) -> p tc ct", tc=NTT)[:, tc, :],
            ident[:], start=True, stop=True), sem="pe")
        P.wait("vector", "pe", tp)
        hst_dv[tc] = P.do("vector", lambda v, tc=tc, ps=ps: v.tensor_copy(
            hsT[:, tc * 128:(tc + 1) * 128], ps[0:GO, 0:128]),
            sem=("dv" if tc < NTT - 1 else "ln"))
    env["_gu_done"] = hst_dv[NTT - 1]


def emit_down(P, env):
    htin, wdt, xacc = env["htin"], env["wdt"], env["xacc"]
    zsdmb, hsT = env["zsdmb"], env["hsT"]
    bank = env["bank"]
    ht_d, wd_d, xbase_d, xout = env["ht_d"], env["wd_d"], env["xbase_d"], env["xout"]
    ht_wtok, wd_wtok = env["_ht_wtok"], env["_wd_wtok"]

    P.wait("tensor", "ln", env["_gu_done"])
    P.wait("vector", "ln", env["_gu_done"])
    P.wait("sync", "xbw", P.tok("xbw"))

    htok, wtok2 = {}, {}
    dn_use = {}
    env["_dn_use"] = dn_use
    wg_sb, wu_sb, xnT = env["wg_sb"], env["wu_sb"], env["xnT"]
    chunks = []
    c0 = 0
    while c0 < GO:
        chunks.append((c0, min(4, GO - c0)))
        c0 += 4
    NCH = len(chunks)

    def ht4(s):
        base = wg_sb if s < 2 else wu_sb
        return base[:, (s % 2) * 4 * TOK:(s % 2 + 1) * 4 * TOK] \
            .rearrange("p (b c) -> p b c", c=TOK)

    def wd4(s):
        return xnT[:, s * 4 * 512:(s + 1) * 4 * 512] \
            .rearrange("p (b c) -> p b c", c=512)

    P.wait("scalar", "pe", P.tok("pe"))
    P.wait("sync", "pe", P.tok("pe"))

    def load_chunk(k):
        mb, kc = divmod(k, NCH)
        ct0, nct = chunks[kc]
        s = k % 4
        if k >= 4:
            P.wait("scalar", "pe", dn_use[k - 4])
            P.wait("sync", "pe", dn_use[k - 4])
        P.wait("scalar", ht_wtok[ct0 + nct - 1])
        P.wait("sync", wd_wtok[ct0 + nct - 1])
        htok[k] = P.dma(
            "scalar", ht4(s)[:, 0:nct, :],
            ht_d[ct0:ct0 + nct, :, :].rearrange("ct p t -> p ct t"),
            sem="ht%d" % s)
        wtok2[k] = P.dma(
            "sync", wd4(s)[:, 0:nct, :],
            wd_d[ct0:ct0 + nct, :, mb * 512:(mb + 1) * 512]
            .rearrange("ct p t -> p ct t"),
            sem="wd%d" % s)

    for _k in range(4):
        load_chunk(_k)
    zt_prev = None
    evac_toks = [None] * NTT
    xw_tok = {}
    for mb in range(8):
        if zt_prev is not None:
            P.wait("sync", "pe", zt_prev)
        tz = P.dma("sync", zsdmb[:, :], env["zsd_d"][:, mb * 512:(mb + 1) * 512],
                   sem="zsm")
        for kc in range(NCH):
            k = mb * NCH + kc
            ct0, nct = chunks[kc]
            s = k % 4
            P.wait("tensor", htok[k])
            P.wait("tensor", wtok2[k])
            for j in range(nct):
                for tt in range(NTT):
                    if ct0 + j == 0 and evac_toks[tt] is not None:
                        P.wait("tensor", "dv", evac_toks[tt])
                    P.do("tensor", lambda t, tt=tt, j=j, s=s: t.matmul(
                        bank[tt][:], ht4(s)[:, j, tt * 128:(tt + 1) * 128],
                        wd4(s)[:, j, :], start=(ct0 + j == 0), stop=False),
                        sem="pe")
            dn_use[k] = P.tok("pe")
            if k + 4 < 8 * NCH:
                load_chunk(k + 4)
        P.wait("tensor", tz)
        for tt in range(NTT):
            tzc = P.do("tensor", lambda t, tt=tt: t.matmul(
                bank[tt][:], hsT[:, tt * 128:(tt + 1) * 128], zsdmb[:, :],
                start=False, stop=True), sem="pe")
            if tt == NTT - 1:
                zt_prev = tzc
            if (tt % 2) in xw_tok:
                P.wait("sync", xw_tok[tt % 2])
            txa = P.dma("sync", xacc[:, tt % 2, :],
                        xbase_d[tt * 128:(tt + 1) * 128, mb * 512:(mb + 1) * 512],
                        sem="xa%d" % (tt % 2))
            P.wait("vector", "pe", tzc)
            P.wait("vector", txa)
            ta = P.do("vector", lambda v, tt=tt: v.tensor_add(
                xacc[:, tt % 2, :], xacc[:, tt % 2, :], bank[tt][:]), sem="dv")
            evac_toks[tt] = ta
            P.wait("sync", "dv", ta)
            xw_tok[tt % 2] = P.dma(
                "sync", xout[tt * 128:(tt + 1) * 128, mb * 512:(mb + 1) * 512],
                xacc[:, tt % 2, :], sem="xo%d" % (tt % 2))
    env["_down_done"] = P.tok("dv")


def emit_projections(P, env):
    wg_sb, wu_sb = env["wg_sb"], env["wu_sb"]
    xnT, pstg = env["xnT"], env["pstg"]
    bank = env["bank"]

    jobs = [(env["wq"], env["qt"], D, QSC),
            (env["wk"], env["kt"], NKV * HD, 1.0),
            (env["wv"], env["vt"], NKV * HD, 1.0)]

    P.wait("gpsimd", "dv", env["_down_done"])
    P.wait("gpsimd", "pe", P.tok("pe"))
    P.wait("tensor", "ln", env["_ln_done"])
    P.wait("tensor", "ac", P.tok("ac"))
    P.wait("scalar", "dv", env["_down_done"])

    seq = []
    for (w_in, out_d, ncols, scale) in jobs:
        w_v = w_in.rearrange("(g p) c -> p g c", p=128)
        for db in range(ncols // 256):
            seq.append((w_v, out_d, db, scale))

    ltoks, used, bufs = {}, {}, {}

    def load_j(j):
        w_v, out_d, db, scale = seq[j]
        buf = wg_sb[:, :] if j % 2 == 0 else wu_sb[:, :]
        if (j - 2) in used:
            P.wait("gpsimd", "pe", used[j - 2])
        sl = slice(db * 256, (db + 1) * 256)
        ltoks[j] = P.dma("gpsimd", buf.rearrange("p (g c) -> p g c", c=256),
                         w_v[:, :, sl], sem="pj%d" % (j % 2))
        return buf

    for j in range(min(2, len(seq))):
        bufs[j] = load_j(j)
    ev_free = {}
    pj_w = {}
    for j in range(len(seq)):
        w_v, out_d, db, scale = seq[j]
        buf = bufs[j]
        P.wait("tensor", ltoks[j])
        for ct in range(2):
            for tb in range(2):
                bki = ct * 2 + tb
                bk = bank[bki]
                if bki in ev_free:
                    P.wait("tensor", "ac", ev_free[bki])
                for gi in range(GI):
                    P.do("tensor", lambda t, gi=gi, ct=ct, tb=tb, bk=bk, buf=buf: t.matmul(
                        bk[:], buf[:, gi * 256 + ct * 128: gi * 256 + ct * 128 + 128],
                        xnT[:, gi * TOK + tb * 512: gi * TOK + (tb + 1) * 512],
                        start=(gi == 0), stop=(gi == GI - 1)), sem="pe")
                tmm = P.tok("pe")
                P.wait("scalar", "pe", tmm)
                if (bki % 2) in pj_w:
                    P.wait("scalar", pj_w[bki % 2])
                tev = P.do("scalar", lambda s, bk=bk, bki=bki, scale=scale: s.activation(
                    out=pstg[:, bki % 2, :], in_=bk[:], func=AF.Copy, scale=scale),
                    sem="ac")
                ev_free[bki] = tev
                P.wait("scalar", "ac", tev)
                r0 = db * 256 + ct * 128
                pj_w[bki % 2] = P.dma(
                    "scalar", out_d[r0:r0 + 128, tb * 512:(tb + 1) * 512],
                    pstg[:, bki % 2, :], sem="pw%d" % (bki % 2))
        used[j] = P.tok("pe")
        if j + 2 < len(seq):
            bufs[j + 2] = load_j(j + 2)


def emit_oproj(P, env):
    wg_sb, wu_sb = env["wg_sb"], env["wu_sb"]
    xnT, xacc, xld = env["xnT"], env["xacc"], env["xld"]
    bank = env["bank"]
    attnT, wo, xin, xbase_d = env["attnT"], env["wo"], env["xin"], env["xbase_d"]

    at_v = attnT.rearrange("(g p) c -> p g c", p=128)
    t_at = P.dma("sync", xnT.rearrange("p (g c) -> p g c", c=TOK), at_v, sem="ati")
    wo_v = wo.rearrange("(g p) c -> p g c", p=128)

    P.wait("sync", "dv", P.tok("dv"))
    P.wait("scalar", "dv", P.tok("dv"))
    for s_ in ("pwa", "pwb", "pw"):
        P.wait("sync", s_, P.tok(s_))
        P.wait("scalar", s_, P.tok(s_))

    ldt, castt, wuse = {}, {}, {}
    NQ = 4 * (D // 256)

    def stage(q):
        mbq, qh = divmod(q, 4)
        eng = "sync" if q % 2 == 0 else "scalar"
        if q >= 2:
            P.wait(eng, "ac", castt[q - 2])
        sl = slice(mbq * 256, (mbq + 1) * 256)
        ldt[q] = P.dma(
            eng, xld[:, (q % 2) * 2048:(q % 2) * 2048 + 2048]
            .rearrange("p (g c) -> p g c", c=256),
            wo_v[:, qh * 8:(qh + 1) * 8, sl], sem="pj%d" % (q % 4))

    def docast(q):
        mbq, qh = divmod(q, 4)
        buf = wg_sb[:, :] if mbq % 2 == 0 else wu_sb[:, :]
        P.wait("scalar", ldt[q])
        if qh == 0 and (mbq - 2) in wuse:
            P.wait("scalar", "pe", wuse[mbq - 2])
        castt[q] = P.do("scalar", lambda s, buf=buf, qh=qh, q=q: s.activation(
            out=buf[:, qh * 2048:(qh + 1) * 2048],
            in_=xld[:, (q % 2) * 2048:(q % 2) * 2048 + 2048],
            func=AF.Copy), sem="ac")

    stage(0)
    stage(1)
    P.wait("tensor", t_at)
    evd = {}
    xw = {}
    for mbq in range(D // 256):
        buf = wg_sb[:, :] if mbq % 2 == 0 else wu_sb[:, :]
        for q in range(4 * mbq, 4 * mbq + 4):
            docast(q)
            if q + 2 < NQ:
                stage(q + 2)
        P.wait("tensor", "ac", castt[4 * mbq + 3])
        for gi in range(GI):
            for tt in range(NTT):
                if gi == 0 and tt in evd:
                    P.wait("tensor", "dv", evd[tt])
                P.do("tensor", lambda t, gi=gi, tt=tt, buf=buf: t.matmul(
                    bank[tt][:, 0:256],
                    xnT[:, gi * TOK + tt * 128: gi * TOK + (tt + 1) * 128],
                    buf[:, gi * 256:(gi + 1) * 256],
                    start=(gi == 0), stop=(gi == GI - 1)), sem="pe")
        tmm = P.tok("pe")
        for tt in range(NTT):
            if (tt % 2) in xw:
                P.wait("sync", xw[tt % 2])
            txa = P.dma("sync", xacc[:, tt % 2, 0:256],
                        xin[tt * 128:(tt + 1) * 128, mbq * 256:(mbq + 1) * 256],
                        sem="xa%d" % (tt % 2))
            P.wait("vector", "pe", tmm)
            P.wait("vector", txa)
            ta = P.do("vector", lambda v, tt=tt: v.tensor_add(
                xacc[:, tt % 2, 0:256], xacc[:, tt % 2, 0:256],
                bank[tt][:, 0:256]), sem="dv")
            evd[tt] = ta
            P.wait("sync", "dv", ta)
            xw[tt % 2] = P.dma(
                "sync", xbase_d[tt * 128:(tt + 1) * 128, mbq * 256:(mbq + 1) * 256],
                xacc[:, tt % 2, 0:256], sem="xo%d" % (tt % 2))
        wuse[mbq] = P.tok("pe")
    env["_op_done"] = P.do("vector", lambda v: v.nop(), sem="ln")



def build_attention():
    nc = bass.Bass("TRN2", target_bir_lowering=False, debug=False)
    ap = lambda t: t.ap()
    qtp = ap(nc.dram_tensor("qtp", [2, 4 * HD, S], BF16, kind="ExternalInput"))
    ktp = ap(nc.dram_tensor("ktp", [2, HD, S], BF16, kind="ExternalInput"))
    vtp = ap(nc.dram_tensor("vtp", [2, HD, S], BF16, kind="ExternalInput"))
    ao = ap(nc.dram_tensor("ao", [2 * 4 * HD, S], BF16, kind="ExternalOutput"))

    import contextlib
    with contextlib.ExitStack() as _es:
        _e = _es.enter_context
        qts = _e(nc.sbuf_tensor("qts", [128, 4 * S], BF16))
        kts = _e(nc.sbuf_tensor("kts", [128, S], BF16))
        vtq = _e(nc.sbuf_tensor("vtq", [128, S], BF16))
        vts = _e(nc.sbuf_tensor("vts", [128, S], BF16))
        et = _e(nc.sbuf_tensor("et", [128, 4, 512], BF16))
        eacc = _e(nc.sbuf_tensor("eacc", [128, 512], F32))
        eaccb = _e(nc.sbuf_tensor("eaccb", [128, 512], BF16))
        lrec = _e(nc.sbuf_tensor("lrec", [128, 512], F32))
        masks = _e(nc.sbuf_tensor("masks", [128, 4 * 512], BF16))
        ones128 = _e(nc.sbuf_tensor("ones128", [128, 128], BF16))
        ident = _e(nc.sbuf_tensor("ident", [128, 128], BF16))
        astg = _e(nc.sbuf_tensor("astg", [128, 2, 512], BF16))
        expb = _e(nc.sbuf_tensor("expb", [128, 1], F32))
        negb = _e(nc.sbuf_tensor("negb", [128, 1], F32))
        bk0 = _e(nc.psum_tensor("bk0", [128, 512], F32))
        bk1 = _e(nc.psum_tensor("bk1", [128, 512], F32))
        bk2 = _e(nc.psum_tensor("bk2", [128, 512], F32))
        bk3 = _e(nc.psum_tensor("bk3", [128, 512], F32))
        bk4 = _e(nc.psum_tensor("bk4", [128, 512], F32))
        bk5 = _e(nc.psum_tensor("bk5", [128, 512], F32))
        semnames = ["pe", "dv", "ac", "use", "gp", "qld", "kld", "vld",
                    "ao0", "ao1"]
        handles = {n: _e(nc.semaphore(n)) for n in semnames}
        block = _e(nc.Block())
        bank = [bk0, bk1, bk2, bk3, bk4, bk5]

        def body(P):
            P.gp(lambda g: g.memset(ones128[:], 1.0), sem="use")
            P.gp(lambda g: g.memset(expb[:], EXPB), sem="use")
            P.gp(lambda g: g.memset(negb[:], -1.0), sem="use")
            P.gp(lambda g: g.memset(ident[:], 1.0), sem="use")
            P.gp(lambda g: g.affine_select(
                ident[:], ident[:], pattern=[[1, 128]], compare_op=ALU.is_equal,
                fill=0.0, channel_multiplier=-1), sem="use")
            P.gp(lambda g: g.memset(masks[:], 1.0), sem="use")
            tmask = None
            for j in range(4):
                tmask = P.gp(lambda g, j=j: g.affine_select(
                    masks[:, j * 512:(j + 1) * 512], masks[:, j * 512:(j + 1) * 512],
                    pattern=[[1, 512]], compare_op=ALU.is_ge, fill=0.0,
                    base=-128 * j, channel_multiplier=-1), sem="use")
            ao_last = {}
            for pp in range(2):
                if pp == 1:
                    P.wait("sync", "ao0", P.tok("ao0"))
                    P.wait("sync", "ao1", P.tok("ao1"))
                    P.wait("sync", "ac", P.tok("ac"))
                    P.wait("sync", "pe", P.tok("pe"))
                tq = P.dma("sync", qts.rearrange("p (h c) -> p h c", c=S),
                           qtp[pp].rearrange("(h p) c -> p h c", p=128), sem="qld")
                tk = P.dma("sync", kts[:, :], ktp[pp, :, :], sem="kld")
                tv = P.dma("sync", vtq[:, :], vtp[pp, :, :], sem="vld")
                P.wait("tensor", tv)
                if pp == 0:
                    P.wait("tensor", "use", tmask)
                vt_ev = []
                for kt in range(32):
                    ps = bank[4 + kt % 2]
                    if len(vt_ev) >= 2:
                        P.wait("tensor", "dv", vt_ev[-2])
                    tp = P.do("tensor", lambda t, kt=kt, ps=ps: t.matmul(
                        ps[:, 0:128], vtq[:, kt * 128:(kt + 1) * 128], ident[:],
                        start=True, stop=True), sem="pe")
                    P.wait("vector", "pe", tp)
                    te = P.do("vector", lambda v, kt=kt, ps=ps: v.tensor_copy(
                        vts[:, kt * 128:(kt + 1) * 128], ps[:, 0:128]), sem="dv")
                    vt_ev.append(te)
                P.wait("tensor", tq)
                P.wait("tensor", tk)
                P.wait("tensor", "dv", vt_ev[-1])
                if pp == 0:
                    P.wait("scalar", "use", tmask)
                    P.wait("vector", "use", tmask)
                sc_free = [None, None, None]
                et_use = [None] * 4
                av_free = None
                l_free = None
                lm_tok = None
                lrec_free = None
                for hh in range(4):
                    for qb in range(8):
                        nkt = 4 * (qb + 1)
                        av_tok = None
                        prev_tla = None
                        pend = {}
                        for kt in range(nkt):
                            sb = bank[(0, 1, 5)[kt % 3]]
                            if sc_free[kt % 3] is not None:
                                P.wait("tensor", "ac", sc_free[kt % 3])
                            tmm = P.do("tensor", lambda t, kt=kt, hh=hh, qb=qb, sb=sb: t.matmul(
                                sb[:], kts[:, kt * 128:(kt + 1) * 128],
                                qts[:, hh * S + qb * 512: hh * S + (qb + 1) * 512],
                                start=True, stop=True), sem="pe")
                            P.wait("scalar", "pe", tmm)
                            if et_use[kt % 4] is not None:
                                pe_t, dv_t = et_use[kt % 4]
                                P.wait("scalar", "pe", pe_t)
                                P.wait("scalar", "dv", dv_t)
                            tex = P.do("scalar", lambda s, kt=kt, sb=sb: s.activation(
                                out=et[:, kt % 4, :], in_=sb[:], func=AF.Exp,
                                bias=expb[:, 0:1]), sem="ac")
                            sc_free[kt % 3] = tex
                            dj = kt - 4 * qb
                            P.wait("vector", "ac", tex)
                            if dj >= 0:
                                tmk = P.do("vector", lambda v, kt=kt, dj=dj: v.tensor_mul(
                                    et[:, kt % 4, :], et[:, kt % 4, :],
                                    masks[:, dj * 512:(dj + 1) * 512]), sem="dv")
                            else:
                                tmk = None
                            if prev_tla is not None:
                                P.wait("vector", "dv", prev_tla)
                            if kt == 0:
                                tla = P.do("vector", lambda v, kt=kt: v.tensor_copy(
                                    eacc[:, :], et[:, kt % 4, :]), sem="dv")
                            else:
                                tla = P.do("vector", lambda v, kt=kt: v.tensor_add(
                                    eacc[:, :], eacc[:, :], et[:, kt % 4, :]), sem="dv")
                            prev_tla = tla
                            pend[kt] = (tmk, tex, tla)

                            def emit_av(k):
                                tmk_, tex_, tla_ = pend[k]
                                if tmk_ is not None:
                                    P.wait("tensor", "dv", tmk_)
                                else:
                                    P.wait("tensor", "ac", tex_)
                                if k == 0 and av_free is not None:
                                    P.wait("tensor", "dv", av_free)
                                tav = P.do("tensor", lambda t, k=k, nkt=nkt: t.matmul(
                                    bank[2][:], vts[:, k * 128:(k + 1) * 128],
                                    et[:, k % 4, :], start=(k == 0),
                                    stop=(k == nkt - 1)), sem="pe")
                                et_use[k % 4] = (tav, tla_)
                                return tav

                            if kt >= 1:
                                av_tok = emit_av(kt - 1)
                        av_tok = emit_av(nkt - 1)
                        if lm_tok is not None:
                            P.wait("vector", "pe", lm_tok)
                        if prev_tla is not None:
                            P.wait("vector", "dv", prev_tla)
                        tcb = P.do("vector", lambda v: v.tensor_copy(
                            eaccb[:, :], eacc[:, :]), sem="dv")
                        P.wait("tensor", "dv", tcb)
                        if l_free is not None:
                            P.wait("tensor", "ac", l_free)
                        tlm = P.do("tensor", lambda t: t.matmul(
                            bank[3][:], ones128[:], eaccb[:, :],
                            start=True, stop=True), sem="pe")
                        lm_tok = tlm
                        P.wait("scalar", "pe", tlm)
                        if lrec_free is not None:
                            P.wait("scalar", "dv", lrec_free)
                        tln = P.do("scalar", lambda s: s.activation(
                            out=lrec[:, :], in_=bank[3][:], func=AF.Ln), sem="ac")
                        P.wait("scalar", "ac", tln)
                        trc = P.do("scalar", lambda s: s.activation(
                            out=lrec[:, :], in_=lrec[:, :], func=AF.Exp,
                            scale=negb[:, 0:1]), sem="ac")
                        l_free = trc
                        P.wait("vector", "pe", av_tok)
                        P.wait("vector", "ac", trc)
                        if ((hh * 8 + qb) % 2) in ao_last:
                            P.wait("vector", ao_last[(hh * 8 + qb) % 2])
                        tnm = P.do("vector", lambda v, hh=hh, qb=qb: v.tensor_mul(
                            astg[:, (hh * 8 + qb) % 2, :], bank[2][:], lrec[:, :]),
                            sem="dv")
                        av_free = tnm
                        lrec_free = tnm
                        P.wait("sync", "dv", tnm)
                        ao_last[(hh * 8 + qb) % 2] = P.dma(
                            "sync",
                            ao[pp * 512 + hh * 128: pp * 512 + (hh + 1) * 128,
                               qb * 512:(qb + 1) * 512],
                            astg[:, (hh * 8 + qb) % 2, :],
                            sem="ao%d" % ((hh * 8 + qb) % 2))
            P.wait("sync", "ao0")
            P.wait("sync", "ao1")

        run_spmd_program(nc, block, handles, body)
    return nc



_NC_A = None
_NC_B = None
_NC_C = None


def _get_programs():
    global _NC_A, _NC_B, _NC_C
    if _NC_A is None:
        _NC_A = build_tokenside(with_proj=True, with_oproj=False)
        _NC_B = build_attention()
        _NC_C = build_tokenside(with_proj=False, with_oproj=True)
    return _NC_A, _NC_B, _NC_C


def kernel(L0_attn_hidden_states, L0_attn_w_q, L0_attn_w_k, L0_attn_w_v,
           L0_attn_w_o, L0_ffn_qw_gate, L0_ffn_z_gate, L0_ffn_s_gate,
           L0_ffn_qw_up, L0_ffn_z_up, L0_ffn_s_up, L0_ffn_qw_down,
           L0_ffn_z_down, L0_ffn_s_down):
    nc_a, nc_b, nc_c = _get_programs()

    x0 = np.ascontiguousarray(np.asarray(L0_attn_hidden_states, dtype=np.float32)
                              .reshape(B * S, D))
    qwg = np.ascontiguousarray(np.asarray(L0_ffn_qw_gate).astype(np.int8))
    qwu = np.ascontiguousarray(np.asarray(L0_ffn_qw_up).astype(np.int8))
    qwd = np.ascontiguousarray(np.asarray(L0_ffn_qw_down).astype(np.int8))
    zg = np.ascontiguousarray(np.asarray(L0_ffn_z_gate).astype(np.int8))
    sg = np.ascontiguousarray(np.asarray(L0_ffn_s_gate, dtype=np.float32))
    zu = np.ascontiguousarray(np.asarray(L0_ffn_z_up).astype(np.int8))
    su = np.ascontiguousarray(np.asarray(L0_ffn_s_up, dtype=np.float32))
    zd = np.ascontiguousarray(np.asarray(L0_ffn_z_down).astype(np.int8))
    sd = np.ascontiguousarray(np.asarray(L0_ffn_s_down, dtype=np.float32))
    wq = np.ascontiguousarray(np.asarray(L0_attn_w_q, dtype=np.float32))
    wk = np.ascontiguousarray(np.asarray(L0_attn_w_k, dtype=np.float32))
    wv = np.ascontiguousarray(np.asarray(L0_attn_w_v, dtype=np.float32))
    wo = np.ascontiguousarray(np.asarray(L0_attn_w_o, dtype=np.float32))

    common = dict(qwg=qwg, qwu=qwu, qwd=qwd, zg=zg, sg=sg, zu=zu, su=su,
                  zd=zd, sd=sd)

    in_a = []
    for c in range(NC):
        m = dict(common)
        m.update(dict(xin=x0[c * TOK:(c + 1) * TOK], wq=wq, wk=wk, wv=wv))
        in_a.append(m)
    res_a = run_bass_kernel_spmd(nc_a, in_a, core_ids=list(range(NC))).results
    x1 = [res_a[c]["xout"] for c in range(NC)]
    qts = [res_a[c]["qt"] for c in range(NC)]
    kts = [res_a[c]["kt"] for c in range(NC)]
    vts = [res_a[c]["vt"] for c in range(NC)]

    in_b = []
    for c in range(NC):
        b = (2 * c) // 8
        owners = [4 * b + o for o in range(4)]
        qtp = np.empty((2, 4 * HD, S), dtype=BF16NP)
        ktp = np.empty((2, HD, S), dtype=BF16NP)
        vtp = np.empty((2, HD, S), dtype=BF16NP)
        for k in range(2):
            g = (2 * c + k) % 8
            qtp[k] = np.concatenate([qts[o][g * 512:(g + 1) * 512, :] for o in owners], axis=1)
            ktp[k] = np.concatenate([kts[o][g * 128:(g + 1) * 128, :] for o in owners], axis=1)
            vtp[k] = np.concatenate([vts[o][g * 128:(g + 1) * 128, :] for o in owners], axis=1)
        in_b.append(dict(qtp=qtp, ktp=ktp, vtp=vtp))
    res_b = run_bass_kernel_spmd(nc_b, in_b, core_ids=list(range(NC))).results
    aos = [res_b[c]["ao"] for c in range(NC)]

    attnT_full = []
    for b in range(B):
        blocks = []
        for g in range(8):
            oc = 4 * b + g // 2
            slot = g % 2
            blocks.append(aos[oc][slot * 512:(slot + 1) * 512, :])
        attnT_full.append(np.concatenate(blocks, axis=0))
    in_c = []
    for c in range(NC):
        b = c // 4
        cs = (c % 4) * TOK
        m = dict(common)
        m.update(dict(
            xin=np.ascontiguousarray(x1[c]),
            attnT=np.ascontiguousarray(attnT_full[b][:, cs:cs + TOK]),
            wo=wo,
        ))
        in_c.append(m)
    res_c = run_bass_kernel_spmd(nc_c, in_c, core_ids=list(range(NC))).results
    out = np.concatenate([res_c[c]["xout"] for c in range(NC)], axis=0)
    return out.reshape(B, S, D).astype(np.float32)


# version 06
# speedup vs baseline: 1.3998x; 1.0096x over previous
import numpy as np
import ml_dtypes
import concourse.bass as bass
import concourse.mybir as mybir
from concourse.bass_utils import run_bass_kernel_spmd

F32 = mybir.dt.float32
BF16 = mybir.dt.bfloat16
I32 = mybir.dt.int32
I8 = mybir.dt.int8
AF = mybir.ActivationFunctionType
ALU = mybir.AluOpType
AX = mybir.AxisListType
BF16NP = ml_dtypes.bfloat16

D = 4096
HID = 11008
S = 4096
B = 2
NC = 8
TOK = (B * S) // NC
GI = D // 128
GO = HID // 128
NH = 32
NKV = 8
HD = 128
EPS = 1e-5
EXPB = -20.0
NTT = TOK // 128
NCB = HID // 256
QSC = float(HD) ** -0.5


class Prog:

    def __init__(self, me, eng, handles):
        self.me = me
        self.eng = eng
        self.h = handles
        self.v = {k: 0 for k in handles}

    def do(self, engname, fn, sem=None, n=1):
        tok = None
        if sem is not None:
            self.v[sem] += n
            tok = self.v[sem]
        if engname == self.me:
            ins = fn(self.eng)
            if sem is not None:
                ins.then_inc(self.h[sem], n)
        return tok

    def dma(self, engname, out, in_, sem, **kw):
        t = self.do(engname, lambda e: e.dma_start(out=out, in_=in_, **kw),
                    sem=sem, n=16)
        return (sem, t)

    def wait(self, engname, sem, tok=None):
        if isinstance(sem, tuple):
            sem, tok = sem
        elif isinstance(tok, tuple):
            sem, tok = tok
        if engname == self.me:
            self.eng.wait_ge(self.h[sem], self.v[sem] if tok is None else tok)

    def tok(self, sem):
        return self.v[sem]

    def gp(self, fn, sem="gp"):
        t = self.do("gpsimd", fn, sem=sem)
        self.wait("gpsimd", sem, t)
        return t


def run_spmd_program(nc, block, handles, body):
    def mk(engname):
        def walk(eng):
            body(Prog(engname, eng, handles))
        return walk
    block.sync(mk("sync"))
    block.gpsimd(mk("gpsimd"))
    block.tensor(mk("tensor"))
    block.vector(mk("vector"))
    block.scalar(mk("scalar"))



def build_tokenside(with_proj, with_oproj):
    nc = bass.Bass("TRN2", target_bir_lowering=False, debug=False)
    ap = lambda t: t.ap()

    E = {}
    E["xin"] = ap(nc.dram_tensor("xin", [TOK, D], F32, kind="ExternalInput"))
    E["qwg"] = ap(nc.dram_tensor("qwg", [D, HID], I8, kind="ExternalInput"))
    E["qwu"] = ap(nc.dram_tensor("qwu", [D, HID], I8, kind="ExternalInput"))
    E["qwd"] = ap(nc.dram_tensor("qwd", [HID, D], I8, kind="ExternalInput"))
    E["zg"] = ap(nc.dram_tensor("zg", [GI, HID], I8, kind="ExternalInput"))
    E["sg"] = ap(nc.dram_tensor("sg", [GI, HID], F32, kind="ExternalInput"))
    E["zu"] = ap(nc.dram_tensor("zu", [GI, HID], I8, kind="ExternalInput"))
    E["su"] = ap(nc.dram_tensor("su", [GI, HID], F32, kind="ExternalInput"))
    E["zd"] = ap(nc.dram_tensor("zd", [GO, D], I8, kind="ExternalInput"))
    E["sd"] = ap(nc.dram_tensor("sd", [GO, D], F32, kind="ExternalInput"))
    if with_proj:
        E["wq"] = ap(nc.dram_tensor("wq", [D, D], F32, kind="ExternalInput"))
        E["wk"] = ap(nc.dram_tensor("wk", [D, NKV * HD], F32, kind="ExternalInput"))
        E["wv"] = ap(nc.dram_tensor("wv", [D, NKV * HD], F32, kind="ExternalInput"))
        E["qt"] = ap(nc.dram_tensor("qt", [D, TOK], BF16, kind="ExternalOutput"))
        E["kt"] = ap(nc.dram_tensor("kt", [NKV * HD, TOK], BF16, kind="ExternalOutput"))
        E["vt"] = ap(nc.dram_tensor("vt", [NKV * HD, TOK], BF16, kind="ExternalOutput"))
    if with_oproj:
        E["attnT"] = ap(nc.dram_tensor("attnT", [D, TOK], BF16, kind="ExternalInput"))
        E["wo"] = ap(nc.dram_tensor("wo", [D, D], F32, kind="ExternalInput"))
    E["xout"] = ap(nc.dram_tensor("xout", [TOK, D], F32, kind="ExternalOutput"))
    E["xbase_d"] = ap(nc.dram_tensor("xbase_d", [TOK, D], F32))
    E["zsg_d"] = ap(nc.dram_tensor("zsg_d", [GI, HID], BF16))
    E["zsu_d"] = ap(nc.dram_tensor("zsu_d", [GI, HID], BF16))
    E["zsd_d"] = ap(nc.dram_tensor("zsd_d", [GO, D], BF16))
    E["sdeff_d"] = ap(nc.dram_tensor("sdeff_d", [GO, D], BF16))
    E["ht_d"] = ap(nc.dram_tensor("ht_d", [GO, 128, TOK], BF16))
    E["wd_d"] = ap(nc.dram_tensor("wd_d", [GO, 128, D], BF16))

    import contextlib
    with contextlib.ExitStack() as _es:
        _e = _es.enter_context
        xnT = _e(nc.sbuf_tensor("xnT", [128, GI * TOK], BF16))
        wg_sb = _e(nc.sbuf_tensor("wg", [128, GI * 256], BF16))
        wu_sb = _e(nc.sbuf_tensor("wu", [128, GI * 256], BF16))
        qg_i8 = _e(nc.sbuf_tensor("qgi", [128, 2, GI * 256], I8))
        qu_i8 = _e(nc.sbuf_tensor("qui", [128, 2, GI * 256], I8))
        wdq = _e(nc.sbuf_tensor("wdq", [128, D], BF16))
        scb = _e(nc.sbuf_tensor("scb", [GI, 2 * 256], BF16))
        zscb = _e(nc.sbuf_tensor("zscb", [GI, 2, 2 * 256], BF16))
        sdef_sb = _e(nc.sbuf_tensor("sdef", [GO, D], BF16))
        zsdmb = _e(nc.sbuf_tensor("zsdmb", [GO, 512], BF16))
        xsT = _e(nc.sbuf_tensor("xsT", [GI, TOK], BF16))
        hs_sb = _e(nc.sbuf_tensor("hs", [128, NTT * GO], BF16))
        hsT = _e(nc.sbuf_tensor("hsT", [GO, TOK], BF16))
        hct = _e(nc.sbuf_tensor("hct", [128, TOK], BF16))
        htin = _e(nc.sbuf_tensor("htin", [128, 2, TOK], BF16))
        wdt = _e(nc.sbuf_tensor("wdt", [128, 2, 512], BF16))
        ones1 = _e(nc.sbuf_tensor("ones1", [1, 128], BF16))
        onescol = _e(nc.sbuf_tensor("onescol", [128, 1], BF16))
        ident = _e(nc.sbuf_tensor("ident", [128, 128], BF16))
        indg = _e(nc.sbuf_tensor("indg", [128, GI * 128], BF16))
        rowsel = _e(nc.sbuf_tensor("rowsel", [128, GI], F32))
        indone = _e(nc.sbuf_tensor("indone", [128, 128], BF16))
        xld = _e(nc.sbuf_tensor("xld", [128, D], F32))
        xn_sb = _e(nc.sbuf_tensor("xn", [128, D], BF16))
        xs_sb = _e(nc.sbuf_tensor("xs", [128, NTT * GI], BF16))
        xsf = _e(nc.sbuf_tensor("xsf", [128, GI], F32))
        lnst = _e(nc.sbuf_tensor("lnst", [128, 8], F32))
        gsil = _e(nc.sbuf_tensor("gsil", [128, TOK], BF16))
        xacc = _e(nc.sbuf_tensor("xacc", [128, 2, 512], F32))
        pstg = _e(nc.sbuf_tensor("pstg", [128, 2, 512], BF16))
        bk0 = _e(nc.psum_tensor("bk0", [128, 512], F32))
        bk1 = _e(nc.psum_tensor("bk1", [128, 512], F32))
        bk2 = _e(nc.psum_tensor("bk2", [128, 512], F32))
        bk3 = _e(nc.psum_tensor("bk3", [128, 512], F32))
        bk4 = _e(nc.psum_tensor("bk4", [128, 512], F32))
        bk5 = _e(nc.psum_tensor("bk5", [128, 512], F32))
        bk6 = _e(nc.psum_tensor("bk6", [128, 512], F32))
        bk7 = _e(nc.psum_tensor("bk7", [128, 512], F32))
        semnames = ["pe", "dv", "ac", "gp", "pre", "ln", "pz", "ps", "pz0",
                    "pz1", "ps0", "ps1", "pwa", "pwb", "pw", "xli",
                    "xbw", "qg0", "qg1", "qu0", "qu1", "scg", "scu", "zsgq",
                    "zsuq", "wdqi", "wdw", "htw", "ht0", "ht1", "ht2", "ht3",
                    "wd0", "wd1", "wd2", "wd3",
                    "zsm", "xa0", "xa1", "xo0", "xo1", "pj0", "pj1", "pj2",
                    "pj3", "pw0", "pw1", "ati", "wod"]
        handles = {n: _e(nc.semaphore(n)) for n in semnames}
        block = _e(nc.Block())
        env = dict(E)
        env.update(dict(
            xnT=xnT, wg_sb=wg_sb, wu_sb=wu_sb, qg_i8=qg_i8, qu_i8=qu_i8,
            wdq=wdq, scb=scb, zscb=zscb,
            sdef_sb=sdef_sb, zsdmb=zsdmb, xsT=xsT, hs_sb=hs_sb, hsT=hsT,
            hct=hct, htin=htin, wdt=wdt, ones1=ones1, onescol=onescol,
            ident=ident, indg=indg, xld=xld, xn_sb=xn_sb, xs_sb=xs_sb,
            lnst=lnst, gsil=gsil, xacc=xacc, pstg=pstg, rowsel=rowsel,
            indone=indone, xsf=xsf,
            bank=[bk0, bk1, bk2, bk3, bk4, bk5, bk6, bk7],
        ))

        def body(P):
            for k in [k for k in env if k.startswith("_")]:
                del env[k]
            emit_prep(P, env)
            if with_oproj:
                emit_oproj(P, env)
                base = env["xbase_d"]
            else:
                base = env["xin"]
            emit_ln(P, env, base, write_xbase=(not with_oproj))
            emit_gateup(P, env)
            emit_down(P, env)
            if with_proj:
                emit_ln(P, env, env["xout"], write_xbase=False)
                emit_projections(P, env)
            for s_ in ("xo0", "xo1", "xbw", "pw0", "pw1", "wdw", "htw", "pw"):
                P.wait("sync", s_)

        run_spmd_program(nc, block, handles, body)
    return nc


def emit_prep(P, env):
    ones1, onescol, ident, indg = env["ones1"], env["onescol"], env["ident"], env["indg"]
    xld = env["xld"]
    CH = HID // 16

    P.gp(lambda g: g.memset(ones1[:], 1.0))
    P.gp(lambda g: g.memset(onescol[:], 1.0))
    P.gp(lambda g: g.memset(ident[:], 1.0))
    P.gp(lambda g: g.affine_select(
        ident[:], ident[:], pattern=[[1, 128]], compare_op=ALU.is_equal,
        fill=0.0, channel_multiplier=-1))
    rowsel, indone = env["rowsel"], env["indone"]
    P.gp(lambda g: g.memset(rowsel[:], 0.0))
    P.gp(lambda g: g.memset(indone[:], 1.0))
    for base_r in (0, 32, 64):
        P.gp(lambda g, base_r=base_r: g.affine_select(
            rowsel[:], rowsel[:], pattern=[[-1, GI]], compare_op=ALU.not_equal,
            fill=1.0, base=-base_r, channel_multiplier=1))
    tsel = P.do("gpsimd", lambda g: g.nop(), sem="gp")
    P.wait("vector", "gp", tsel)
    for gi in range(GI):
        P.do("vector", lambda v, gi=gi: v.tensor_scalar(
            indg[:, gi * 128:(gi + 1) * 128], indone[:],
            rowsel[:, gi:gi + 1], None, ALU.mult), sem="pre")

    items = []
    for z_, s_, out_ in ((env["zg"], env["sg"], env["zsg_d"]),
                         (env["zu"], env["su"], env["zsu_d"])):
        for c in range(16):
            items.append((z_, s_, out_, slice(c * CH, (c + 1) * CH)))
    ztk, stk, dvt, wtk = {}, {}, {}, {}

    def prow(i):
        return slice(32 * (i % 2), 32 * (i % 2) + GI)

    def pload(i):
        z_, s_, out_, sl = items[i]
        r = prow(i)
        if i >= 2:
            P.wait("gpsimd", "dv", dvt[i - 2])
            P.wait("sync", "dv", dvt[i - 2])
            P.wait("gpsimd", wtk[i - 2])
            P.wait("sync", wtk[i - 2])
        ztk[i] = P.dma("gpsimd", xld[r, 0:CH], z_[:, sl], sem="pz%d" % (i % 2))
        stk[i] = P.dma("sync", xld[r, CH:2 * CH], s_[:, sl], sem="ps%d" % (i % 2))

    pload(0)
    pload(1)
    prev = None
    for i, (z_, s_, out_, sl) in enumerate(items):
        r = prow(i)
        P.wait("vector", ztk[i])
        P.wait("vector", stk[i])
        tv = P.do("vector", lambda v, r=r: v.scalar_tensor_tensor(
            xld[r, 2 * CH:3 * CH], xld[r, 0:CH], -1.0,
            xld[r, CH:2 * CH], ALU.mult, ALU.mult), sem="dv")
        dvt[i] = tv
        P.wait("gpsimd", "dv", tv)
        prev = P.dma("gpsimd", out_[:, sl], xld[r, 2 * CH:3 * CH],
                     sem="pw%d" % (i % 2) if False else "pwa" if i % 2 == 0 else "pwb")
        wtk[i] = prev
        if i + 2 < len(items):
            pload(i + 2)
    P.wait("gpsimd", wtk[len(items) - 2])
    P.wait("sync", wtk[len(items) - 2])
    P.wait("gpsimd", wtk[len(items) - 1])
    P.wait("sync", wtk[len(items) - 1])
    P.wait("sync", "dv", dvt[len(items) - 1])
    P.wait("gpsimd", "dv", dvt[len(items) - 1])
    CH2 = D // 2
    for c in range(2):
        sl = slice(c * CH2, (c + 1) * CH2)
        P.wait("gpsimd", prev)
        P.wait("sync", prev)
        tz = P.dma("gpsimd", xld[0:86, 0:CH2], env["zd"][:, sl], sem="pz")
        ts = P.dma("sync", xld[0:86, CH2:2 * CH2], env["sd"][:, sl], sem="ps")
        P.wait("vector", tz)
        P.wait("vector", ts)
        t1 = P.do("vector", lambda v: v.tensor_scalar_mul(
            xld[0:86, CH2:2 * CH2], xld[0:86, CH2:2 * CH2], 0.5), sem="dv")
        P.wait("vector", "dv", t1)
        t2 = P.do("vector", lambda v: v.scalar_tensor_tensor(
            xld[0:86, 0:CH2], xld[0:86, 0:CH2], -1.0,
            xld[0:86, CH2:2 * CH2], ALU.mult, ALU.mult), sem="dv")
        P.wait("gpsimd", "dv", t2)
        P.dma("gpsimd", env["zsd_d"][:, sl], xld[0:86, 0:CH2], sem="pw")
        prev = P.dma("gpsimd", env["sdeff_d"][:, sl], xld[0:86, CH2:2 * CH2], sem="pw")
    P.wait("sync", prev)
    env["_prep_tok"] = P.dma("sync", env["sdef_sb"][:, :], env["sdeff_d"][:, :], sem="ps")


def emit_ln(P, env, src_d, write_xbase):
    xld, xn, lnst = env["xld"], env["xn_sb"], env["lnst"]
    xs_sb, xsT, xnT = env["xs_sb"], env["xsT"], env["xnT"]
    xsf = env["xsf"]
    ident, bank = env["ident"], env["bank"]
    s1, s2, m, var, sd_, rstd, bias = (lnst[:, i:i + 1] for i in range(7))

    P.wait("sync", "xo0", P.tok("xo0"))
    P.wait("sync", "xo1", P.tok("xo1"))
    P.wait("sync", "ac", P.tok("ac"))
    P.wait("tensor", "dv", P.tok("dv"))
    P.wait("tensor", "ac", P.tok("ac"))

    tr_ev = []
    for tt in range(NTT):
        sl = slice(tt * 128, (tt + 1) * 128)
        if tt >= 1:
            P.wait("sync", "ac", env["_xld_free"])
        tld = P.dma("sync", xld[:, :], src_d[sl, :], sem="xli")
        if write_xbase:
            P.dma("sync", env["xbase_d"][sl, :], src_d[sl, :], sem="xbw")
        P.wait("scalar", tld)
        if tr_ev:
            P.wait("scalar", "pe", env["_ln_trmm"])
            P.wait("scalar", "dv", env["_xsf_cp"])
        ts1 = P.do("scalar", lambda s: s.activation(
            out=xn[:, :], in_=xld[:, :], func=AF.Copy, accum_out=s1), sem="ac")
        P.wait("scalar", "ac", ts1)
        ts2 = P.do("scalar", lambda s: s.activation(
            out=xn[:, :], in_=xld[:, :], func=AF.Square, accum_out=s2), sem="ac")
        P.wait("vector", "ac", ts2)
        P.do("vector", lambda v: v.tensor_scalar(m, s1, 1.0 / D, None, ALU.mult), sem="dv")
        t_mv = P.do("vector", lambda v: v.tensor_scalar(var, s2, 1.0 / D, EPS, ALU.mult, ALU.add), sem="dv")
        P.wait("vector", "dv", t_mv)
        tm = P.do("vector", lambda v: v.scalar_tensor_tensor(
            s1, s1, -1.0 / (float(D) * float(D)), s1, ALU.mult, ALU.mult), sem="dv")
        P.wait("vector", "dv", tm)
        tv = P.do("vector", lambda v: v.tensor_add(var, var, s1), sem="dv")
        P.wait("scalar", "dv", tv)
        tsq = P.do("scalar", lambda s: s.activation(out=sd_, in_=var, func=AF.Sqrt), sem="ac")
        P.wait("vector", "ac", tsq)
        tr = P.do("vector", lambda v: v.reciprocal(rstd, sd_), sem="dv")
        P.wait("vector", "dv", tr)
        tb = P.do("vector", lambda v: v.scalar_tensor_tensor(
            bias, m, -1.0, rstd, ALU.mult, ALU.mult), sem="dv")
        P.wait("scalar", "dv", tb)
        P.wait("scalar", "ac", ts2)
        txn = P.do("scalar", lambda s: s.activation(
            out=xn[:, :], in_=xld[:, :], func=AF.Identity,
            scale=rstd, bias=bias), sem="ac")
        env["_xld_free"] = txn
        P.wait("vector", "ac", txn)
        if "_xsf_cp" in env:
            P.wait("vector", "dv", env["_xsf_cp"])
        trd = None
        for gi in range(GI):
            trd = P.do("vector", lambda v, gi=gi: v.reduce_sum(
                xsf[:, gi:gi + 1],
                xn[:, gi * 128:(gi + 1) * 128], axis=AX.X), sem="dv")
        P.wait("vector", "dv", trd)
        env["_xsf_cp"] = P.do("vector", lambda v, tt=tt: v.tensor_copy(
            xs_sb[:, tt * GI:(tt + 1) * GI], xsf[:, :]), sem="dv")
        P.wait("tensor", "ac", txn)
        for gi in range(GI):
            ps = bank[gi % 4]
            if len(tr_ev) >= 4:
                P.wait("tensor", "ac", tr_ev[-4])
            tp = P.do("tensor", lambda t, gi=gi, ps=ps: t.matmul(
                ps[:, 0:128], xn[:, gi * 128:(gi + 1) * 128],
                ident[:], start=True, stop=True), sem="pe")
            if gi == GI - 1:
                env["_ln_trmm"] = tp
            P.wait("scalar", "pe", tp)
            te = P.do("scalar", lambda s, gi=gi, tt=tt, ps=ps: s.activation(
                out=xnT[:, gi * TOK + tt * 128: gi * TOK + (tt + 1) * 128],
                in_=ps[:, 0:128], func=AF.Copy), sem="ac")
            tr_ev.append(te)
    P.wait("tensor", "dv", P.tok("dv"))
    P.wait("tensor", "ac", tr_ev[-1])
    xst_dv = {}
    for tc in range(NTT):
        ps = bank[tc % 2]
        if tc >= 2:
            P.wait("tensor", "dv", xst_dv[tc - 2])
        tp = P.do("tensor", lambda t, tc=tc, ps=ps: t.matmul(
            ps[0:GI, 0:128], xs_sb[:, tc * GI:(tc + 1) * GI], ident[:],
            start=True, stop=True), sem="pe")
        P.wait("vector", "pe", tp)
        xst_dv[tc] = P.do("vector", lambda v, tc=tc, ps=ps: v.tensor_copy(
            xsT[:, tc * 128:(tc + 1) * 128], ps[0:GI, 0:128]),
            sem=("dv" if tc < NTT - 1 else "ln"))
    env["_ln_done"] = xst_dv[NTT - 1]
    env["_lnxst_dv"] = P.tok("dv")


def emit_gateup(P, env):
    wg_sb, wu_sb, scb, zscb = env["wg_sb"], env["wu_sb"], env["scb"], env["zscb"]
    qg_i8, qu_i8 = env["qg_i8"], env["qu_i8"]
    xnT, xsT, indg = env["xnT"], env["xsT"], env["indg"]
    gsil, hct, hs_sb = env["gsil"], env["hct"], env["hs_sb"]
    bank = env["bank"]
    onescol, ident, hsT = env["onescol"], env["ident"], env["hsT"]
    ht_d, wd_d, wdq = env["ht_d"], env["wd_d"], env["wdq"]
    sdef_sb = env["sdef_sb"]

    P.wait("tensor", "ln", env["_ln_done"])
    P.wait("gpsimd", "ln", env["_ln_done"])
    P.wait("sync", "ln", env["_ln_done"])
    P.wait("tensor", env["_prep_tok"])
    P.wait("tensor", "ac", P.tok("ac"))
    P.wait("vector", "pe", P.tok("pe"))

    qwg_v = env["qwg"].rearrange("(g p) c -> p g c", p=128)
    qwu_v = env["qwu"].rearrange("(g p) c -> p g c", p=128)
    qwd_v = env["qwd"].rearrange("(ct p) c -> ct p c", p=128)

    wtok, utok, dtok = {}, {}, {}
    ht_wtok = [None] * GO
    wd_wtok = [None] * GO
    env["_ht_wtok"] = ht_wtok
    env["_wd_wtok"] = wd_wtok

    dqend = {}
    bc_pe = {}

    def load_cb(mat, cb):
        src = qwg_v if mat == 0 else qwu_v
        dst = qg_i8 if mat == 0 else qu_i8
        s_in = env["sg"] if mat == 0 else env["su"]
        zs_in = env["zsg_d"] if mat == 0 else env["zsu_d"]
        bb = cb % 2
        eng = "sync" if mat == 0 else "scalar"
        if (mat, cb - 2) in dqend:
            P.wait(eng, "dv", dqend[(mat, cb - 2)])
        if (mat, cb - 1) in bc_pe:
            P.wait("gpsimd", "pe", bc_pe[(mat, cb - 1)])
        if (mat, cb - 2) in utok:
            P.wait("sync", "pe", utok[(mat, cb - 2)])
        if env.get("_zs_prev") is not None:
            P.wait("sync", env["_zs_prev"])
        sl = slice(cb * 256, (cb + 1) * 256)
        t1 = P.dma(eng, dst[:, bb, :].rearrange("p (g c) -> p g c", c=256),
                   src[:, :, sl], sem=("qg%d" % bb if mat == 0 else "qu%d" % bb))
        t2 = P.dma("gpsimd", scb[:, mat * 256:(mat + 1) * 256], s_in[:, sl],
                   sem=("scg" if mat == 0 else "scu"))
        t3 = P.dma("sync", zscb[:, cb % 2, mat * 256:(mat + 1) * 256],
                   zs_in[:, sl], sem=("zsgq" if mat == 0 else "zsuq"))
        env["_zs_prev"] = t3
        wtok[(mat, cb)] = (t1, t2, t3)

    def load_dct(ct):
        if ct >= 1 and wd_wtok[ct - 1] is not None:
            P.wait("gpsimd", wd_wtok[ct - 1])
        dtok[ct] = P.dma("gpsimd", wdq[:, :], qwd_v[ct, :, :], sem="wdqi")

    bc_state = [0, None, None]

    def bcast(lhs_lo, lhs_n, lhs_col, rhs, ncol):
        i = bc_state[0] % 2
        bc_state[0] += 1
        bkx = bank[4 + i]
        if bc_state[1 + i] is not None:
            P.wait("tensor", "dv", bc_state[1 + i])
        tp = P.do("tensor", lambda t: t.matmul(
            bkx[:, 0:ncol], indg[lhs_lo:lhs_lo + lhs_n,
                                 lhs_col:lhs_col + 128], rhs,
            start=True, stop=True), sem="pe")
        return i, bkx, tp

    def bcast_done(i, tok):
        bc_state[1 + i] = tok

    def dq(mat, cb):
        Wt = wg_sb if mat == 0 else wu_sb
        src8 = (qg_i8 if mat == 0 else qu_i8)[:, cb % 2, :]
        t1, t2, t3 = wtok[(mat, cb)]
        P.wait("tensor", t2)
        P.wait("vector", t1)
        if (mat, cb - 1) in utok:
            P.wait("vector", "pe", utok[(mat, cb - 1)])
        for gp in range(GI // 2):
            gi0, gi1 = 2 * gp, 2 * gp + 1
            i = bc_state[0] % 2
            bc_state[0] += 1
            bkx = bank[4 + i]
            if bc_state[1 + i] is not None:
                P.wait("tensor", "dv", bc_state[1 + i])
            P.do("tensor", lambda t, gi0=gi0, bkx=bkx, mat=mat: t.matmul(
                bkx[:, 0:256], indg[0:GI, gi0 * 128:gi0 * 128 + 128],
                scb[:, mat * 256:(mat + 1) * 256],
                start=True, stop=True), sem="pe")
            tp = P.do("tensor", lambda t, gi1=gi1, bkx=bkx, mat=mat: t.matmul(
                bkx[:, 256:512], indg[0:GI, gi1 * 128:gi1 * 128 + 128],
                scb[:, mat * 256:(mat + 1) * 256],
                start=True, stop=True), sem="pe")
            P.wait("vector", "pe", tp)
            tv = P.do("vector", lambda v, gp=gp, Wt=Wt, src8=src8, bkx=bkx:
                      v.tensor_mul(
                          Wt[:, gp * 512:(gp + 1) * 512],
                          src8[:, gp * 512:(gp + 1) * 512],
                          bkx[:, 0:512]), sem="dv")
            bc_state[1 + i] = tv
        dqend[(mat, cb)] = P.tok("dv")
        bc_pe[(mat, cb)] = P.tok("pe")

    load_cb(0, 0)
    load_cb(1, 0)
    load_dct(0)
    dq(0, 0)
    dq(1, 0)
    load_cb(0, 1)
    load_cb(1, 1)

    hs_ev = {}
    acc_free = {}
    hct_free = None
    hs_pe_tok = None

    for cb in range(NCB):
        env["_dqg"] = dqend[(0, cb)]
        env["_dqu"] = dqend[(1, cb)]
        for ct in range(2):
            gct = 2 * cb + ct
            W0 = wg_sb[:, :]
            W1 = wu_sb[:, :]
            _, _, t3g = wtok[(0, cb)]
            _, _, t3u = wtok[(1, cb)]
            P.wait("tensor", t3g)
            P.wait("tensor", t3u)
            P.wait("tensor", "dv", env["_dqg"])
            P.wait("tensor", "dv", env["_dqu"])
            for tb in range(2):
                bk = bank[tb]
                for gi in range(GI):
                    if gi == 0 and (0, tb) in acc_free:
                        P.wait("tensor", "ac", acc_free[(0, tb)])
                    P.do("tensor", lambda t, gi=gi, ct=ct, tb=tb, bk=bk, W0=W0: t.matmul(
                        bk[:], W0[:, gi * 256 + ct * 128: gi * 256 + ct * 128 + 128],
                        xnT[:, gi * TOK + tb * 512: gi * TOK + (tb + 1) * 512],
                        start=(gi == 0), stop=False), sem="pe")
                tzg = P.do("tensor", lambda t, ct=ct, tb=tb, bk=bk, cb=cb: t.matmul(
                    bk[:], zscb[0:GI, cb % 2, ct * 128: ct * 128 + 128],
                    xsT[:, tb * 512:(tb + 1) * 512],
                    start=False, stop=True), sem="pe")
                env["_lastg_pe"] = tzg
                P.wait("scalar", "pe", tzg)
                tsl = P.do("scalar", lambda s, tb=tb, bk=bk: s.activation(
                    out=gsil[:, tb * 512:(tb + 1) * 512], in_=bk[:],
                    func=AF.Silu), sem="ac")
                acc_free[(0, tb)] = tsl
            for tb in range(2):
                bk = bank[2 + tb]
                for gi in range(GI):
                    if gi == 0 and (1, tb) in acc_free:
                        P.wait("tensor", "dv", acc_free[(1, tb)])
                    P.do("tensor", lambda t, gi=gi, ct=ct, tb=tb, bk=bk, W1=W1: t.matmul(
                        bk[:], W1[:, gi * 256 + ct * 128: gi * 256 + ct * 128 + 128],
                        xnT[:, gi * TOK + tb * 512: gi * TOK + (tb + 1) * 512],
                        start=(gi == 0), stop=False), sem="pe")
                tzu = P.do("tensor", lambda t, ct=ct, tb=tb, bk=bk, cb=cb: t.matmul(
                    bk[:], zscb[0:GI, cb % 2, 256 + ct * 128: 256 + ct * 128 + 128],
                    xsT[:, tb * 512:(tb + 1) * 512],
                    start=False, stop=True), sem="pe")
                env["_lastu_pe"] = tzu
                P.wait("vector", "pe", tzu)
                P.wait("vector", "ac", acc_free[(0, tb)])
                if hct_free is not None:
                    P.wait("vector", hct_free)
                if hs_pe_tok is not None:
                    P.wait("vector", "pe", hs_pe_tok)
                th = P.do("vector", lambda v, tb=tb, bk=bk: v.tensor_mul(
                    hct[:, tb * 512:(tb + 1) * 512], bk[:],
                    gsil[:, tb * 512:(tb + 1) * 512]), sem="dv")
                acc_free[(1, tb)] = th
            th_all = P.tok("dv")
            hb = bank[6 + gct % 2]
            P.wait("tensor", "dv", th_all)
            if gct - 2 in hs_ev:
                P.wait("tensor", "dv", hs_ev[gct - 2])
            for tc in range(NTT):
                P.do("tensor", lambda t, tc=tc, hb=hb: t.matmul(
                    hb[:, tc:tc + 1], hct[:, tc * 128:(tc + 1) * 128],
                    onescol[:], start=True, stop=True), sem="pe")
            hs_pe_tok = P.tok("pe")
            P.wait("vector", "pe", hs_pe_tok)
            tev = P.do("vector", lambda v, gct=gct, hb=hb: v.tensor_copy(
                hs_sb[:, gct * NTT:(gct + 1) * NTT], hb[:, 0:NTT]), sem="dv")
            hs_ev[gct] = tev
            P.wait("scalar", "dv", th_all)
            hct_free = P.dma("scalar", ht_d[gct, :, :], hct[:, :], sem="htw")
            ht_wtok[gct] = hct_free
            ct2 = gct
            P.wait("vector", dtok[ct2])
            base = (ct2 // 32) * 32
            nrow = min(32, GO - base)
            last_dq = None
            for mb in range(8):
                i, bkx, tp = bcast(base, nrow, (ct2 - base) * 128,
                                   sdef_sb[base:base + nrow, mb * 512:(mb + 1) * 512],
                                   512)
                P.wait("vector", "pe", tp)
                tv = P.do("vector", lambda v, mb=mb, bkx=bkx: v.tensor_mul(
                    wdq[:, mb * 512:(mb + 1) * 512], wdq[:, mb * 512:(mb + 1) * 512],
                    bkx[:, 0:512]), sem="dv")
                bcast_done(i, tv)
                last_dq = tv
            P.wait("scalar", "dv", last_dq)
            wd_wtok[ct2] = P.dma("scalar", wd_d[ct2, :, :], wdq[:, :], sem="wdw")
            if ct2 + 1 < GO:
                load_dct(ct2 + 1)
        utok[(0, cb)] = env["_lastg_pe"]
        utok[(1, cb)] = env["_lastu_pe"]
        if cb + 1 < NCB:
            dq(0, cb + 1)
            dq(1, cb + 1)
        if cb + 2 < NCB:
            load_cb(0, cb + 2)
            load_cb(1, cb + 2)
    P.wait("tensor", "dv", P.tok("dv"))
    hst_dv = {}
    for tc in range(NTT):
        ps = bank[tc % 2]
        if tc >= 2:
            P.wait("tensor", "dv", hst_dv[tc - 2])
        tp = P.do("tensor", lambda t, tc=tc, ps=ps: t.matmul(
            ps[0:GO, 0:128], hs_sb.rearrange("p (ct tc) -> p tc ct", tc=NTT)[:, tc, :],
            ident[:], start=True, stop=True), sem="pe")
        P.wait("vector", "pe", tp)
        hst_dv[tc] = P.do("vector", lambda v, tc=tc, ps=ps: v.tensor_copy(
            hsT[:, tc * 128:(tc + 1) * 128], ps[0:GO, 0:128]),
            sem=("dv" if tc < NTT - 1 else "ln"))
    env["_gu_done"] = hst_dv[NTT - 1]


def emit_down(P, env):
    htin, wdt, xacc = env["htin"], env["wdt"], env["xacc"]
    zsdmb, hsT = env["zsdmb"], env["hsT"]
    bank = env["bank"]
    ht_d, wd_d, xbase_d, xout = env["ht_d"], env["wd_d"], env["xbase_d"], env["xout"]
    ht_wtok, wd_wtok = env["_ht_wtok"], env["_wd_wtok"]

    P.wait("tensor", "ln", env["_gu_done"])
    P.wait("vector", "ln", env["_gu_done"])
    P.wait("sync", "xbw", P.tok("xbw"))

    htok, wtok2 = {}, {}
    dn_use = {}
    env["_dn_use"] = dn_use
    wg_sb, wu_sb, xnT = env["wg_sb"], env["wu_sb"], env["xnT"]
    chunks = []
    c0 = 0
    while c0 < GO:
        chunks.append((c0, min(8, GO - c0)))
        c0 += 8
    NCH = len(chunks)

    def ht4(s):
        base = wg_sb if s == 0 else wu_sb
        return base[:, :].rearrange("p (b c) -> p b c", c=TOK)

    def wd4(s):
        return xnT[:, s * 8 * 512:(s + 1) * 8 * 512] \
            .rearrange("p (b c) -> p b c", c=512)

    P.wait("scalar", "pe", P.tok("pe"))
    P.wait("sync", "pe", P.tok("pe"))

    def load_chunk(k):
        mb, kc = divmod(k, NCH)
        ct0, nct = chunks[kc]
        s = k % 2
        if k >= 2:
            P.wait("scalar", "pe", dn_use[k - 2])
            P.wait("sync", "pe", dn_use[k - 2])
        P.wait("scalar", ht_wtok[ct0 + nct - 1])
        P.wait("sync", wd_wtok[ct0 + nct - 1])
        htok[k] = P.dma(
            "scalar", ht4(s)[:, 0:nct, :],
            ht_d[ct0:ct0 + nct, :, :].rearrange("ct p t -> p ct t"),
            sem="ht%d" % s)
        wtok2[k] = P.dma(
            "sync", wd4(s)[:, 0:nct, :],
            wd_d[ct0:ct0 + nct, :, mb * 512:(mb + 1) * 512]
            .rearrange("ct p t -> p ct t"),
            sem="wd%d" % s)

    load_chunk(0)
    load_chunk(1)
    zt_prev = None
    evac_toks = [None] * NTT
    xw_tok = {}
    for mb in range(8):
        if zt_prev is not None:
            P.wait("sync", "pe", zt_prev)
        tz = P.dma("sync", zsdmb[:, :], env["zsd_d"][:, mb * 512:(mb + 1) * 512],
                   sem="zsm")
        for kc in range(NCH):
            k = mb * NCH + kc
            ct0, nct = chunks[kc]
            s = k % 2
            P.wait("tensor", htok[k])
            P.wait("tensor", wtok2[k])
            for j in range(nct):
                for tt in range(NTT):
                    if ct0 + j == 0 and evac_toks[tt] is not None:
                        P.wait("tensor", "dv", evac_toks[tt])
                    P.do("tensor", lambda t, tt=tt, j=j, s=s: t.matmul(
                        bank[tt][:], ht4(s)[:, j, tt * 128:(tt + 1) * 128],
                        wd4(s)[:, j, :], start=(ct0 + j == 0), stop=False),
                        sem="pe")
            dn_use[k] = P.tok("pe")
            if k + 2 < 8 * NCH:
                load_chunk(k + 2)
        P.wait("tensor", tz)
        for tt in range(NTT):
            tzc = P.do("tensor", lambda t, tt=tt: t.matmul(
                bank[tt][:], hsT[:, tt * 128:(tt + 1) * 128], zsdmb[:, :],
                start=False, stop=True), sem="pe")
            if tt == NTT - 1:
                zt_prev = tzc
            if (tt % 2) in xw_tok:
                P.wait("sync", xw_tok[tt % 2])
            txa = P.dma("sync", xacc[:, tt % 2, :],
                        xbase_d[tt * 128:(tt + 1) * 128, mb * 512:(mb + 1) * 512],
                        sem="xa%d" % (tt % 2))
            P.wait("vector", "pe", tzc)
            P.wait("vector", txa)
            ta = P.do("vector", lambda v, tt=tt: v.tensor_add(
                xacc[:, tt % 2, :], xacc[:, tt % 2, :], bank[tt][:]), sem="dv")
            evac_toks[tt] = ta
            P.wait("sync", "dv", ta)
            xw_tok[tt % 2] = P.dma(
                "sync", xout[tt * 128:(tt + 1) * 128, mb * 512:(mb + 1) * 512],
                xacc[:, tt % 2, :], sem="xo%d" % (tt % 2))
    env["_down_done"] = P.tok("dv")


def emit_projections(P, env):
    wg_sb, wu_sb = env["wg_sb"], env["wu_sb"]
    xnT, pstg = env["xnT"], env["pstg"]
    bank = env["bank"]

    jobs = [(env["wq"], env["qt"], D, QSC),
            (env["wk"], env["kt"], NKV * HD, 1.0),
            (env["wv"], env["vt"], NKV * HD, 1.0)]

    P.wait("gpsimd", "dv", env["_down_done"])
    P.wait("gpsimd", "pe", P.tok("pe"))
    P.wait("tensor", "ln", env["_ln_done"])
    P.wait("tensor", "ac", P.tok("ac"))
    P.wait("scalar", "dv", env["_down_done"])

    seq = []
    for (w_in, out_d, ncols, scale) in jobs:
        w_v = w_in.rearrange("(g p) c -> p g c", p=128)
        for db in range(ncols // 256):
            seq.append((w_v, out_d, db, scale))

    ltoks, used, bufs = {}, {}, {}

    def load_j(j):
        w_v, out_d, db, scale = seq[j]
        buf = wg_sb[:, :] if j % 2 == 0 else wu_sb[:, :]
        if (j - 2) in used:
            P.wait("gpsimd", "pe", used[j - 2])
        sl = slice(db * 256, (db + 1) * 256)
        ltoks[j] = P.dma("gpsimd", buf.rearrange("p (g c) -> p g c", c=256),
                         w_v[:, :, sl], sem="pj%d" % (j % 2))
        return buf

    for j in range(min(2, len(seq))):
        bufs[j] = load_j(j)
    ev_free = {}
    pj_w = {}
    for j in range(len(seq)):
        w_v, out_d, db, scale = seq[j]
        buf = bufs[j]
        P.wait("tensor", ltoks[j])
        for ct in range(2):
            for tb in range(2):
                bki = ct * 2 + tb
                bk = bank[bki]
                if bki in ev_free:
                    P.wait("tensor", "ac", ev_free[bki])
                for gi in range(GI):
                    P.do("tensor", lambda t, gi=gi, ct=ct, tb=tb, bk=bk, buf=buf: t.matmul(
                        bk[:], buf[:, gi * 256 + ct * 128: gi * 256 + ct * 128 + 128],
                        xnT[:, gi * TOK + tb * 512: gi * TOK + (tb + 1) * 512],
                        start=(gi == 0), stop=(gi == GI - 1)), sem="pe")
                tmm = P.tok("pe")
                P.wait("scalar", "pe", tmm)
                if (bki % 2) in pj_w:
                    P.wait("scalar", pj_w[bki % 2])
                tev = P.do("scalar", lambda s, bk=bk, bki=bki, scale=scale: s.activation(
                    out=pstg[:, bki % 2, :], in_=bk[:], func=AF.Copy, scale=scale),
                    sem="ac")
                ev_free[bki] = tev
                P.wait("scalar", "ac", tev)
                r0 = db * 256 + ct * 128
                pj_w[bki % 2] = P.dma(
                    "scalar", out_d[r0:r0 + 128, tb * 512:(tb + 1) * 512],
                    pstg[:, bki % 2, :], sem="pw%d" % (bki % 2))
        used[j] = P.tok("pe")
        if j + 2 < len(seq):
            bufs[j + 2] = load_j(j + 2)


def emit_oproj(P, env):
    wg_sb, wu_sb = env["wg_sb"], env["wu_sb"]
    xnT, xacc, xld = env["xnT"], env["xacc"], env["xld"]
    bank = env["bank"]
    attnT, wo, xin, xbase_d = env["attnT"], env["wo"], env["xin"], env["xbase_d"]

    at_v = attnT.rearrange("(g p) c -> p g c", p=128)
    t_at = P.dma("sync", xnT.rearrange("p (g c) -> p g c", c=TOK), at_v, sem="ati")
    wo_v = wo.rearrange("(g p) c -> p g c", p=128)

    P.wait("sync", "dv", P.tok("dv"))
    P.wait("scalar", "dv", P.tok("dv"))
    for s_ in ("pwa", "pwb", "pw"):
        P.wait("sync", s_, P.tok(s_))
        P.wait("scalar", s_, P.tok(s_))

    ldt, castt, wuse, wtoks = {}, {}, {}, {}
    NQ = 4 * (D // 512)

    def stage(q):
        mbh, qh = divmod(q, 4)
        mbq = 2 * mbh
        eng = "sync" if q % 2 == 0 else "scalar"
        if q >= 2:
            P.wait(eng, "ac", castt[q - 2])
        sl = slice(mbq * 256, (mbq + 1) * 256)
        ldt[q] = P.dma(
            eng, xld[:, (q % 2) * 2048:(q % 2) * 2048 + 2048]
            .rearrange("p (g c) -> p g c", c=256),
            wo_v[:, qh * 8:(qh + 1) * 8, sl], sem="pj%d" % (q % 4))

    def docast(q):
        mbh, qh = divmod(q, 4)
        mbq = 2 * mbh
        buf = wg_sb[:, :]
        P.wait("scalar", ldt[q])
        if qh == 0 and (mbq - 2) in wuse:
            P.wait("scalar", "pe", wuse[mbq - 2])
        castt[q] = P.do("scalar", lambda s, buf=buf, qh=qh, q=q: s.activation(
            out=buf[:, qh * 2048:(qh + 1) * 2048],
            in_=xld[:, (q % 2) * 2048:(q % 2) * 2048 + 2048],
            func=AF.Copy), sem="ac")

    def load_w_direct(mbq):
        buf = wu_sb[:, :]
        if (mbq - 2) in wuse:
            P.wait("gpsimd", "pe", wuse[mbq - 2])
        sl = slice(mbq * 256, (mbq + 1) * 256)
        wtoks[mbq] = P.dma("gpsimd", buf.rearrange("p (g c) -> p g c", c=256),
                           wo_v[:, :, sl], sem="wod")

    stage(0)
    stage(1)
    load_w_direct(1)
    P.wait("tensor", t_at)
    evd = {}
    xw = {}
    for mbq in range(D // 256):
        buf = wg_sb[:, :] if mbq % 2 == 0 else wu_sb[:, :]
        if mbq % 2 == 0:
            for q in range(4 * (mbq // 2), 4 * (mbq // 2) + 4):
                docast(q)
                if q + 2 < NQ:
                    stage(q + 2)
            P.wait("tensor", "ac", castt[4 * (mbq // 2) + 3])
        else:
            P.wait("tensor", wtoks[mbq])
        for gi in range(GI):
            for tt in range(NTT):
                if gi == 0 and tt in evd:
                    P.wait("tensor", "dv", evd[tt])
                P.do("tensor", lambda t, gi=gi, tt=tt, buf=buf: t.matmul(
                    bank[tt][:, 0:256],
                    xnT[:, gi * TOK + tt * 128: gi * TOK + (tt + 1) * 128],
                    buf[:, gi * 256:(gi + 1) * 256],
                    start=(gi == 0), stop=(gi == GI - 1)), sem="pe")
        tmm = P.tok("pe")
        for tt in range(NTT):
            if (tt % 2) in xw:
                P.wait("sync", xw[tt % 2])
            txa = P.dma("sync", xacc[:, tt % 2, 0:256],
                        xin[tt * 128:(tt + 1) * 128, mbq * 256:(mbq + 1) * 256],
                        sem="xa%d" % (tt % 2))
            P.wait("vector", "pe", tmm)
            P.wait("vector", txa)
            ta = P.do("vector", lambda v, tt=tt: v.tensor_add(
                xacc[:, tt % 2, 0:256], xacc[:, tt % 2, 0:256],
                bank[tt][:, 0:256]), sem="dv")
            evd[tt] = ta
            P.wait("sync", "dv", ta)
            xw[tt % 2] = P.dma(
                "sync", xbase_d[tt * 128:(tt + 1) * 128, mbq * 256:(mbq + 1) * 256],
                xacc[:, tt % 2, 0:256], sem="xo%d" % (tt % 2))
        wuse[mbq] = P.tok("pe")
        if mbq % 2 == 1 and mbq + 2 < D // 256:
            load_w_direct(mbq + 2)
    env["_op_done"] = P.do("vector", lambda v: v.nop(), sem="ln")



def build_attention():
    nc = bass.Bass("TRN2", target_bir_lowering=False, debug=False)
    ap = lambda t: t.ap()
    qtp = ap(nc.dram_tensor("qtp", [2, 4 * HD, S], BF16, kind="ExternalInput"))
    ktp = ap(nc.dram_tensor("ktp", [2, HD, S], BF16, kind="ExternalInput"))
    vtp = ap(nc.dram_tensor("vtp", [2, HD, S], BF16, kind="ExternalInput"))
    ao = ap(nc.dram_tensor("ao", [2 * 4 * HD, S], BF16, kind="ExternalOutput"))

    import contextlib
    with contextlib.ExitStack() as _es:
        _e = _es.enter_context
        qts = _e(nc.sbuf_tensor("qts", [128, 4 * S], BF16))
        kts = _e(nc.sbuf_tensor("kts", [128, S], BF16))
        vtq = _e(nc.sbuf_tensor("vtq", [128, S], BF16))
        vts = _e(nc.sbuf_tensor("vts", [128, S], BF16))
        et = _e(nc.sbuf_tensor("et", [128, 4, 512], BF16))
        eacc = _e(nc.sbuf_tensor("eacc", [128, 512], F32))
        eaccb = _e(nc.sbuf_tensor("eaccb", [128, 512], BF16))
        lrec = _e(nc.sbuf_tensor("lrec", [128, 512], F32))
        masks = _e(nc.sbuf_tensor("masks", [128, 4 * 512], BF16))
        ones128 = _e(nc.sbuf_tensor("ones128", [128, 128], BF16))
        ident = _e(nc.sbuf_tensor("ident", [128, 128], BF16))
        astg = _e(nc.sbuf_tensor("astg", [128, 2, 512], BF16))
        expb = _e(nc.sbuf_tensor("expb", [128, 1], F32))
        negb = _e(nc.sbuf_tensor("negb", [128, 1], F32))
        bk0 = _e(nc.psum_tensor("bk0", [128, 512], F32))
        bk1 = _e(nc.psum_tensor("bk1", [128, 512], F32))
        bk2 = _e(nc.psum_tensor("bk2", [128, 512], F32))
        bk3 = _e(nc.psum_tensor("bk3", [128, 512], F32))
        bk4 = _e(nc.psum_tensor("bk4", [128, 512], F32))
        bk5 = _e(nc.psum_tensor("bk5", [128, 512], F32))
        semnames = ["pe", "dv", "ac", "use", "gp", "qld", "kld", "vld",
                    "ao0", "ao1"]
        handles = {n: _e(nc.semaphore(n)) for n in semnames}
        block = _e(nc.Block())
        bank = [bk0, bk1, bk2, bk3, bk4, bk5]

        def body(P):
            P.gp(lambda g: g.memset(ones128[:], 1.0), sem="use")
            P.gp(lambda g: g.memset(expb[:], EXPB), sem="use")
            P.gp(lambda g: g.memset(negb[:], -1.0), sem="use")
            P.gp(lambda g: g.memset(ident[:], 1.0), sem="use")
            P.gp(lambda g: g.affine_select(
                ident[:], ident[:], pattern=[[1, 128]], compare_op=ALU.is_equal,
                fill=0.0, channel_multiplier=-1), sem="use")
            P.gp(lambda g: g.memset(masks[:], 1.0), sem="use")
            tmask = None
            for j in range(4):
                tmask = P.gp(lambda g, j=j: g.affine_select(
                    masks[:, j * 512:(j + 1) * 512], masks[:, j * 512:(j + 1) * 512],
                    pattern=[[1, 512]], compare_op=ALU.is_ge, fill=0.0,
                    base=-128 * j, channel_multiplier=-1), sem="use")
            ao_last = {}
            for pp in range(2):
                if pp == 1:
                    P.wait("sync", "ao0", P.tok("ao0"))
                    P.wait("sync", "ao1", P.tok("ao1"))
                    P.wait("sync", "ac", P.tok("ac"))
                    P.wait("sync", "pe", P.tok("pe"))
                tq = P.dma("sync", qts.rearrange("p (h c) -> p h c", c=S),
                           qtp[pp].rearrange("(h p) c -> p h c", p=128), sem="qld")
                tk = P.dma("sync", kts[:, :], ktp[pp, :, :], sem="kld")
                tv = P.dma("sync", vtq[:, :], vtp[pp, :, :], sem="vld")
                P.wait("tensor", tv)
                if pp == 0:
                    P.wait("tensor", "use", tmask)
                vt_ev = []
                for kt in range(32):
                    ps = bank[4 + kt % 2]
                    if len(vt_ev) >= 2:
                        P.wait("tensor", "dv", vt_ev[-2])
                    tp = P.do("tensor", lambda t, kt=kt, ps=ps: t.matmul(
                        ps[:, 0:128], vtq[:, kt * 128:(kt + 1) * 128], ident[:],
                        start=True, stop=True), sem="pe")
                    P.wait("vector", "pe", tp)
                    te = P.do("vector", lambda v, kt=kt, ps=ps: v.tensor_copy(
                        vts[:, kt * 128:(kt + 1) * 128], ps[:, 0:128]), sem="dv")
                    vt_ev.append(te)
                P.wait("tensor", tq)
                P.wait("tensor", tk)
                P.wait("tensor", "dv", vt_ev[-1])
                if pp == 0:
                    P.wait("scalar", "use", tmask)
                    P.wait("vector", "use", tmask)
                sc_free = [None, None, None]
                et_use = [None] * 4
                av_free = None
                l_free = None
                lm_tok = None
                lrec_free = None
                for hh in range(4):
                    for qb in range(8):
                        nkt = 4 * (qb + 1)
                        av_tok = None
                        prev_tla = None
                        pend = {}
                        for kt in range(nkt):
                            sb = bank[(0, 1, 5)[kt % 3]]
                            if sc_free[kt % 3] is not None:
                                P.wait("tensor", "ac", sc_free[kt % 3])
                            tmm = P.do("tensor", lambda t, kt=kt, hh=hh, qb=qb, sb=sb: t.matmul(
                                sb[:], kts[:, kt * 128:(kt + 1) * 128],
                                qts[:, hh * S + qb * 512: hh * S + (qb + 1) * 512],
                                start=True, stop=True), sem="pe")
                            P.wait("scalar", "pe", tmm)
                            if et_use[kt % 4] is not None:
                                pe_t, dv_t = et_use[kt % 4]
                                P.wait("scalar", "pe", pe_t)
                                P.wait("scalar", "dv", dv_t)
                            tex = P.do("scalar", lambda s, kt=kt, sb=sb: s.activation(
                                out=et[:, kt % 4, :], in_=sb[:], func=AF.Exp,
                                bias=expb[:, 0:1]), sem="ac")
                            sc_free[kt % 3] = tex
                            dj = kt - 4 * qb
                            P.wait("vector", "ac", tex)
                            if dj >= 0:
                                tmk = P.do("vector", lambda v, kt=kt, dj=dj: v.tensor_mul(
                                    et[:, kt % 4, :], et[:, kt % 4, :],
                                    masks[:, dj * 512:(dj + 1) * 512]), sem="dv")
                            else:
                                tmk = None
                            if prev_tla is not None:
                                P.wait("vector", "dv", prev_tla)
                            if kt == 0:
                                tla = P.do("vector", lambda v, kt=kt: v.tensor_copy(
                                    eacc[:, :], et[:, kt % 4, :]), sem="dv")
                            else:
                                tla = P.do("vector", lambda v, kt=kt: v.tensor_add(
                                    eacc[:, :], eacc[:, :], et[:, kt % 4, :]), sem="dv")
                            prev_tla = tla
                            pend[kt] = (tmk, tex, tla)

                            def emit_av(k):
                                tmk_, tex_, tla_ = pend[k]
                                if tmk_ is not None:
                                    P.wait("tensor", "dv", tmk_)
                                else:
                                    P.wait("tensor", "ac", tex_)
                                if k == 0 and av_free is not None:
                                    P.wait("tensor", "dv", av_free)
                                tav = P.do("tensor", lambda t, k=k, nkt=nkt: t.matmul(
                                    bank[2][:], vts[:, k * 128:(k + 1) * 128],
                                    et[:, k % 4, :], start=(k == 0),
                                    stop=(k == nkt - 1)), sem="pe")
                                et_use[k % 4] = (tav, tla_)
                                return tav

                            if kt >= 1:
                                av_tok = emit_av(kt - 1)
                        av_tok = emit_av(nkt - 1)
                        if lm_tok is not None:
                            P.wait("vector", "pe", lm_tok)
                        if prev_tla is not None:
                            P.wait("vector", "dv", prev_tla)
                        tcb = P.do("vector", lambda v: v.tensor_copy(
                            eaccb[:, :], eacc[:, :]), sem="dv")
                        P.wait("tensor", "dv", tcb)
                        if l_free is not None:
                            P.wait("tensor", "ac", l_free)
                        tlm = P.do("tensor", lambda t: t.matmul(
                            bank[3][:], ones128[:], eaccb[:, :],
                            start=True, stop=True), sem="pe")
                        lm_tok = tlm
                        P.wait("scalar", "pe", tlm)
                        if lrec_free is not None:
                            P.wait("scalar", "dv", lrec_free)
                        tln = P.do("scalar", lambda s: s.activation(
                            out=lrec[:, :], in_=bank[3][:], func=AF.Ln), sem="ac")
                        P.wait("scalar", "ac", tln)
                        trc = P.do("scalar", lambda s: s.activation(
                            out=lrec[:, :], in_=lrec[:, :], func=AF.Exp,
                            scale=negb[:, 0:1]), sem="ac")
                        l_free = trc
                        P.wait("vector", "pe", av_tok)
                        P.wait("vector", "ac", trc)
                        if ((hh * 8 + qb) % 2) in ao_last:
                            P.wait("vector", ao_last[(hh * 8 + qb) % 2])
                        tnm = P.do("vector", lambda v, hh=hh, qb=qb: v.tensor_mul(
                            astg[:, (hh * 8 + qb) % 2, :], bank[2][:], lrec[:, :]),
                            sem="dv")
                        av_free = tnm
                        lrec_free = tnm
                        P.wait("sync", "dv", tnm)
                        ao_last[(hh * 8 + qb) % 2] = P.dma(
                            "sync",
                            ao[pp * 512 + hh * 128: pp * 512 + (hh + 1) * 128,
                               qb * 512:(qb + 1) * 512],
                            astg[:, (hh * 8 + qb) % 2, :],
                            sem="ao%d" % ((hh * 8 + qb) % 2))
            P.wait("sync", "ao0")
            P.wait("sync", "ao1")

        run_spmd_program(nc, block, handles, body)
    return nc



_NC_A = None
_NC_B = None
_NC_C = None


def _get_programs():
    global _NC_A, _NC_B, _NC_C
    if _NC_A is None:
        _NC_A = build_tokenside(with_proj=True, with_oproj=False)
        _NC_B = build_attention()
        _NC_C = build_tokenside(with_proj=False, with_oproj=True)
    return _NC_A, _NC_B, _NC_C


def kernel(L0_attn_hidden_states, L0_attn_w_q, L0_attn_w_k, L0_attn_w_v,
           L0_attn_w_o, L0_ffn_qw_gate, L0_ffn_z_gate, L0_ffn_s_gate,
           L0_ffn_qw_up, L0_ffn_z_up, L0_ffn_s_up, L0_ffn_qw_down,
           L0_ffn_z_down, L0_ffn_s_down):
    nc_a, nc_b, nc_c = _get_programs()

    x0 = np.ascontiguousarray(np.asarray(L0_attn_hidden_states, dtype=np.float32)
                              .reshape(B * S, D))
    qwg = np.ascontiguousarray(np.asarray(L0_ffn_qw_gate).astype(np.int8))
    qwu = np.ascontiguousarray(np.asarray(L0_ffn_qw_up).astype(np.int8))
    qwd = np.ascontiguousarray(np.asarray(L0_ffn_qw_down).astype(np.int8))
    zg = np.ascontiguousarray(np.asarray(L0_ffn_z_gate).astype(np.int8))
    sg = np.ascontiguousarray(np.asarray(L0_ffn_s_gate, dtype=np.float32))
    zu = np.ascontiguousarray(np.asarray(L0_ffn_z_up).astype(np.int8))
    su = np.ascontiguousarray(np.asarray(L0_ffn_s_up, dtype=np.float32))
    zd = np.ascontiguousarray(np.asarray(L0_ffn_z_down).astype(np.int8))
    sd = np.ascontiguousarray(np.asarray(L0_ffn_s_down, dtype=np.float32))
    wq = np.ascontiguousarray(np.asarray(L0_attn_w_q, dtype=np.float32))
    wk = np.ascontiguousarray(np.asarray(L0_attn_w_k, dtype=np.float32))
    wv = np.ascontiguousarray(np.asarray(L0_attn_w_v, dtype=np.float32))
    wo = np.ascontiguousarray(np.asarray(L0_attn_w_o, dtype=np.float32))

    common = dict(qwg=qwg, qwu=qwu, qwd=qwd, zg=zg, sg=sg, zu=zu, su=su,
                  zd=zd, sd=sd)

    in_a = []
    for c in range(NC):
        m = dict(common)
        m.update(dict(xin=x0[c * TOK:(c + 1) * TOK], wq=wq, wk=wk, wv=wv))
        in_a.append(m)
    res_a = run_bass_kernel_spmd(nc_a, in_a, core_ids=list(range(NC))).results
    x1 = [res_a[c]["xout"] for c in range(NC)]
    qts = [res_a[c]["qt"] for c in range(NC)]
    kts = [res_a[c]["kt"] for c in range(NC)]
    vts = [res_a[c]["vt"] for c in range(NC)]

    in_b = []
    for c in range(NC):
        b = (2 * c) // 8
        owners = [4 * b + o for o in range(4)]
        qtp = np.empty((2, 4 * HD, S), dtype=BF16NP)
        ktp = np.empty((2, HD, S), dtype=BF16NP)
        vtp = np.empty((2, HD, S), dtype=BF16NP)
        for k in range(2):
            g = (2 * c + k) % 8
            qtp[k] = np.concatenate([qts[o][g * 512:(g + 1) * 512, :] for o in owners], axis=1)
            ktp[k] = np.concatenate([kts[o][g * 128:(g + 1) * 128, :] for o in owners], axis=1)
            vtp[k] = np.concatenate([vts[o][g * 128:(g + 1) * 128, :] for o in owners], axis=1)
        in_b.append(dict(qtp=qtp, ktp=ktp, vtp=vtp))
    res_b = run_bass_kernel_spmd(nc_b, in_b, core_ids=list(range(NC))).results
    aos = [res_b[c]["ao"] for c in range(NC)]

    attnT_full = []
    for b in range(B):
        blocks = []
        for g in range(8):
            oc = 4 * b + g // 2
            slot = g % 2
            blocks.append(aos[oc][slot * 512:(slot + 1) * 512, :])
        attnT_full.append(np.concatenate(blocks, axis=0))
    in_c = []
    for c in range(NC):
        b = c // 4
        cs = (c % 4) * TOK
        m = dict(common)
        m.update(dict(
            xin=np.ascontiguousarray(x1[c]),
            attnT=np.ascontiguousarray(attnT_full[b][:, cs:cs + TOK]),
            wo=wo,
        ))
        in_c.append(m)
    res_c = run_bass_kernel_spmd(nc_c, in_c, core_ids=list(range(NC))).results
    out = np.concatenate([res_c[c]["xout"] for c in range(NC)], axis=0)
    return out.reshape(B, S, D).astype(np.float32)
